# version 20
# baseline (speedup 1.0000x reference)
import contextlib
import math

import numpy as np
import ml_dtypes

import concourse.bass as bass
import concourse.mybir as mybir
from concourse.bass_utils import run_bass_kernel_spmd

F32 = mybir.dt.float32
BF16 = mybir.dt.bfloat16
AF = mybir.ActivationFunctionType
ALU = mybir.AluOpType
AX = mybir.AxisListType

MAGIC = 12582912.0
RS2 = 0.7071067811865476
EPS_MP = 1e-4
SQS = 45.254833995939045

B_PER_CORE = 2
S = 2048
T = B_PER_CORE * S
NT = T // 128
NT_B = S // 128
D = 1024
EC = D // 128
I_FFN = 2816
IC = I_FFN // 128
NH = 16
HD = 64

ENGINES = ("sync", "scalar", "vector", "tensor", "gpsimd")
DMA_QUEUES = ("sync", "scalar", "gpsimd")
CONST_VALS = [2.0, 0.5, -0.5, -2.0, -4.0, MAGIC, -MAGIC]



class Buf:
    __slots__ = ("space", "name", "rows", "cols", "dtype", "kind", "h",
                 "wrecs", "rrecs")

    def __init__(self, space, name, rows, cols, dtype, kind=None):
        self.space, self.name = space, name
        self.rows, self.cols, self.dtype, self.kind = rows, cols, dtype, kind
        self.h = None
        self.wrecs = []
        self.rrecs = []

    def __getitem__(self, idx):
        rs, cs = idx
        if isinstance(rs, slice):
            r0, r1, _ = rs.indices(self.rows)
        else:
            r0, r1 = rs, rs + 1
        if isinstance(cs, slice):
            c0, c1, _ = cs.indices(self.cols)
        else:
            c0, c1 = cs, cs + 1
        return View(self, r0, r1, c0, c1)

    @property
    def full(self):
        return View(self, 0, self.rows, 0, self.cols)


class View:
    __slots__ = ("buf", "r0", "r1", "c0", "c1")

    def __init__(self, buf, r0, r1, c0, c1):
        self.buf, self.r0, self.r1, self.c0, self.c1 = buf, r0, r1, c0, c1

    @property
    def ap(self):
        return self.buf.h[self.r0:self.r1, self.c0:self.c1]

    def sub(self, r0, r1, c0, c1):
        return View(self.buf, self.r0 + r0, self.r0 + r1, self.c0 + c0, self.c0 + c1)

    def rect(self):
        if self.buf.space == "psum":
            return (0, self.buf.rows, 0, self.buf.cols)
        return (self.r0, self.r1, self.c0, self.c1)

    @property
    def shape(self):
        return (self.r1 - self.r0, self.c1 - self.c0)


def _ovl(a, b):
    return a[0] < b[1] and b[0] < a[1] and a[2] < b[3] and b[2] < a[3]


class Op:
    __slots__ = ("idx", "engine", "fn", "is_dma", "tag", "inc_needed",
                 "inc_count", "wait_edges", "narrow", "minspan", "has_psum")

    def __init__(self, idx, engine, fn, is_dma, tag):
        self.idx, self.engine, self.fn, self.is_dma, self.tag = \
            idx, engine, fn, is_dma, tag
        self.inc_needed = is_dma
        self.inc_count = None
        self.wait_edges = []


class Prog:
    def __init__(self):
        self.ops = []
        self.bufs = []
        self.n_preload = 0

    def sbuf(self, name, rows, cols, dtype):
        b = Buf("sbuf", name, rows, cols, dtype)
        self.bufs.append(b)
        return b

    def psum(self, name, rows, cols):
        b = Buf("psum", name, rows, cols, F32)
        self.bufs.append(b)
        return b

    def dram(self, name, rows, cols, dtype, kind="Internal"):
        b = Buf("dram", name, rows, cols, dtype, kind)
        self.bufs.append(b)
        return b

    def hint(self, engine, views, tag="hint"):
        o = Op(len(self.ops), engine, None, False, tag)
        o.has_psum = False
        o.minspan = 0
        o.narrow = False
        edges = set()
        for v in views:
            r = v.rect()
            for rec in v.buf.wrecs:
                if _ovl(r, rec):
                    edges.add(rec[4])
        for pidx in edges:
            p = self.ops[pidx]
            if p.engine == engine and not p.is_dma:
                continue
            o.wait_edges.append(p)
            p.inc_needed = True
        self.ops.append(o)
        return o

    def op(self, engine, fn, reads=(), writes=(), tag="", is_dma=False):
        o = Op(len(self.ops), engine, fn, is_dma, tag)
        allv = list(reads) + list(writes)
        o.has_psum = any(v.buf.space == "psum" for v in allv)
        o.minspan = min((v.c1 - v.c0 for v in allv), default=0)
        if is_dma:
            spans = [v.r1 - v.r0 for v in allv if v.buf.space == "sbuf"]
            o.narrow = bool(spans) and min(spans) < 128
        else:
            o.narrow = False
        edges = set()
        for v in reads:
            r = v.rect()
            for rec in v.buf.wrecs:
                if _ovl(r, rec):
                    edges.add(rec[4])
        for v in writes:
            r = v.rect()
            for rec in v.buf.wrecs:
                if _ovl(r, rec):
                    edges.add(rec[4])
            for rec in v.buf.rrecs:
                if _ovl(r, rec):
                    edges.add(rec[4])
        for pidx in edges:
            p = self.ops[pidx]
            if (p.engine == "tensor" and engine == "tensor"
                    and not p.is_dma and not is_dma):
                continue
            o.wait_edges.append(p)
            p.inc_needed = True
        for v in writes:
            r = v.rect()
            def cov(rec, _r=r):
                return (_r[0] <= rec[0] and rec[1] <= _r[1]
                        and _r[2] <= rec[2] and rec[3] <= _r[3])
            v.buf.wrecs = [rec for rec in v.buf.wrecs if not cov(rec)]
            v.buf.rrecs = [rec for rec in v.buf.rrecs if not cov(rec)]
            v.buf.wrecs.append((*r, o.idx))
        for v in reads:
            v.buf.rrecs.append((*v.rect(), o.idx))
        self.ops.append(o)
        return o

    def build(self):
        nc = bass.Bass("TRN2", target_bir_lowering=False)
        stack = contextlib.ExitStack()
        for b in self.bufs:
            if b.space == "dram":
                b.h = nc.dram_tensor(b.name, [b.rows, b.cols], b.dtype,
                                     kind=b.kind).ap()
        for b in self.bufs:
            if b.space == "sbuf":
                b.h = stack.enter_context(
                    nc.sbuf_tensor(b.name, [b.rows, b.cols], b.dtype)).ap()
            elif b.space == "psum":
                b.h = stack.enter_context(
                    nc.psum_tensor(b.name, [b.rows, b.cols], b.dtype)).ap()
        cb = next(b for b in self.bufs if b.name == "consts_sb")
        for i, val in enumerate(CONST_VALS):
            nc.const_aps.register(F32, val, cb.h[:, i:i + 1])

        RSEM_Q = {"sync": 40, "scalar": 8, "gpsimd": 32}
        sem_counts = {}
        queue_counts = {}
        for o in self.ops:
            if o.is_dma:
                i = queue_counts.get(o.engine, 0)
                queue_counts[o.engine] = i + 1
                o.inc_count = i
            elif o.inc_needed:
                key = "e_" + o.engine
                sem_counts[key] = sem_counts.get(key, 0) + 1
                o.inc_count = sem_counts[key]
        sems = {k: stack.enter_context(nc.semaphore(k)) for k in sem_counts}
        dpool = {}
        for q in queue_counts:
            dpool[q] = [stack.enter_context(nc.semaphore(f"d_{q}{r}"))
                        for r in range(RSEM_Q[q])]

        per_engine = {e: [] for e in ENGINES}
        for o in self.ops:
            per_engine[o.engine].append(o)
        n_pre = self.n_preload

        block = stack.enter_context(nc.Block())

        def dma_key_val(p, consumer=True):
            R = RSEM_Q[p.engine]
            slot = p.inc_count % R
            rnd = p.inc_count // R
            return ("dq", p.engine, slot), 16 * (rnd + (1 if consumer else 0))

        def mk(ename):
            def run(eng):
                wm = {}
                if n_pre and ename != "sync" and "sync" in dpool:
                    for i in range(n_pre):
                        eng.wait_ge(dpool["sync"][i % RSEM_Q["sync"]], 16)
                        wm[("dq", "sync", i % RSEM_Q["sync"])] = 16
                for o in per_engine[ename]:
                    for p in o.wait_edges:
                        if p.is_dma:
                            key, val = dma_key_val(p, consumer=True)
                            sem = dpool[p.engine][key[2]]
                        else:
                            key = "e_" + p.engine
                            val = p.inc_count
                            sem = sems[key]
                        if wm.get(key, 0) >= val:
                            continue
                        eng.wait_ge(sem, val)
                        wm[key] = val
                    if o.is_dma:
                        key, val = dma_key_val(o, consumer=False)
                        if val > 0 and wm.get(key, 0) < val:
                            eng.wait_ge(dpool[o.engine][key[2]], val)
                            wm[key] = val
                    if o.fn is None:
                        continue
                    instr = o.fn(eng)
                    if o.is_dma:
                        instr.then_inc(dpool[o.engine][o.inc_count % RSEM_Q[o.engine]], 16)
                    elif o.inc_needed:
                        instr.then_inc(sems["e_" + o.engine], 1)
                if ename == "gpsimd":
                    for q, cnt in queue_counts.items():
                        R = RSEM_Q[q]
                        for r in range(R):
                            n_r = (cnt - 1 - r) // R + 1 if r < cnt else 0
                            if n_r > 0:
                                eng.wait_ge(dpool[q][r], 16 * n_r)
            return run

        for ename in ENGINES:
            getattr(block, ename)(mk(ename))
        return nc, stack



def dma(P, queue, dst, src, tag="", dst_fn=None, src_fn=None,
        track_writes=None, track_reads=None):
    def fn(e, _d=dst, _s=src):
        dap = dst_fn(_d.ap) if dst_fn else _d.ap
        sap = src_fn(_s.ap) if src_fn else _s.ap
        return e.dma_start(out=dap, in_=sap)
    return P.op(queue, fn, track_reads or [src], track_writes or [dst], tag,
                is_dma=True)


def dmat(P, queue, dst, src, tag=""):
    def fn(e, _d=dst, _s=src):
        return e.dma_start_transpose(_d.ap, _s.ap)
    return P.op(queue, fn, [src], [dst], tag, is_dma=True)


def dmat_blocks(P, queue, dst, src, tag=""):
    R = src.r1 - src.r0
    def fn(e, _d=dst, _s=src, _R=R):
        dap = _d.ap.rearrange("p (k r) -> p k r", r=_R)
        return e.dma_start_transpose(dap, _s.ap)
    return P.op(queue, fn, [src], [dst], tag, is_dma=True)


def ts(P, dst, src, s1, s2=None, op0=ALU.mult, op1=None, accum=None, tag="ts"):
    reads = [src] + [x for x in (s1, s2) if isinstance(x, View)]
    writes = [dst] + ([accum] if accum is not None else [])

    def fn(e, _d=dst, _s=src, _1=s1, _2=s2, _a=accum, _o0=op0, _o1=op1):
        k1 = _1.ap if isinstance(_1, View) else _1
        k2 = _2.ap if isinstance(_2, View) else _2
        kw = {"accum_out": _a.ap} if _a is not None else {}
        if _o1 is None:
            return e.tensor_scalar(_d.ap, _s.ap, k1, None, _o0, **kw)
        return e.tensor_scalar(_d.ap, _s.ap, k1, k2, _o0, _o1, **kw)

    return P.op("vector", fn, reads, writes, tag)


def tt(P, dst, a, b, op=ALU.mult, eng="vector", tag="tt",
       dst_fn=None, a_fn=None, b_fn=None):
    def fn(e, _d=dst, _a=a, _b=b, _op=op):
        dap = dst_fn(_d.ap) if dst_fn else _d.ap
        aap = a_fn(_a.ap) if a_fn else _a.ap
        bap = b_fn(_b.ap) if b_fn else _b.ap
        return e.tensor_tensor(dap, aap, bap, _op)
    return P.op(eng, fn, [a, b], [dst], tag)


def act(P, dst, src, func, bias=None, scale=None, accum=None, tag="act"):
    reads = [src] + ([bias] if isinstance(bias, View) else [])
    writes = [dst] + ([accum] if accum is not None else [])

    def fn(e, _d=dst, _s=src, _b=bias, _sc=scale, _a=accum, _f=func):
        kw = {}
        if _b is not None:
            kw["bias"] = _b.ap if isinstance(_b, View) else _b
        if _sc is not None:
            kw["scale"] = _sc
        if _a is not None:
            kw["accum_out"] = _a.ap
        return e.activation(_d.ap, _s.ap, _f, **kw)

    return P.op("scalar", fn, reads, writes, tag)


def red(P, dst, src, op=ALU.max, absval=False, negate=False, src_fn=None,
        tag="red"):
    def fn(e, _d=dst, _s=src, _op=op):
        sap = src_fn(_s.ap) if src_fn else _s.ap
        kw = {}
        if absval:
            kw["apply_absolute_value"] = True
        if negate:
            kw["negate"] = True
        return e.tensor_reduce(_d.ap, sap, AX.X, _op, **kw)
    return P.op("vector", fn, [src], [dst], tag)


def copy(P, dst, src, eng="vector", tag="copy"):
    if eng == "scalar":
        def fn(e, _d=dst, _s=src):
            return e.activation(_d.ap, _s.ap, AF.Copy)
    else:
        def fn(e, _d=dst, _s=src):
            return e.tensor_copy(_d.ap, _s.ap)
    return P.op(eng, fn, [src], [dst], tag)


def recip(P, dst, src, tag="recip"):
    return P.op("vector", lambda e, _d=dst, _s=src: e.reciprocal(_d.ap, _s.ap),
                [src], [dst], tag)


def mm(P, out, lhsT, rhs, start, stop, tag="mm"):
    def fn(e, _o=out, _l=lhsT, _r=rhs, _st=start, _sp=stop):
        return e.matmul(_o.ap, _l.ap, _r.ap, start=_st, stop=_sp)
    return P.op("tensor", fn, [lhsT, rhs], [out], tag)


def petr(P, out, src, ident, tag="ptr"):
    def fn(e, _o=out, _s=src, _i=ident):
        return e.transpose(_o.ap, _s.ap, _i.ap)
    return P.op("tensor", fn, [src, ident], [out], tag)


def memset(P, dst, val, eng="vector", tag="memset"):
    return P.op(eng, lambda e, _d=dst, _v=val: e.memset(_d.ap, _v), [], [dst], tag)


def gbm(P, dst, src, small, gsize, op=ALU.mult, eng="vector", tag="gbm"):
    ng = small.shape[1]

    def fn(e, _d=dst, _s=src, _m=small, _op=op, _g=gsize, _ng=ng):
        dap = _d.ap.rearrange("p (g b) -> p g b", b=_g)
        sap = _s.ap.rearrange("p (g b) -> p g b", b=_g)
        map_ = _m.ap.unsqueeze(2).broadcast_to((_m.shape[0], _ng, _g))
        return e.tensor_tensor(dap, sap, map_, _op)

    return P.op(eng, fn, [src, small], [dst], tag)



def fp4_quant_multi(P, jobs, tag="q", act_rounds=False):
    def each(fn):
        for i, (ob, x, t) in enumerate(jobs):
            fn(i, ob, x, t)
    each(lambda i, ob, x, t: red(P, t["bm"], x, ALU.max, absval=True,
         src_fn=lambda ap: ap.rearrange("p (g b) -> p g b", b=16), tag=f"{tag}.bm{i}"))
    each(lambda i, ob, x, t: ts(P, t["bm"], t["bm"], 1.0 / 6.0, 1e-12,
         ALU.mult, ALU.max, tag=f"{tag}.s{i}"))
    each(lambda i, ob, x, t: recip(P, t["rs"], t["bm"], tag=f"{tag}.rs{i}"))
    each(lambda i, ob, x, t: gbm(P, t["w"], x, t["rs"], 16, tag=f"{tag}.w{i}"))
    each(lambda i, ob, x, t: act(P, t["A"], t["w"], AF.Abs, scale=2.0, tag=f"{tag}.a2{i}"))
    each(lambda i, ob, x, t: ts(P, t["A"], t["A"], MAGIC, -MAGIC, ALU.add, ALU.add, tag=f"{tag}.r1{i}"))
    each(lambda i, ob, x, t: ts(P, t["A"], t["A"], 0.5, 2.0, ALU.mult, ALU.min, tag=f"{tag}.t1{i}"))
    each(lambda i, ob, x, t: act(P, t["B"], t["w"], AF.Abs, tag=f"{tag}.a1{i}"))
    if act_rounds:
        each(lambda i, ob, x, t: act(P, t["B"], t["B"], AF.Identity, bias=MAGIC, tag=f"{tag}.r2a{i}"))
        each(lambda i, ob, x, t: act(P, t["B"], t["B"], AF.Identity, bias=-MAGIC, tag=f"{tag}.r2b{i}"))
    else:
        each(lambda i, ob, x, t: ts(P, t["B"], t["B"], MAGIC, -MAGIC, ALU.add, ALU.add, tag=f"{tag}.r2{i}"))
    each(lambda i, ob, x, t: ts(P, t["B"], t["B"], -2.0, 0.0, ALU.add, ALU.max, tag=f"{tag}.t2{i}"))
    each(lambda i, ob, x, t: ts(P, t["B"], t["B"], 2.0, None, ALU.min, tag=f"{tag}.t2b{i}"))
    each(lambda i, ob, x, t: act(P, t["C"], t["w"], AF.Abs, scale=0.5, tag=f"{tag}.a5{i}"))
    if act_rounds:
        each(lambda i, ob, x, t: act(P, t["C"], t["C"], AF.Identity, bias=MAGIC, tag=f"{tag}.r3a{i}"))
        each(lambda i, ob, x, t: act(P, t["C"], t["C"], AF.Identity, bias=-MAGIC, tag=f"{tag}.r3b{i}"))
    else:
        each(lambda i, ob, x, t: ts(P, t["C"], t["C"], MAGIC, -MAGIC, ALU.add, ALU.add, tag=f"{tag}.r3{i}"))
    each(lambda i, ob, x, t: ts(P, t["C"], t["C"], -2.0, 0.0, ALU.add, ALU.max, tag=f"{tag}.t4{i}"))
    each(lambda i, ob, x, t: ts(P, t["C"], t["C"], 2.0, None, ALU.mult, tag=f"{tag}.t4b{i}"))
    each(lambda i, ob, x, t: act(P, t["D"], t["w"], AF.Sign, tag=f"{tag}.sg{i}"))
    each(lambda i, ob, x, t: tt(P, t["A"], t["A"], t["B"], ALU.add, eng="gpsimd", tag=f"{tag}.u1{i}"))
    each(lambda i, ob, x, t: tt(P, t["A"], t["A"], t["C"], ALU.add, eng="gpsimd", tag=f"{tag}.u2{i}"))
    each(lambda i, ob, x, t: tt(P, t["A"], t["A"], t["D"], ALU.mult, eng="gpsimd", tag=f"{tag}.sx{i}"))
    each(lambda i, ob, x, t: gbm(P, ob, t["A"], t["bm"], 16, tag=f"{tag}.o{i}"))


def fp4_quant(P, out_bf, x, tmp, tag="q"):
    fp4_quant_multi(P, [(out_bf, x, tmp)], tag=tag)



def build_prog(debug_taps=()):
    P = Prog()

    def tap(nm):
        return "ExternalOutput" if nm in debug_taps else "Internal"

    io = {}
    io["x_in"] = P.dram("x_in", T, D, F32, kind="ExternalInput")
    io["gain"] = P.dram("gain", 1, 1, F32, kind="ExternalInput")
    for nm in ("w_q", "w_k", "w_v", "w_o"):
        io[nm] = P.dram(nm, D, D, F32, kind="ExternalInput")
    io["w_gate"] = P.dram("w_gate", I_FFN, D, F32, kind="ExternalInput")
    io["w_up"] = P.dram("w_up", I_FFN, D, F32, kind="ExternalInput")
    io["w_down"] = P.dram("w_down", D, I_FFN, F32, kind="ExternalInput")
    for nm in ("WqT", "WkT", "WvT", "WoT", "WfT"):
        io[nm] = P.dram(nm, D, D, F32, kind="ExternalInput")
    io["biases"] = P.dram("biases", 8, 512, F32, kind="ExternalInput")
    io["bo_row"] = P.dram("bo_row", 1, D, F32, kind="ExternalInput")
    io["bv_row"] = P.dram("bv_row", 1, D, F32, kind="ExternalInput")
    io["identf"] = P.dram("identf", 128, 128, F32, kind="ExternalInput")
    io["tri01"] = P.dram("tri01", 128, 128, BF16, kind="ExternalInput")
    io["consts"] = P.dram("consts", 128, 8, F32, kind="ExternalInput")
    io["out"] = P.dram("out", T, D, F32, kind="ExternalOutput")

    sc = {}
    sc["xn_tm"] = P.dram("xn_tm", T, D, F32, kind=tap("xn_tm"))
    sc["xn_bf"] = P.dram("xn_bf", T, D, BF16)
    sc["xq"] = P.dram("xq", T, D, BF16, kind=tap("xq"))
    sc["s_tm"] = P.dram("s_tm", T, D, F32, kind=tap("s_tm"))
    sc["x1"] = P.dram("x1", T, D, F32, kind=tap("x1"))
    sc["gate_s"] = P.dram("gate_s", T, I_FFN, F32, kind=tap("gate_s"))
    sc["attn_sc"] = P.dram("attn_sc", B_PER_CORE * NH * 64, 64, BF16)
    sc["hq"] = P.dram("hq", T, I_FFN, BF16, kind=tap("hq"))
    sc["xn1_bf"] = P.dram("xn1_bf", T, D, BF16, kind=tap("xn1_bf"))
    sc["qf"] = P.dram("qf", B_PER_CORE * D, S, BF16, kind=tap("qf"))
    sc["kf"] = P.dram("kf", B_PER_CORE * D, S, BF16, kind=tap("kf"))
    sc["vsc"] = P.dram("vsc", B_PER_CORE * NH * S, HD, BF16, kind=tap("vsc"))
    sc["g1"] = P.dram("g1", T, D, F32, kind=tap("g1"))
    sc["obsc"] = P.dram("obsc", B_PER_CORE * D, S, BF16, kind=tap("obsc"))
    for nm in ("wq_n", "wk_n", "wv_n", "wo_n"):
        sc[nm] = P.dram(nm, D, D, BF16)
    sc["wg_n"] = P.dram("wg_n", I_FFN, D, BF16)
    sc["wu_n"] = P.dram("wu_n", I_FFN, D, BF16)
    sc["wd_n"] = P.dram("wd_n", D, I_FFN, BF16)

    sb = {}
    sb["consts_sb"] = P.sbuf("consts_sb", 128, 8, F32)
    sb["identf"] = P.sbuf("identf_sb", 128, 128, F32)
    sb["tri01"] = P.sbuf("tri01_sb", 128, 128, BF16)
    sb["gain_bc"] = P.sbuf("gain_bc", 128, 1, F32)
    sb["ones_row"] = P.sbuf("ones_row", 1, 128, F32)
    sb["ones_col"] = P.sbuf("ones_col", 128, 1, F32)
    sb["bias_pp"] = P.sbuf("bias_pp", 128, 32, F32)
    sb["btmp"] = P.sbuf("btmp", 8, 512, F32)
    sb["grow"] = P.sbuf("grow", 1, D, F32)
    sb["bo_bc"] = P.sbuf("bo_bc", 128, D, F32)
    sb["bv_bc"] = P.sbuf("bv_bc", 128, D, F32)
    sb["attnT"] = P.sbuf("attnT", 128, B_PER_CORE * NH * HD, BF16)
    sb["wpool"] = P.sbuf("wpool", 128, 22528, BF16)
    sb["apool"] = P.sbuf("apool", 128, 18560, BF16)
    sb["fpool"] = P.sbuf("fpool", 128, 20480, F32)
    sb["bpool"] = P.sbuf("bpool", 128, 14336, BF16)

    pb = [P.psum(f"pb{i}", 128, 512) for i in range(8)]

    dma(P, "sync", sb["consts_sb"].full, io["consts"].full, tag="pre.c")
    dma(P, "sync", sb["identf"].full, io["identf"].full, tag="pre.i")
    dma(P, "sync", sb["tri01"].full, io["tri01"].full, tag="pre.t")
    P.n_preload = 3

    return dict(P=P, io=io, sc=sc, sb=sb, pb=pb, side=[])


class Alloc:
    def __init__(self, buf):
        self.buf = buf
        self.off = 0

    def take(self, cols, rows=None):
        r = self.buf.rows if rows is None else rows
        v = View(self.buf, 0, r, self.off, self.off + cols)
        self.off += cols
        assert self.off <= self.buf.cols, (self.buf.name, self.off)
        return v


def setup_misc(ctx):
    P, io, sb, pb = ctx["P"], ctx["io"], ctx["sb"], ctx["pb"]
    memset(P, sb["ones_row"].full, 1.0, tag="ones_r")
    memset(P, sb["ones_col"].full, 1.0, tag="ones_c")
    gtmp = sb["btmp"][0:1, 0:1]
    dma(P, "sync", gtmp, io["gain"].full, tag="gain.ld")
    mm(P, pb[0][0:128, 0:1], sb["ones_row"].full, gtmp, True, True, tag="gain.bc")
    copy(P, sb["gain_bc"].full, pb[0][0:128, 0:1], eng="scalar", tag="gain.ev")
    dma(P, "sync", sb["btmp"].full, io["biases"].full, tag="bias.ld")
    for i in range(4):
        pv = pb[1][0:128, 0:8]
        petr(P, pv, sb["btmp"][0:8, 128 * i:128 * (i + 1)], sb["identf"][0:8, 0:8],
             tag=f"bias.tr{i}")
        dstv = sb["bias_pp"][0:128, 8 * i:8 * i + 8]
        if i == 0:
            ts(P, dstv, pv, 0.125, None, ALU.mult, tag="bias.evq")
        else:
            copy(P, dstv, pv, eng="scalar", tag=f"bias.ev{i}")
    dma(P, "sync", sb["grow"].full, io["bo_row"].full, tag="bo.ld")
    for h in range(2):
        pv = pb[2][0:128, 0:512]
        mm(P, pv, sb["ones_row"].full, sb["grow"][0:1, 512 * h:512 * (h + 1)],
           True, True, tag="bo.bc")
        copy(P, sb["bo_bc"][0:128, 512 * h:512 * (h + 1)], pv, eng="scalar",
             tag="bo.ev")
    dma(P, "sync", sb["grow"].full, io["bv_row"].full, tag="bv.ld")
    for h in range(2):
        pv = pb[3][0:128, 0:512]
        mm(P, pv, sb["ones_row"].full, sb["grow"][0:1, 512 * h:512 * (h + 1)],
           True, True, tag="bv.bc")
        copy(P, sb["bv_bc"][0:128, 512 * h:512 * (h + 1)], pv, eng="scalar",
             tag="bv.ev")


def prep_mp_weights(ctx, names):
    P, io, sc, sb = ctx["P"], ctx["io"], ctx["sc"], ctx["sb"]
    f = Alloc(sb["fpool"])
    wt = [f.take(D) for _ in range(2)]
    dump = f.take(D)
    ssb = f.take(2)
    fcb = f.take(2)
    wbf_buf = Alloc(sb["bpool"])
    wb2 = [wbf_buf.take(D) for _ in range(2)]
    for nm in names:
        src = io[nm]
        dst = sc[nm.replace("w_", "w") + "_n"]
        for c in range(EC):
            b = c % 2
            dma(P, "sync", wt[b], src[128 * c:128 * (c + 1), 0:D], tag=f"{nm}.ld{c}")
            sv = ssb.sub(0, 128, b, b + 1)
            fv = fcb.sub(0, 128, b, b + 1)
            act(P, dump, wt[b], AF.Square, accum=sv, tag=f"{nm}.sq{c}")
            act(P, fv, sv, AF.Sqrt, tag=f"{nm}.rt{c}")
            ts(P, fv, fv, 32.0 * EPS_MP, None, ALU.add, tag=f"{nm}.ep{c}")
            recip(P, fv, fv, tag=f"{nm}.rc{c}")
            tt(P, fv, fv, sb["gain_bc"].full, ALU.mult, tag=f"{nm}.g{c}")
            ts(P, wb2[b], wt[b], fv, None, ALU.mult, tag=f"{nm}.sc{c}")
            dma(P, "gpsimd", dst[128 * c:128 * (c + 1), 0:D], wb2[b], tag=f"{nm}.st{c}")


def prep_fp4_weight(ctx, src_name, dst_name, nrows, pools):
    P, io, sc, sb = ctx["P"], ctx["io"], ctx["sc"], ctx["sb"]
    ncol = io[src_name].cols
    nh = 1 if ncol <= 2048 else 2
    hw = ncol // nh
    f = Alloc(sb["fpool"]); f.off = pools[0]
    wt = [f.take(ncol) for _ in range(2)]
    ntmp = 2 if nh == 1 else 1
    tmp = [dict(w=None, A=f.take(hw), B=f.take(hw), C=f.take(hw),
                D=f.take(hw), bm=f.take(hw // 16), rs=f.take(hw // 16))
           for _ in range(ntmp)]
    bb = Alloc(sb["bpool"]); bb.off = pools[1]
    ob = [bb.take(ncol) for _ in range(2)]
    nr = nrows // 128

    def mk_pair(c0):
        def emit():
            cs = [c for c in (c0, c0 + 1) if c < nr]
            jobs = []
            for c in cs:
                b = c % 2
                dma(P, "sync", wt[b], io[src_name][128 * c:128 * (c + 1), 0:ncol],
                    tag=f"{src_name}.ld{c}")
                for hf in range(nh):
                    t = dict(tmp[b % ntmp])
                    t["w"] = wt[b].sub(0, 128, hw * hf, hw * (hf + 1))
                    jobs.append((ob[b].sub(0, 128, hw * hf, hw * (hf + 1)),
                                 wt[b].sub(0, 128, hw * hf, hw * (hf + 1)), t))
            if ntmp >= len(jobs):
                fp4_quant_multi(P, jobs, tag=f"{src_name}.q{c0}")
            else:
                for ji, job in enumerate(jobs):
                    fp4_quant_multi(P, [job], tag=f"{src_name}.q{c0}.{ji}")
        
            for c in cs:
                b = c % 2
                dma(P, "gpsimd", sc[dst_name][128 * c:128 * (c + 1), 0:ncol], ob[b],
                    tag=f"{src_name}.st{c}")
        return emit

    step = 2 if nh == 1 else 1
    if step == 2:
        for c0 in range(0, nr, 2):
            ctx["side"].append(mk_pair(c0))
    else:
        def mk_one(c):
            def emit():
                b = c % 2
                dma(P, "sync", wt[b], io[src_name][128 * c:128 * (c + 1), 0:ncol],
                    tag=f"{src_name}.ld{c}")
                for hf in range(nh):
                    t = dict(tmp[0])
                    t["w"] = wt[b].sub(0, 128, hw * hf, hw * (hf + 1))
                    fp4_quant_multi(
                        P, [(ob[b].sub(0, 128, hw * hf, hw * (hf + 1)),
                             wt[b].sub(0, 128, hw * hf, hw * (hf + 1)), t)],
                        tag=f"{src_name}.q{c}.{hf}")
                dma(P, "gpsimd", sc[dst_name][128 * c:128 * (c + 1), 0:ncol], ob[b],
                    tag=f"{src_name}.st{c}")
            return emit
        for c in range(nr):
            ctx["side"].append(mk_one(c))


def load_wT(ctx, dst, src, tag):
    P = ctx["P"]
    dmat_blocks(P, "sync", dst, src, tag=f"{tag}.tr")


def load_l1_weight(ctx, dst, src_name, scale=None, sign=False, tag=""):
    P, io, sb = ctx["P"], ctx["io"], ctx["sb"]
    f = Alloc(sb["fpool"])
    wt = [f.take(D) for _ in range(2)]
    for c in range(EC):
        b = c % 2
        dma(P, "sync", wt[b], io[src_name][128 * c:128 * (c + 1), 0:D],
            tag=f"{tag}.ld{c}")
        dv = dst.sub(0, 128, D * c, D * (c + 1))
        if sign:
            act(P, dv, wt[b], AF.Sign, tag=f"{tag}.sg{c}")
        elif scale is not None:
            ts(P, dv, wt[b], scale, None, ALU.mult, tag=f"{tag}.sc{c}")
        else:
            copy(P, dv, wt[b], eng="vector", tag=f"{tag}.cp{c}")


def ln_tile2(P, x2, sqbuf, f, tag):
    m2 = f.take(2)
    ss = f.take(2)
    var = f.take(2)
    g3 = lambda ap: ap.rearrange("p (g b) -> p g b", b=D)
    red(P, m2, x2, ALU.add, src_fn=g3, tag=tag + ".mu")
    tt(P, sqbuf, x2, x2, ALU.mult, tag=tag + ".sq")
    red(P, ss, sqbuf, ALU.add, src_fn=g3, tag=tag + ".ss")
    ts(P, m2, m2, 1.0 / D, None, ALU.mult, tag=tag + ".m")
    tt(P, var, m2, m2, ALU.mult, tag=tag + ".m2")
    ts(P, ss, ss, 1.0 / D, 1e-5, ALU.mult, ALU.add, tag=tag + ".v")
    tt(P, ss, ss, var, ALU.subtract, tag=tag + ".ve")
    act(P, var, ss, AF.Sqrt, tag=tag + ".sd")
    recip(P, var, var, tag=tag + ".rs")
    gbm(P, x2, x2, m2, D, op=ALU.subtract, tag=tag + ".apm")
    gbm(P, x2, x2, var, D, op=ALU.mult, tag=tag + ".apr")


def ln_tile(P, xn, xt, f, tag):
    mean = f.take(1)
    ssum = f.take(1)
    var = f.take(1)
    dump = f.take(D)
    red(P, mean, xt, ALU.add, tag=tag + ".mu")
    act(P, dump, xt, AF.Square, accum=ssum, tag=tag + ".sq")
    ts(P, mean, mean, 1.0 / D, None, ALU.mult, tag=tag + ".m")
    tt(P, var, mean, mean, ALU.mult, tag=tag + ".m2")
    ts(P, ssum, ssum, 1.0 / D, var, ALU.mult, ALU.subtract, tag=tag + ".v")
    ts(P, ssum, ssum, 1e-5, None, ALU.add, tag=tag + ".ve")
    act(P, var, ssum, AF.Sqrt, tag=tag + ".sd")
    recip(P, var, var, tag=tag + ".rs")
    ts(P, xn, xt, mean, var, ALU.subtract, ALU.mult, tag=tag + ".ap")
    return xn


def mpnorm(P, out_bf, qf32, f, extra_scale, tag):
    sq = f.take(D)
    ss16 = f.take(NH)
    act(P, sq, qf32, AF.Square, tag=tag + ".sq")
    red(P, ss16, sq, ALU.add,
        src_fn=lambda ap: ap.rearrange("p (g b) -> p g b", b=HD), tag=tag + ".ss")
    act(P, ss16, ss16, AF.Sqrt, tag=tag + ".rt")
    ts(P, ss16, ss16, 0.125 * extra_scale, EPS_MP * extra_scale,
       ALU.mult, ALU.add, tag=tag + ".d")
    recip(P, ss16, ss16, tag=tag + ".rc")
    gbm(P, out_bf, qf32, ss16, HD, tag=tag + ".ap")


def emit_p0(ctx, wqT, wkT):
    P, io, sc, sb, pb = ctx["P"], ctx["io"], ctx["sc"], ctx["sb"], ctx["pb"]
    f = Alloc(sb["fpool"])
    xt2 = [f.take(2 * D) for _ in range(2)]
    qtmp2 = dict(w=f.take(2 * D), A=f.take(2 * D), B=f.take(2 * D),
                 C=f.take(2 * D), D=f.take(2 * D),
                 bm=f.take(2 * D // 16), rs=f.take(2 * D // 16))
    qfd = [f.take(D) for _ in range(2)]
    smalls = Alloc(sb["fpool"])
    smalls.off = f.off
    mpn_off = f.off + 64
    bb = Alloc(sb["bpool"])
    xnb2 = bb.take(2 * D)
    xqb2 = bb.take(2 * D)
    xnT = [bb.take(D) for _ in range(2)]
    qhd = [bb.take(D) for _ in range(2)]
    khd = [bb.take(D) for _ in range(2)]

    lag = 4
    for i in range(NT + lag):
        if i < NT and i % 2 == 0:
            i2 = i // 2
            b = i2 % 2
            x2 = xt2[b]
            dma(P, "sync", x2, io["x_in"][256 * i2:256 * (i2 + 1), 0:D],
                tag=f"p0.x{i2}",
                src_fn=lambda ap: ap.rearrange("(a p) c -> p a c", p=128))
            fs = Alloc(sb["fpool"]); fs.off = smalls.off
            ln_tile2(P, x2, qtmp2["A"], fs, tag=f"p0.ln{i2}")
            dma(P, "gpsimd", sc["xn_tm"][256 * i2:256 * (i2 + 1), 0:D], x2,
                tag=f"p0.sxn{i2}",
                dst_fn=lambda ap: ap.rearrange("(a p) c -> p a c", p=128))
            copy(P, xnb2, x2, eng="scalar", tag=f"p0.cb{i2}")
            dma(P, "gpsimd", sc["xn_bf"][256 * i2:256 * (i2 + 1), 0:D], xnb2,
                tag=f"p0.sxb{i2}",
                dst_fn=lambda ap: ap.rearrange("(a p) c -> p a c", p=128))
            fp4_quant(P, xqb2, x2, qtmp2, tag=f"p0.q{i2}")
            dma(P, "gpsimd", sc["xq"][256 * i2:256 * (i2 + 1), 0:D], xqb2,
                tag=f"p0.sxq{i2}",
                dst_fn=lambda ap: ap.rearrange("(a p) c -> p a c", p=128))
        j = i - lag
        j = i - lag
        if j < 0:
            continue
        bj = j % 2
        xT = xnT[bj]
        dmat_blocks(P, "sync", xT, sc["xn_bf"][128 * j:128 * (j + 1), 0:D],
                    tag=f"p0.xT{j}")
        bloc = j // NT_B
        tl = j % NT_B
        P.hint("tensor", [xT], tag=f"p0.h{j}")
        qhb = qhd[bj]
        khb = khd[bj]
        qf32 = qfd[bj]
        for (wT, pbs, outb, xsc, nm) in ((wqT, (0, 1), qhb, 1.0, "q"),
                                         (wkT, (2, 3), khb, SQS, "k")):
            for half in range(2):
                pv = pb[pbs[half]][0:128, 0:512]
                for ec in range(EC):
                    mm(P, pv, xT.sub(0, 128, 128 * ec, 128 * (ec + 1)),
                       wT.sub(0, 128, D * ec + 512 * half, D * ec + 512 * (half + 1)),
                       start=(ec == 0), stop=(ec == EC - 1),
                       tag=f"p0.{nm}mm{j}.{half}.{ec}")
            for half in range(2):
                copy(P, qf32.sub(0, 128, 512 * half, 512 * (half + 1)),
                     pb[pbs[half]][0:128, 0:512], eng="scalar",
                     tag=f"p0.{nm}ev{j}.{half}")
            fs = Alloc(sb["fpool"]); fs.off = mpn_off
            mpnorm(P, outb, qf32, fs, xsc, tag=f"p0.{nm}n{j}")
        for n in range(NH):
            bank = 4 + 2 * bloc + (n // 8)
            pv = pb[bank][0:64, 64 * (n % 8):64 * (n % 8) + 64]
            mm(P, pv, View(qhb.buf, 0, 128, qhb.c0 + 64 * n, qhb.c0 + 64 * (n + 1)),
               View(khb.buf, 0, 128, khb.c0 + 64 * n, khb.c0 + 64 * (n + 1)),
               start=(tl == 0 and n % 8 == 0), stop=(tl == NT_B - 1 and n % 8 == 7),
               tag=f"p0.sc{j}.{n}")

    f2 = Alloc(sb["fpool"])
    scf = f2.take(2 * D)
    mx = f2.take(2 * NH)
    pnorm = f2.take(2 * NH)
    for bloc in range(B_PER_CORE):
        for half in range(2):
            copy(P, scf.sub(0, 64, D * bloc + 512 * half, D * bloc + 512 * (half + 1)),
                 pb[4 + 2 * bloc + half][0:64, 0:512], eng="scalar",
                 tag=f"p0.scev{bloc}.{half}")
    sc64 = scf.sub(0, 64, 0, 2 * D)
    mxv = mx.sub(0, 64, 0, 2 * NH)
    red(P, mxv, sc64, ALU.max, negate=True,
        src_fn=lambda ap: ap.rearrange("p (g b) -> p g b", b=HD), tag="p0.smx")
    gbm(P, sc64, sc64, mxv, HD, op=ALU.add, tag="p0.ssub")
    act(P, sc64, sc64, AF.Exp, tag="p0.sexp")
    pn = pnorm.sub(0, 64, 0, 2 * NH)
    red(P, pn, sc64, ALU.add,
        src_fn=lambda ap: ap.rearrange("p (g b) -> p g b", b=HD), tag="p0.ssum")
    recip(P, pn, pn, tag="p0.srec")
    gbm(P, sc64, sc64, pn, HD, tag="p0.spn")
    memset(P, sb["attnT"].full, 0.0, tag="p0.atz")
    atmp = bb.take(64)
    for bn in range(B_PER_CORE * NH):
        pv = pb[bn % 2][0:64, 0:64]
        petr(P, pv, scf.sub(0, 64, 64 * bn, 64 * (bn + 1)),
             sb["identf"][0:64, 0:64], tag=f"p0.ptr{bn}")
        copy(P, View(atmp.buf, 0, 64, atmp.c0, atmp.c0 + 64), pv, eng="scalar",
             tag=f"p0.ptc{bn}")
        dma(P, "gpsimd", ctx["sc"]["attn_sc"][64 * bn:64 * (bn + 1), 0:64],
            View(atmp.buf, 0, 64, atmp.c0, atmp.c0 + 64), tag=f"p0.pts{bn}")
    for pair in range(B_PER_CORE * NH // 2):
        for hh in range(2):
            bn = 2 * pair + hh
            dma(P, "sync",
                sb["attnT"][64 * hh:64 * (hh + 1),
                            128 * pair + 64 * hh:128 * pair + 64 * (hh + 1)],
                ctx["sc"]["attn_sc"][64 * bn:64 * (bn + 1), 0:64],
                tag=f"p0.ptl{bn}")


def sc_x_tile(buf, i):
    return buf[128 * i:128 * (i + 1), 0:buf.cols]


def drain_side(ctx, k=1):
    for _ in range(k):
        if not ctx["side"]:
            return
        ctx["side"].pop(0)()


def emit_p1a(ctx, wvT, woT):
    P, sc, sb, pb = ctx["P"], ctx["sc"], ctx["sb"], ctx["pb"]
    f = Alloc(sb["fpool"])
    xnf = [f.take(D) for _ in range(2)]
    soutd = [f.take(D) for _ in range(2)]
    bb = Alloc(sb["bpool"])
    xnT = [bb.take(D) for _ in range(2)]
    vbd = [bb.take(D) for _ in range(2)]
    obd = [bb.take(D) for _ in range(2)]
    for j in range(NT):
        drain_side(ctx, 1)
        bj = j % 2
        xT = xnT[bj]
        vbf = vbd[bj]
        obf = obd[bj]
        sout = soutd[bj]
        dmat_blocks(P, "sync", xT, sc["xn_bf"][128 * j:128 * (j + 1), 0:D],
                    tag=f"p1a.xT{j}")
        dma(P, "sync", xnf[bj], sc_x_tile(sc["xn_tm"], j), tag=f"p1a.xn{j}")
        P.hint("tensor", [xT, sb["attnT"].full], tag=f"p1a.h{j}")
        bloc = j // NT_B
        for oc in range(EC):
            pv = pb[oc % 2][0:128, 0:128]
            for ec in range(EC):
                mm(P, pv, wvT.sub(0, 128, D * ec + 128 * oc, D * ec + 128 * (oc + 1)),
                   xT.sub(0, 128, 128 * ec, 128 * (ec + 1)),
                   start=(ec == 0), stop=(ec == EC - 1), tag=f"p1a.v{j}.{oc}.{ec}")
            copy(P, vbf.sub(0, 128, 128 * oc, 128 * (oc + 1)), pv, eng="scalar",
                 tag=f"p1a.vev{j}.{oc}")
        for oc in range(EC):
            pv = pb[2 + oc % 2]
            mm(P, pv[0:128, 0:128],
               sb["attnT"][0:128, 1024 * bloc + 128 * oc:1024 * bloc + 128 * (oc + 1)],
               vbf.sub(0, 128, 128 * oc, 128 * (oc + 1)),
               start=True, stop=True, tag=f"p1a.o{j}.{oc}")
            copy(P, obf.sub(0, 128, 128 * oc, 128 * (oc + 1)),
                 pv[0:128, 0:128], eng="scalar", tag=f"p1a.oev{j}.{oc}")
        for half in range(2):
            pv = pb[4 + half][0:128, 0:512]
            for oc in range(EC):
                mm(P, pv, obf.sub(0, 128, 128 * oc, 128 * (oc + 1)),
                   woT.sub(0, 128, D * oc + 512 * half, D * oc + 512 * (half + 1)),
                   start=(oc == 0), stop=(oc == EC - 1), tag=f"p1a.ao{j}.{half}.{oc}")
        for half in range(2):
            sv = sout.sub(0, 128, 512 * half, 512 * (half + 1))
            tt(P, sv, xnf[bj].sub(0, 128, 512 * half, 512 * (half + 1)),
               pb[4 + half][0:128, 0:512], ALU.add, tag=f"p1a.sadd{j}.{half}")
            ts(P, sv, sv, RS2, None, ALU.mult, tag=f"p1a.smul{j}.{half}")
        dma(P, "gpsimd", sc_x_tile(sc["s_tm"], j), sout, tag=f"p1a.ss{j}")


def emit_p1b_gate(ctx, wgT):
    P, sc, sb, pb = ctx["P"], ctx["sc"], ctx["sb"], ctx["pb"]
    f = Alloc(sb["fpool"])
    goutd = [f.take(I_FFN) for _ in range(2)]
    bb = Alloc(sb["bpool"])
    xqT = [bb.take(D) for _ in range(2)]
    nsub = (I_FFN + 511) // 512
    for j in range(NT):
        drain_side(ctx, 1)
        bj = j % 2
        xT = xqT[bj]
        gout = goutd[bj]
        dmat_blocks(P, "sync", xT, sc["xq"][128 * j:128 * (j + 1), 0:D],
                    tag=f"p1g.xT{j}")
        P.hint("tensor", [xT], tag=f"p1g.h{j}")
        for si in range(nsub):
            i0 = 512 * si
            nw = min(512, I_FFN - i0)
            pv = pb[si % 2][0:128, 0:nw]
            for ec in range(EC):
                mm(P, pv, xT.sub(0, 128, 128 * ec, 128 * (ec + 1)),
                   wgT.sub(0, 128, I_FFN * ec + i0, I_FFN * ec + i0 + nw),
                   start=(ec == 0), stop=(ec == EC - 1), tag=f"p1g.mm{j}.{si}.{ec}")
            act(P, gout.sub(0, 128, i0, i0 + nw), pv, AF.Silu,
                tag=f"p1g.si{j}.{si}")
        dma(P, "gpsimd", sc_x_tile(sc["gate_s"], j), gout, tag=f"p1g.st{j}")


def emit_p1b_up(ctx, wuT):
    P, sc, sb, pb = ctx["P"], ctx["sc"], ctx["sb"], ctx["pb"]
    f = Alloc(sb["fpool"])
    h = [f.take(I_FFN) for _ in range(2)]
    HB = I_FFN // 2
    qtmpd = [dict(w=None, A=f.take(HB), B=f.take(HB), C=f.take(HB),
                  D=f.take(HB), bm=f.take(HB // 16), rs=f.take(HB // 16))
             for _ in range(2)]
    bb = Alloc(sb["bpool"])
    xqT = [bb.take(D) for _ in range(2)]
    hqd = [bb.take(I_FFN) for _ in range(2)]
    nsub = (I_FFN + 511) // 512

    def stage_mm(j):
        bj = j % 2
        xT = xqT[bj]
        ht = h[bj]
        dmat_blocks(P, "sync", xT, sc["xq"][128 * j:128 * (j + 1), 0:D],
                    tag=f"p1u.xT{j}")
        dma(P, "sync", ht, sc_x_tile(sc["gate_s"], j), tag=f"p1u.g{j}")
        P.hint("tensor", [xT], tag=f"p1u.h{j}")
        for si in range(nsub):
            i0 = 512 * si
            nw = min(512, I_FFN - i0)
            pv = pb[si % 2][0:128, 0:nw]
            for ec in range(EC):
                mm(P, pv, xT.sub(0, 128, 128 * ec, 128 * (ec + 1)),
                   wuT.sub(0, 128, I_FFN * ec + i0, I_FFN * ec + i0 + nw),
                   start=(ec == 0), stop=(ec == EC - 1), tag=f"p1u.mm{j}.{si}.{ec}")
            tt(P, ht.sub(0, 128, i0, i0 + nw), ht.sub(0, 128, i0, i0 + nw),
               pv, ALU.mult, tag=f"p1u.h{j}.{si}")

    def stage_quant(j):
        bj = j % 2
        ht = h[bj]
        hqb = hqd[bj]
        jobs = []
        for hf in range(2):
            t = dict(qtmpd[hf])
            t["w"] = ht.sub(0, 128, HB * hf, HB * (hf + 1))
            jobs.append((hqb.sub(0, 128, HB * hf, HB * (hf + 1)),
                         ht.sub(0, 128, HB * hf, HB * (hf + 1)), t))
        fp4_quant_multi(P, jobs, tag=f"p1u.q{j}", act_rounds=True)
        dma(P, "gpsimd", sc_x_tile(sc["hq"], j), hqb, tag=f"p1u.st{j}")

    for j in range(NT + 1):
        if j < NT:
            stage_mm(j)
        if j >= 1:
            stage_quant(j - 1)

def emit_p1b_down(ctx, wdT):
    P, io, sc, sb, pb = ctx["P"], ctx["io"], ctx["sc"], ctx["sb"], ctx["pb"]
    f = Alloc(sb["fpool"])
    gd = [f.take(D) for _ in range(2)]
    xtile = [f.take(D) for _ in range(2)]
    stile = [f.take(D) for _ in range(2)]
    bb = Alloc(sb["bpool"])
    hqT = [bb.take(IC * 128) for _ in range(2)]
    for j in range(NT):
        bj = j % 2
        hT = hqT[bj]
        g = gd[bj]
        dmat_blocks(P, "sync", hT, sc["hq"][128 * j:128 * (j + 1), 0:I_FFN],
                    tag=f"p1d.hT{j}")
        dma(P, "sync", xtile[bj], sc_x_tile(io["x_in"], j), tag=f"p1d.x{j}")
        dma(P, "sync", stile[bj], sc_x_tile(sc["s_tm"], j), tag=f"p1d.s{j}")
        P.hint("tensor", [hT], tag=f"p1d.h{j}")
        for half in range(2):
            pv = pb[half][0:128, 0:512]
            for ic in range(IC):
                mm(P, pv, hT.sub(0, 128, 128 * ic, 128 * (ic + 1)),
                   wdT.sub(0, 128, D * ic + 512 * half, D * ic + 512 * (half + 1)),
                   start=(ic == 0), stop=(ic == IC - 1), tag=f"p1d.mm{j}.{half}.{ic}")
            gv = g.sub(0, 128, 512 * half, 512 * (half + 1))
            act(P, gv, pv, AF.Silu, tag=f"p1d.si{j}.{half}")
            tt(P, gv, gv, stile[bj].sub(0, 128, 512 * half, 512 * (half + 1)),
               ALU.mult, tag=f"p1d.sg{j}.{half}")
            tt(P, gv, gv, xtile[bj].sub(0, 128, 512 * half, 512 * (half + 1)),
               ALU.add, tag=f"p1d.x1{j}.{half}")
        dma(P, "gpsimd", sc_x_tile(sc["x1"], j), g, tag=f"p1d.st{j}")


def emit_p2a(ctx, WqTs, WkTs, t0=0, t1=NT, drain=0):
    P, sc, sb, pb = ctx["P"], ctx["sc"], ctx["sb"], ctx["pb"]
    f = Alloc(sb["fpool"])
    x1t2 = [f.take(2 * D) for _ in range(2)]
    sqb = f.take(2 * D)
    smalls = f.off
    bb = Alloc(sb["bpool"])
    xnb2 = bb.take(2 * D)
    xT = [bb.take(D) for _ in range(2)]
    qkd = [bb.take(D) for _ in range(2)]
    lag = 4
    for i in range(t0, t1 + lag):
        if drain:
            drain_side(ctx, drain)
        if i < t1 and i % 2 == 0:
            i2 = i // 2
            b = i2 % 2
            x2 = x1t2[b]
            dma(P, "sync", x2, sc["x1"][256 * i2:256 * (i2 + 1), 0:D],
                tag=f"p2.x{i2}",
                src_fn=lambda ap: ap.rearrange("(a p) c -> p a c", p=128))
            fs = Alloc(sb["fpool"]); fs.off = smalls
            ln_tile2(P, x2, sqb, fs, tag=f"p2.ln{i2}")
            copy(P, xnb2, x2, eng="scalar", tag=f"p2.cb{i2}")
            dma(P, "gpsimd", sc["xn1_bf"][256 * i2:256 * (i2 + 1), 0:D], xnb2,
                tag=f"p2.sxb{i2}",
                dst_fn=lambda ap: ap.rearrange("(a p) c -> p a c", p=128))
        j = i - lag
        if j < t0:
            continue
        bj = j % 2
        xTj = xT[bj]
        qkb = qkd[bj]
        dmat_blocks(P, "sync", xTj, sc["xn1_bf"][128 * j:128 * (j + 1), 0:D],
                    tag=f"p2.xT{j}")
        bloc = j // NT_B
        tl = j % NT_B
        P.hint("tensor", [xTj], tag=f"p2a.h{j}")
        for (wT, dst, bcol, nm) in ((WqTs, sc["qf"], 0, "q"), (WkTs, sc["kf"], 1, "k")):
            for oc in range(EC):
                pv = pb[oc % 2][0:128, 0:128]
                for ec in range(EC):
                    mm(P, pv, wT.sub(0, 128, D * ec + 128 * oc, D * ec + 128 * (oc + 1)),
                       xTj.sub(0, 128, 128 * ec, 128 * (ec + 1)),
                       start=(ec == 0), stop=(ec == EC - 1),
                       tag=f"p2.{nm}{j}.{oc}.{ec}")
                ov = qkb.sub(0, 128, 128 * oc, 128 * (oc + 1))
                ts(P, ov, pv, sb["bias_pp"][0:128, 8 * bcol + oc:8 * bcol + oc + 1],
                   None, ALU.add, tag=f"p2.{nm}b{j}.{oc}")
            dma(P, "gpsimd",
                dst[D * bloc:D * bloc + D, 128 * tl:128 * (tl + 1)],
                qkb, tag=f"p2.{nm}st{j}",
                dst_fn=lambda ap: ap.rearrange("(o p) c -> p o c", p=128),
                src_fn=lambda ap: ap.rearrange("p (o c) -> p o c", c=128))


def emit_p2b(ctx, WvTs, WfTs, t0=0, t1=NT, drain=0):
    P, sc, sb, pb = ctx["P"], ctx["sc"], ctx["sb"], ctx["pb"]
    f = Alloc(sb["fpool"])
    g1d = [f.take(D) for _ in range(2)]
    bb = Alloc(sb["bpool"])
    xT = [bb.take(D) for _ in range(2)]
    vbd = [bb.take(D) for _ in range(2)]
    for j in range(t0, t1):
        if drain:
            drain_side(ctx, drain)
        bj = j % 2
        xTj = xT[bj]
        vb = vbd[bj]
        g1t = g1d[bj]
        dmat_blocks(P, "sync", xTj, sc["xn1_bf"][128 * j:128 * (j + 1), 0:D],
                    tag=f"p2b.xT{j}")
        bloc = j // NT_B
        tl = j % NT_B
        P.hint("tensor", [xTj], tag=f"p2b.h{j}")
        for half in range(2):
            pv = pb[half][0:128, 0:512]
            for ec in range(EC):
                mm(P, pv, xTj.sub(0, 128, 128 * ec, 128 * (ec + 1)),
                   WvTs.sub(0, 128, D * ec + 512 * half, D * ec + 512 * (half + 1)),
                   start=(ec == 0), stop=(ec == EC - 1), tag=f"p2b.v{j}.{half}.{ec}")
            tt(P, vb.sub(0, 128, 512 * half, 512 * (half + 1)), pv,
               sb["bv_bc"][0:128, 512 * half:512 * (half + 1)], ALU.add,
               tag=f"p2b.vb{j}.{half}")
        tw = [sc["vsc"][(bloc * NH + n) * S + 128 * tl:
                        (bloc * NH + n) * S + 128 * tl + 128, 0:HD]
              for n in range(NH)]
        dma(P, "gpsimd", sc["vsc"].full, vb, tag=f"p2b.vst{j}",
            dst_fn=lambda ap, _b=bloc, _t=tl: ap.rearrange(
                "(b n q p) c -> b q p n c", b=B_PER_CORE, n=NH, p=128)[_b, _t],
            src_fn=lambda ap: ap.rearrange("p (n c) -> p n c", c=HD),
            track_writes=tw)
        for half in range(2):
            pv = pb[half][0:128, 0:512]
            for ec in range(EC):
                mm(P, pv, xTj.sub(0, 128, 128 * ec, 128 * (ec + 1)),
                   WfTs.sub(0, 128, D * ec + 512 * half, D * ec + 512 * (half + 1)),
                   start=(ec == 0), stop=(ec == EC - 1), tag=f"p2b.f{j}.{half}.{ec}")
            act(P, g1t.sub(0, 128, 512 * half, 512 * (half + 1)), pv, AF.Silu,
                tag=f"p2b.g1{j}.{half}")
        dma(P, "gpsimd", sc_x_tile(sc["g1"], j), g1t, tag=f"p2b.g1st{j}")


def emit_p3(ctx, WoTs):
    P, io, sc, sb, pb = ctx["P"], ctx["io"], ctx["sc"], ctx["sb"], ctx["pb"]
    a = Alloc(sb["apool"])
    qtd = [a.take(S, rows=65) for _ in range(2)]
    ktd = [a.take(S, rows=65) for _ in range(2)]
    vtd = [a.take(NH * 65) for _ in range(2)]
    PTd = [a.take(S) for _ in range(2)]
    opair = a.take(S)
    obt = [a.take(1024) for _ in range(2)]
    f = Alloc(sb["fpool"])
    q2 = f.take(S, rows=64)
    ssq = f.take(S, rows=1)
    aq = f.take(S, rows=1)
    kmax = f.take(1, rows=1)
    rl = f.take(S, rows=1)
    lbc = f.take(S, rows=64)
    s1d = [f.take(D) for _ in range(2)]
    x1d = [f.take(D) for _ in range(2)]
    g1d = [f.take(D) for _ in range(2)]
    for z in range(2):
        memset(P, View(vtd[z].buf, 0, 128, vtd[z].c0, vtd[z].c1), 1.0, tag=f"p3.vones{z}")
        memset(P, View(ktd[z].buf, 64, 65, ktd[z].c0, ktd[z].c1), 1.0, tag=f"p3.kones{z}")

    nsub = S // 512

    def stage_load_norms(bloc, n):
        qt, kt, vt = qtd[n % 2], ktd[n % 2], vtd[n % 2]
        qrow = D * bloc + HD * (n % 2) + 128 * (n // 2)
        dma(P, "sync", View(qt.buf, 0, 64, qt.c0, qt.c1),
            sc["qf"][qrow:qrow + 64, 0:S], tag=f"p3.q{bloc}.{n}")
        dma(P, "sync", View(kt.buf, 0, 64, kt.c0, kt.c1),
            sc["kf"][qrow:qrow + 64, 0:S], tag=f"p3.k{bloc}.{n}")
        vrow = (bloc * NH + n) * S
        dma(P, "sync", View(vt.buf, 0, 128, vt.c0, vt.c1),
            sc["vsc"][vrow:vrow + S, 0:HD], tag=f"p3.v{bloc}.{n}",
            dst_fn=lambda ap: ap.rearrange("p (j c) -> p j c", c=65)[:, :, 0:HD],
            src_fn=lambda ap: ap.rearrange("(j p) c -> p j c", p=128))
        qv = View(qt.buf, 0, 64, qt.c0, qt.c1)
        kv = View(kt.buf, 0, 64, kt.c0, kt.c1)
        tt(P, q2, qv, qv, ALU.mult, tag=f"p3.q2{bloc}.{n}")
        spots = [(2, 0), (2, 32), (2, 64), (3, 0)]
        for si in range(nsub):
            bk, p0 = spots[si]
            mm(P, pb[bk][p0:p0 + 1, 0:512], sb["ones_col"][0:64, 0:1],
               q2.sub(0, 64, 512 * si, 512 * (si + 1)), True, True,
               tag=f"p3.ssq{bloc}.{n}.{si}")
        for si in range(nsub):
            bk, p0 = spots[si]
            copy(P, aq.sub(0, 1, 512 * si, 512 * (si + 1)),
                 pb[bk][p0:p0 + 1, 0:512], eng="vector",
                 tag=f"p3.ssqe{bloc}.{n}.{si}")
        tt(P, q2, kv, kv, ALU.mult, tag=f"p3.k2{bloc}.{n}")
        for si in range(nsub):
            bk, p0 = spots[si]
            mm(P, pb[bk][p0:p0 + 1, 0:512], sb["ones_col"][0:64, 0:1],
               q2.sub(0, 64, 512 * si, 512 * (si + 1)), True, True,
               tag=f"p3.ssk{bloc}.{n}.{si}")
        for si in range(nsub):
            bk, p0 = spots[si]
            copy(P, ssq.sub(0, 1, 512 * si, 512 * (si + 1)),
                 pb[bk][p0:p0 + 1, 0:512], eng="vector",
                 tag=f"p3.sske{bloc}.{n}.{si}")
        red(P, kmax, ssq, ALU.max, absval=True, tag=f"p3.km{bloc}.{n}")
        ts(P, aq, aq, kmax, None, ALU.mult, tag=f"p3.mh1{bloc}.{n}")
        act(P, aq, aq, AF.Sqrt, tag=f"p3.mh2{bloc}.{n}")
        ts(P, aq, aq, -1.0, None, ALU.mult, tag=f"p3.mh3{bloc}.{n}")
        copy(P, View(qt.buf, 64, 65, qt.c0, qt.c1), aq, eng="vector",
             tag=f"p3.mhc{bloc}.{n}")

    def stage_jloop(bloc, n):
        qt, kt, vt = qtd[n % 2], ktd[n % 2], vtd[n % 2]
        segs = []
        for jb in range(NT_B):
            i0 = 128 * jb
            for q4 in range(4):
                ci = max(i0, 512 * q4)
                ce = 512 * (q4 + 1)
                if ci < ce:
                    segs.append((jb, q4, ci, ce - ci))
        nseg = len(segs)
        LAGS = 2
        for idx in range(nseg + LAGS):
            if LAGS <= idx < nseg + LAGS:
                jb, q4, ci, nw = segs[idx - LAGS]
                PT = PTd[(idx - LAGS) % 2]
                opv = pb[4 + q4][0:65, (ci % 512):(ci % 512) + nw]
                mm(P, opv, View(vt.buf, 0, 128, vt.c0 + 65 * jb, vt.c0 + 65 * (jb + 1)),
                   PT.sub(0, 128, ci, ci + nw),
                   start=(jb == 0), stop=(jb == min(4 * q4 + 3, NT_B - 1)),
                   tag=f"p3.O{bloc}.{n}.{jb}.{ci}")
            if idx < nseg:
                jb, q4, ci, nw = segs[idx]
                i0 = 128 * jb
                PT = PTd[idx % 2]
                spv = pb[2 + idx % 2][0:128, 0:nw]
                mm(P, spv, View(kt.buf, 0, 65, kt.c0 + i0, kt.c0 + i0 + 128),
                   View(qt.buf, 0, 65, qt.c0 + ci, qt.c0 + ci + nw),
                   start=True, stop=True, tag=f"p3.S{bloc}.{n}.{jb}.{ci}")
                act(P, PT.sub(0, 128, ci, ci + nw), spv, AF.Exp,
                    tag=f"p3.E{bloc}.{n}.{jb}.{ci}")
                if ci == i0:
                    tt(P, PT.sub(0, 128, i0, i0 + 128),
                       PT.sub(0, 128, i0, i0 + 128), sb["tri01"].full,
                       ALU.mult, eng="vector", tag=f"p3.D{bloc}.{n}.{jb}")

    def stage_norm_evac(bloc, n):
        for q4 in range(4):
            copy(P, rl.sub(0, 1, 512 * q4, 512 * (q4 + 1)),
                 pb[4 + q4][64:65, 0:512], eng="vector", tag=f"p3.rlc{bloc}.{n}.{q4}")
        act(P, rl, rl, AF.Ln, tag=f"p3.rln{bloc}.{n}")
        act(P, rl, rl, AF.Exp, scale=-1.0, tag=f"p3.rlx{bloc}.{n}")
        lspots = [(2, 0), (2, 64), (3, 0), (3, 64)]
        for q4 in range(4):
            bk, p0 = lspots[q4]
            mm(P, pb[bk][p0:p0 + 64, 0:512], sb["ones_row"][0:1, 0:64],
               rl.sub(0, 1, 512 * q4, 512 * (q4 + 1)), True, True,
               tag=f"p3.lb{bloc}.{n}.{q4}")
        for q4 in range(4):
            bk, p0 = lspots[q4]
            copy(P, lbc.sub(0, 64, 512 * q4, 512 * (q4 + 1)),
                 pb[bk][p0:p0 + 64, 0:512], eng="vector",
                 tag=f"p3.lbe{bloc}.{n}.{q4}")
        hh = n % 2
        for q4 in range(4):
            tt(P, opair.sub(64 * hh, 64 * hh + 64, 512 * q4, 512 * (q4 + 1)),
               pb[4 + q4][0:64, 0:512],
               lbc.sub(0, 64, 512 * q4, 512 * (q4 + 1)),
               ALU.mult, tag=f"p3.on{bloc}.{n}.{q4}")
        if hh == 1:
            oc = n // 2
            r0 = D * bloc + 128 * oc
            dma(P, "gpsimd", sc["obsc"][r0:r0 + 128, 0:S],
                View(opair.buf, 0, 128, opair.c0, opair.c1),
                tag=f"p3.ost{bloc}.{oc}")

    def head_iter(bloc, n):
        if n < NH:
            stage_load_norms(bloc, n)
        if n >= 1:
            stage_jloop(bloc, n - 1)
            stage_norm_evac(bloc, n - 1)

    def queue_heads(bloc):
        for n in range(NH + 1):
            ctx["side"].append(lambda _b=bloc, _n=n: head_iter(_b, _n))

    def run_heads(bloc, drain=0):
        for n in range(NH + 1):
            head_iter(bloc, n)
            if drain:
                drain_side(ctx, drain)

    def outproj(bloc):
        for tl in range(NT_B):
            outproj_tile(bloc, tl)

    def outproj_tile(bloc, tl):
            jt = bloc * NT_B + tl
            bj = tl % 2
            ot = obt[bj]
            s1 = s1d[bj]
            x1t = x1d[bj]
            g1t = g1d[bj]
            dma(P, "sync", ot,
                sc["obsc"][D * bloc:D * bloc + D, 128 * tl:128 * (tl + 1)],
                tag=f"p3.ol{jt}",
                dst_fn=lambda ap: ap.rearrange("p (o c) -> p o c", c=128),
                src_fn=lambda ap: ap.rearrange("(o p) c -> p o c", p=128))
            dma(P, "sync", x1t, sc_x_tile(sc["x1"], jt), tag=f"p3.x1{jt}")
            dma(P, "sync", g1t, sc_x_tile(sc["g1"], jt), tag=f"p3.g1{jt}")
            for half in range(2):
                pv = pb[half][0:128, 0:512]
                for oc in range(EC):
                    mm(P, pv, ot.sub(0, 128, 128 * oc, 128 * (oc + 1)),
                       WoTs.sub(0, 128, D * oc + 512 * half, D * oc + 512 * (half + 1)),
                       start=(oc == 0), stop=(oc == EC - 1),
                       tag=f"p3.pr{jt}.{half}.{oc}")
                sv = s1.sub(0, 128, 512 * half, 512 * (half + 1))
                tt(P, sv, pv, sb["bo_bc"][0:128, 512 * half:512 * (half + 1)],
                   ALU.add, tag=f"p3.bo{jt}.{half}")
                tt(P, sv, sv, g1t.sub(0, 128, 512 * half, 512 * (half + 1)),
                   ALU.mult, tag=f"p3.sg{jt}.{half}")
                tt(P, sv, sv, x1t.sub(0, 128, 512 * half, 512 * (half + 1)),
                   ALU.add, tag=f"p3.x2{jt}.{half}")
            dma(P, "gpsimd", sc_x_tile(io["out"], jt), s1, tag=f"p3.out{jt}")

    def queue_outproj(bloc):
        for tl in range(NT_B):
            ctx["side"].append(lambda _b=bloc, _t=tl: outproj_tile(_b, _t))

    ctx["p3"] = dict(queue_heads=queue_heads, run_heads=run_heads,
                     outproj=outproj, queue_outproj=queue_outproj)



def build_full(debug_taps=(), upto=99):
    ctx = build_prog(debug_taps)
    P, sb = ctx["P"], ctx["sb"]
    wp = Alloc(sb["wpool"])
    wA = wp.take(EC * D)
    wB = wp.take(EC * D)
    wBig = View(sb["wpool"], 0, 128, 0, EC * I_FFN)

    setup_misc(ctx)
    if upto >= 0:
        prep_mp_weights(ctx, ("w_q", "w_k"))
        load_wT(ctx, wA, ctx["sc"]["wq_n"].full, "wqT")
        load_wT(ctx, wB, ctx["sc"]["wk_n"].full, "wkT")
        emit_p0(ctx, wA, wB)
    SIDE_POOLS = (6144, 6144)
    if upto >= 1:
        prep_mp_weights(ctx, ("w_v", "w_o"))
        load_wT(ctx, wA, ctx["sc"]["wv_n"].full, "wvT")
        load_wT(ctx, wB, ctx["sc"]["wo_n"].full, "woT")
        if upto >= 2:
            prep_fp4_weight(ctx, "w_gate", "wg_n", I_FFN, SIDE_POOLS)
        emit_p1a(ctx, wA, wB)
        drain_side(ctx, 99)
    if upto >= 2:
        load_wT(ctx, wBig, ctx["sc"]["wg_n"].full, "wgT")
        if upto >= 3:
            prep_fp4_weight(ctx, "w_up", "wu_n", I_FFN, SIDE_POOLS)
        if upto >= 4:
            prep_fp4_weight(ctx, "w_down", "wd_n", D, SIDE_POOLS)
        emit_p1b_gate(ctx, wBig)
        drain_side(ctx, 99)
    if upto >= 3:
        load_wT(ctx, wBig, ctx["sc"]["wu_n"].full, "wuT")
        emit_p1b_up(ctx, wBig)
    if upto >= 4:
        load_wT(ctx, wBig, ctx["sc"]["wd_n"].full, "wdT")
        emit_p1b_down(ctx, wBig)
    if upto >= 5 and upto < 6:
        load_l1_weight(ctx, wA, "WqT", scale=0.125, tag="l1q")
        load_l1_weight(ctx, wB, "WkT", tag="l1k")
        emit_p2a(ctx, wA, wB)
        load_l1_weight(ctx, wC, "WvT", tag="l1v")
        load_l1_weight(ctx, wD, "WfT", sign=True, tag="l1f")
        emit_p2b(ctx, wC, wD)
    if upto >= 6:
        load_l1_weight(ctx, wA, "WqT", scale=0.125, tag="l1q")
        load_l1_weight(ctx, wB, "WkT", tag="l1k")
        emit_p2a(ctx, wA, wB, 0, NT // 2)
        load_l1_weight(ctx, wA, "WvT", tag="l1v0")
        load_l1_weight(ctx, wB, "WfT", sign=True, tag="l1f0")
        emit_p2b(ctx, wA, wB, 0, NT // 2)
        wO = View(ctx["sb"]["bpool"], 0, 128, 6144, 6144 + EC * D)
        load_l1_weight(ctx, wO, "WoT", tag="l1o")
        emit_p3(ctx, wO)
        ctx["p3"]["queue_heads"](0)
        load_l1_weight(ctx, wA, "WqT", scale=0.125, tag="l1q1")
        load_l1_weight(ctx, wB, "WkT", tag="l1k1")
        emit_p2a(ctx, wA, wB, NT // 2, NT, drain=1)
        load_l1_weight(ctx, wA, "WvT", tag="l1v1")
        load_l1_weight(ctx, wB, "WfT", sign=True, tag="l1f1")
        emit_p2b(ctx, wA, wB, NT // 2, NT, drain=1)
        drain_side(ctx, 99)
        ctx["p3"]["queue_outproj"](0)
        ctx["p3"]["run_heads"](1, drain=1)
        drain_side(ctx, 99)
        ctx["p3"]["outproj"](1)
    return ctx



def make_inputs_per_core(inputs, core):
    f32 = np.float32
    x = np.ascontiguousarray(
        inputs["L0_attn_query"][2 * core:2 * core + 2].reshape(T, D), dtype=f32)
    d = {"x_in": x}
    d["gain"] = np.asarray(inputs["L0_attn_gain"], f32).reshape(1, 1)
    d["w_q"] = np.ascontiguousarray(inputs["L0_attn_w_q"], f32)
    d["w_k"] = np.ascontiguousarray(inputs["L0_attn_w_k"], f32)
    d["w_v"] = np.ascontiguousarray(inputs["L0_attn_w_v"], f32)
    d["w_o"] = np.ascontiguousarray(inputs["L0_attn_w_o"], f32)
    d["w_gate"] = np.ascontiguousarray(inputs["L0_ffn_w_gate"], f32)
    d["w_up"] = np.ascontiguousarray(inputs["L0_ffn_w_up"], f32)
    d["w_down"] = np.ascontiguousarray(inputs["L0_ffn_w_down"], f32)
    d["WqT"] = np.ascontiguousarray(np.asarray(inputs["L1_attn_Wq"], f32).T)
    d["WkT"] = np.ascontiguousarray(np.asarray(inputs["L1_attn_Wk"], f32).T)
    d["WvT"] = np.ascontiguousarray(np.asarray(inputs["L1_attn_Wv"], f32).T)
    d["WoT"] = np.ascontiguousarray(np.asarray(inputs["L1_attn_Wo"], f32).T)
    d["WfT"] = np.ascontiguousarray(np.asarray(inputs["L1_ffn_weight"], f32).T)
    bias = np.concatenate([np.asarray(inputs["L1_attn_b" + k], f32).reshape(8, 128)
                           for k in ("q", "k", "v", "o")], axis=1)
    d["biases"] = np.ascontiguousarray(bias)
    d["bo_row"] = np.asarray(inputs["L1_attn_bo"], f32).reshape(1, D)
    d["bv_row"] = np.asarray(inputs["L1_attn_bv"], f32).reshape(1, D)
    d["identf"] = np.eye(128, dtype=f32)
    tri = (np.arange(128)[None, :] >= np.arange(128)[:, None])
    d["tri01"] = tri.astype(ml_dtypes.bfloat16)
    d["consts"] = np.tile(np.array(CONST_VALS + [0.0] * (8 - len(CONST_VALS)),
                                   dtype=f32)[None, :], (128, 1))
    return d


def kernel(**inputs):
    ctx = build_full()
    nc, stack = ctx["P"].build()
    try:
        in_maps = [make_inputs_per_core(inputs, c) for c in range(8)]
        res = run_bass_kernel_spmd(nc, in_maps, core_ids=list(range(8)))
    finally:
        stack.close()
    out = np.concatenate(
        [res.results[c]["out"].reshape(B_PER_CORE, S, D) for c in range(8)], axis=0)
    return out.astype(np.float32)


# version 21
# speedup vs baseline: 1.0114x; 1.0114x over previous
import contextlib
import math

import numpy as np
import ml_dtypes

import concourse.bass as bass
import concourse.mybir as mybir
from concourse.bass_utils import run_bass_kernel_spmd

F32 = mybir.dt.float32
BF16 = mybir.dt.bfloat16
AF = mybir.ActivationFunctionType
ALU = mybir.AluOpType
AX = mybir.AxisListType

MAGIC = 12582912.0
RS2 = 0.7071067811865476
EPS_MP = 1e-4
SQS = 45.254833995939045

B_PER_CORE = 2
S = 2048
T = B_PER_CORE * S
NT = T // 128
NT_B = S // 128
D = 1024
EC = D // 128
I_FFN = 2816
IC = I_FFN // 128
NH = 16
HD = 64

ENGINES = ("sync", "scalar", "vector", "tensor", "gpsimd")
DMA_QUEUES = ("sync", "scalar", "gpsimd")
CONST_VALS = [2.0, 0.5, -0.5, -2.0, -4.0, MAGIC, -MAGIC]



class Buf:
    __slots__ = ("space", "name", "rows", "cols", "dtype", "kind", "h",
                 "wrecs", "rrecs")

    def __init__(self, space, name, rows, cols, dtype, kind=None):
        self.space, self.name = space, name
        self.rows, self.cols, self.dtype, self.kind = rows, cols, dtype, kind
        self.h = None
        self.wrecs = []
        self.rrecs = []

    def __getitem__(self, idx):
        rs, cs = idx
        if isinstance(rs, slice):
            r0, r1, _ = rs.indices(self.rows)
        else:
            r0, r1 = rs, rs + 1
        if isinstance(cs, slice):
            c0, c1, _ = cs.indices(self.cols)
        else:
            c0, c1 = cs, cs + 1
        return View(self, r0, r1, c0, c1)

    @property
    def full(self):
        return View(self, 0, self.rows, 0, self.cols)


class View:
    __slots__ = ("buf", "r0", "r1", "c0", "c1")

    def __init__(self, buf, r0, r1, c0, c1):
        self.buf, self.r0, self.r1, self.c0, self.c1 = buf, r0, r1, c0, c1

    @property
    def ap(self):
        return self.buf.h[self.r0:self.r1, self.c0:self.c1]

    def sub(self, r0, r1, c0, c1):
        return View(self.buf, self.r0 + r0, self.r0 + r1, self.c0 + c0, self.c0 + c1)

    def rect(self):
        if self.buf.space == "psum":
            return (0, self.buf.rows, 0, self.buf.cols)
        return (self.r0, self.r1, self.c0, self.c1)

    @property
    def shape(self):
        return (self.r1 - self.r0, self.c1 - self.c0)


def _ovl(a, b):
    return a[0] < b[1] and b[0] < a[1] and a[2] < b[3] and b[2] < a[3]


class Op:
    __slots__ = ("idx", "engine", "fn", "is_dma", "tag", "inc_needed",
                 "inc_count", "wait_edges", "narrow", "minspan", "has_psum")

    def __init__(self, idx, engine, fn, is_dma, tag):
        self.idx, self.engine, self.fn, self.is_dma, self.tag = \
            idx, engine, fn, is_dma, tag
        self.inc_needed = is_dma
        self.inc_count = None
        self.wait_edges = []


class Prog:
    def __init__(self):
        self.ops = []
        self.bufs = []
        self.n_preload = 0

    def sbuf(self, name, rows, cols, dtype):
        b = Buf("sbuf", name, rows, cols, dtype)
        self.bufs.append(b)
        return b

    def psum(self, name, rows, cols):
        b = Buf("psum", name, rows, cols, F32)
        self.bufs.append(b)
        return b

    def dram(self, name, rows, cols, dtype, kind="Internal"):
        b = Buf("dram", name, rows, cols, dtype, kind)
        self.bufs.append(b)
        return b

    def hint(self, engine, views, tag="hint"):
        o = Op(len(self.ops), engine, None, False, tag)
        o.has_psum = False
        o.minspan = 0
        o.narrow = False
        edges = set()
        for v in views:
            r = v.rect()
            for rec in v.buf.wrecs:
                if _ovl(r, rec):
                    edges.add(rec[4])
        for pidx in edges:
            p = self.ops[pidx]
            if p.engine == engine and not p.is_dma:
                continue
            o.wait_edges.append(p)
            p.inc_needed = True
        self.ops.append(o)
        return o

    def op(self, engine, fn, reads=(), writes=(), tag="", is_dma=False):
        o = Op(len(self.ops), engine, fn, is_dma, tag)
        allv = list(reads) + list(writes)
        o.has_psum = any(v.buf.space == "psum" for v in allv)
        o.minspan = min((v.c1 - v.c0 for v in allv), default=0)
        if is_dma:
            spans = [v.r1 - v.r0 for v in allv if v.buf.space == "sbuf"]
            o.narrow = bool(spans) and min(spans) < 128
        else:
            o.narrow = False
        edges = set()
        for v in reads:
            r = v.rect()
            for rec in v.buf.wrecs:
                if _ovl(r, rec):
                    edges.add(rec[4])
        for v in writes:
            r = v.rect()
            for rec in v.buf.wrecs:
                if _ovl(r, rec):
                    edges.add(rec[4])
            for rec in v.buf.rrecs:
                if _ovl(r, rec):
                    edges.add(rec[4])
        for pidx in edges:
            p = self.ops[pidx]
            if (p.engine == "tensor" and engine == "tensor"
                    and not p.is_dma and not is_dma):
                continue
            o.wait_edges.append(p)
            p.inc_needed = True
        for v in writes:
            r = v.rect()
            def cov(rec, _r=r):
                return (_r[0] <= rec[0] and rec[1] <= _r[1]
                        and _r[2] <= rec[2] and rec[3] <= _r[3])
            v.buf.wrecs = [rec for rec in v.buf.wrecs if not cov(rec)]
            v.buf.rrecs = [rec for rec in v.buf.rrecs if not cov(rec)]
            v.buf.wrecs.append((*r, o.idx))
        for v in reads:
            v.buf.rrecs.append((*v.rect(), o.idx))
        self.ops.append(o)
        return o

    def build(self):
        nc = bass.Bass("TRN2", target_bir_lowering=False)
        stack = contextlib.ExitStack()
        for b in self.bufs:
            if b.space == "dram":
                b.h = nc.dram_tensor(b.name, [b.rows, b.cols], b.dtype,
                                     kind=b.kind).ap()
        for b in self.bufs:
            if b.space == "sbuf":
                b.h = stack.enter_context(
                    nc.sbuf_tensor(b.name, [b.rows, b.cols], b.dtype)).ap()
            elif b.space == "psum":
                b.h = stack.enter_context(
                    nc.psum_tensor(b.name, [b.rows, b.cols], b.dtype)).ap()
        cb = next(b for b in self.bufs if b.name == "consts_sb")
        for i, val in enumerate(CONST_VALS):
            nc.const_aps.register(F32, val, cb.h[:, i:i + 1])

        RSEM_Q = {"sync": 40, "scalar": 8, "gpsimd": 32}
        sem_counts = {}
        queue_counts = {}
        for o in self.ops:
            if o.is_dma:
                i = queue_counts.get(o.engine, 0)
                queue_counts[o.engine] = i + 1
                o.inc_count = i
            elif o.inc_needed:
                key = "e_" + o.engine
                sem_counts[key] = sem_counts.get(key, 0) + 1
                o.inc_count = sem_counts[key]
        sems = {k: stack.enter_context(nc.semaphore(k)) for k in sem_counts}
        dpool = {}
        for q in queue_counts:
            dpool[q] = [stack.enter_context(nc.semaphore(f"d_{q}{r}"))
                        for r in range(RSEM_Q[q])]

        per_engine = {e: [] for e in ENGINES}
        for o in self.ops:
            per_engine[o.engine].append(o)
        n_pre = self.n_preload

        block = stack.enter_context(nc.Block())

        def dma_key_val(p, consumer=True):
            R = RSEM_Q[p.engine]
            slot = p.inc_count % R
            rnd = p.inc_count // R
            return ("dq", p.engine, slot), 16 * (rnd + (1 if consumer else 0))

        def mk(ename):
            def run(eng):
                wm = {}
                if n_pre and ename != "sync" and "sync" in dpool:
                    for i in range(n_pre):
                        eng.wait_ge(dpool["sync"][i % RSEM_Q["sync"]], 16)
                        wm[("dq", "sync", i % RSEM_Q["sync"])] = 16
                for o in per_engine[ename]:
                    for p in o.wait_edges:
                        if p.is_dma:
                            key, val = dma_key_val(p, consumer=True)
                            sem = dpool[p.engine][key[2]]
                        else:
                            key = "e_" + p.engine
                            val = p.inc_count
                            sem = sems[key]
                        if wm.get(key, 0) >= val:
                            continue
                        eng.wait_ge(sem, val)
                        wm[key] = val
                    if o.is_dma:
                        key, val = dma_key_val(o, consumer=False)
                        if val > 0 and wm.get(key, 0) < val:
                            eng.wait_ge(dpool[o.engine][key[2]], val)
                            wm[key] = val
                    if o.fn is None:
                        continue
                    instr = o.fn(eng)
                    if o.is_dma:
                        instr.then_inc(dpool[o.engine][o.inc_count % RSEM_Q[o.engine]], 16)
                    elif o.inc_needed:
                        instr.then_inc(sems["e_" + o.engine], 1)
                if ename == "gpsimd":
                    for q, cnt in queue_counts.items():
                        R = RSEM_Q[q]
                        for r in range(R):
                            n_r = (cnt - 1 - r) // R + 1 if r < cnt else 0
                            if n_r > 0:
                                eng.wait_ge(dpool[q][r], 16 * n_r)
            return run

        for ename in ENGINES:
            getattr(block, ename)(mk(ename))
        return nc, stack



def dma(P, queue, dst, src, tag="", dst_fn=None, src_fn=None,
        track_writes=None, track_reads=None):
    def fn(e, _d=dst, _s=src):
        dap = dst_fn(_d.ap) if dst_fn else _d.ap
        sap = src_fn(_s.ap) if src_fn else _s.ap
        return e.dma_start(out=dap, in_=sap)
    return P.op(queue, fn, track_reads or [src], track_writes or [dst], tag,
                is_dma=True)


def dmat(P, queue, dst, src, tag=""):
    def fn(e, _d=dst, _s=src):
        return e.dma_start_transpose(_d.ap, _s.ap)
    return P.op(queue, fn, [src], [dst], tag, is_dma=True)


def dmat_blocks(P, queue, dst, src, tag=""):
    R = src.r1 - src.r0
    def fn(e, _d=dst, _s=src, _R=R):
        dap = _d.ap.rearrange("p (k r) -> p k r", r=_R)
        return e.dma_start_transpose(dap, _s.ap)
    return P.op(queue, fn, [src], [dst], tag, is_dma=True)


def ts(P, dst, src, s1, s2=None, op0=ALU.mult, op1=None, accum=None, tag="ts"):
    reads = [src] + [x for x in (s1, s2) if isinstance(x, View)]
    writes = [dst] + ([accum] if accum is not None else [])

    def fn(e, _d=dst, _s=src, _1=s1, _2=s2, _a=accum, _o0=op0, _o1=op1):
        k1 = _1.ap if isinstance(_1, View) else _1
        k2 = _2.ap if isinstance(_2, View) else _2
        kw = {"accum_out": _a.ap} if _a is not None else {}
        if _o1 is None:
            return e.tensor_scalar(_d.ap, _s.ap, k1, None, _o0, **kw)
        return e.tensor_scalar(_d.ap, _s.ap, k1, k2, _o0, _o1, **kw)

    return P.op("vector", fn, reads, writes, tag)


def tt(P, dst, a, b, op=ALU.mult, eng="vector", tag="tt",
       dst_fn=None, a_fn=None, b_fn=None):
    def fn(e, _d=dst, _a=a, _b=b, _op=op):
        dap = dst_fn(_d.ap) if dst_fn else _d.ap
        aap = a_fn(_a.ap) if a_fn else _a.ap
        bap = b_fn(_b.ap) if b_fn else _b.ap
        return e.tensor_tensor(dap, aap, bap, _op)
    return P.op(eng, fn, [a, b], [dst], tag)


def act(P, dst, src, func, bias=None, scale=None, accum=None, tag="act"):
    reads = [src] + ([bias] if isinstance(bias, View) else [])
    writes = [dst] + ([accum] if accum is not None else [])

    def fn(e, _d=dst, _s=src, _b=bias, _sc=scale, _a=accum, _f=func):
        kw = {}
        if _b is not None:
            kw["bias"] = _b.ap if isinstance(_b, View) else _b
        if _sc is not None:
            kw["scale"] = _sc
        if _a is not None:
            kw["accum_out"] = _a.ap
        return e.activation(_d.ap, _s.ap, _f, **kw)

    return P.op("scalar", fn, reads, writes, tag)


def red(P, dst, src, op=ALU.max, absval=False, negate=False, src_fn=None,
        tag="red"):
    def fn(e, _d=dst, _s=src, _op=op):
        sap = src_fn(_s.ap) if src_fn else _s.ap
        kw = {}
        if absval:
            kw["apply_absolute_value"] = True
        if negate:
            kw["negate"] = True
        return e.tensor_reduce(_d.ap, sap, AX.X, _op, **kw)
    return P.op("vector", fn, [src], [dst], tag)


def copy(P, dst, src, eng="vector", tag="copy"):
    if eng == "scalar":
        def fn(e, _d=dst, _s=src):
            return e.activation(_d.ap, _s.ap, AF.Copy)
    else:
        def fn(e, _d=dst, _s=src):
            return e.tensor_copy(_d.ap, _s.ap)
    return P.op(eng, fn, [src], [dst], tag)


def recip(P, dst, src, tag="recip"):
    return P.op("vector", lambda e, _d=dst, _s=src: e.reciprocal(_d.ap, _s.ap),
                [src], [dst], tag)


def mm(P, out, lhsT, rhs, start, stop, tag="mm"):
    def fn(e, _o=out, _l=lhsT, _r=rhs, _st=start, _sp=stop):
        return e.matmul(_o.ap, _l.ap, _r.ap, start=_st, stop=_sp)
    return P.op("tensor", fn, [lhsT, rhs], [out], tag)


def petr(P, out, src, ident, tag="ptr"):
    def fn(e, _o=out, _s=src, _i=ident):
        return e.transpose(_o.ap, _s.ap, _i.ap)
    return P.op("tensor", fn, [src, ident], [out], tag)


def memset(P, dst, val, eng="vector", tag="memset"):
    return P.op(eng, lambda e, _d=dst, _v=val: e.memset(_d.ap, _v), [], [dst], tag)


def gbm(P, dst, src, small, gsize, op=ALU.mult, eng="vector", tag="gbm"):
    ng = small.shape[1]

    def fn(e, _d=dst, _s=src, _m=small, _op=op, _g=gsize, _ng=ng):
        dap = _d.ap.rearrange("p (g b) -> p g b", b=_g)
        sap = _s.ap.rearrange("p (g b) -> p g b", b=_g)
        map_ = _m.ap.unsqueeze(2).broadcast_to((_m.shape[0], _ng, _g))
        return e.tensor_tensor(dap, sap, map_, _op)

    return P.op(eng, fn, [src, small], [dst], tag)



def fp4_quant_multi(P, jobs, tag="q", act_rounds=False):
    def each(fn):
        for i, (ob, x, t) in enumerate(jobs):
            fn(i, ob, x, t)
    each(lambda i, ob, x, t: red(P, t["bm"], x, ALU.max, absval=True,
         src_fn=lambda ap: ap.rearrange("p (g b) -> p g b", b=16), tag=f"{tag}.bm{i}"))
    each(lambda i, ob, x, t: ts(P, t["bm"], t["bm"], 1.0 / 6.0, 1e-12,
         ALU.mult, ALU.max, tag=f"{tag}.s{i}"))
    each(lambda i, ob, x, t: recip(P, t["rs"], t["bm"], tag=f"{tag}.rs{i}"))
    each(lambda i, ob, x, t: gbm(P, t["w"], x, t["rs"], 16, tag=f"{tag}.w{i}"))
    each(lambda i, ob, x, t: act(P, t["A"], t["w"], AF.Abs, scale=2.0, tag=f"{tag}.a2{i}"))
    each(lambda i, ob, x, t: ts(P, t["A"], t["A"], MAGIC, -MAGIC, ALU.add, ALU.add, tag=f"{tag}.r1{i}"))
    each(lambda i, ob, x, t: ts(P, t["A"], t["A"], 0.5, 2.0, ALU.mult, ALU.min, tag=f"{tag}.t1{i}"))
    each(lambda i, ob, x, t: act(P, t["B"], t["w"], AF.Abs, tag=f"{tag}.a1{i}"))
    if act_rounds:
        each(lambda i, ob, x, t: act(P, t["B"], t["B"], AF.Identity, bias=MAGIC, tag=f"{tag}.r2a{i}"))
        each(lambda i, ob, x, t: act(P, t["B"], t["B"], AF.Identity, bias=-MAGIC, tag=f"{tag}.r2b{i}"))
    else:
        each(lambda i, ob, x, t: ts(P, t["B"], t["B"], MAGIC, -MAGIC, ALU.add, ALU.add, tag=f"{tag}.r2{i}"))
    each(lambda i, ob, x, t: ts(P, t["B"], t["B"], -2.0, 0.0, ALU.add, ALU.max, tag=f"{tag}.t2{i}"))
    each(lambda i, ob, x, t: ts(P, t["B"], t["B"], 2.0, None, ALU.min, tag=f"{tag}.t2b{i}"))
    each(lambda i, ob, x, t: act(P, t["C"], t["w"], AF.Abs, scale=0.5, tag=f"{tag}.a5{i}"))
    if act_rounds:
        each(lambda i, ob, x, t: act(P, t["C"], t["C"], AF.Identity, bias=MAGIC, tag=f"{tag}.r3a{i}"))
        each(lambda i, ob, x, t: act(P, t["C"], t["C"], AF.Identity, bias=-MAGIC, tag=f"{tag}.r3b{i}"))
    else:
        each(lambda i, ob, x, t: ts(P, t["C"], t["C"], MAGIC, -MAGIC, ALU.add, ALU.add, tag=f"{tag}.r3{i}"))
    each(lambda i, ob, x, t: ts(P, t["C"], t["C"], -2.0, 0.0, ALU.add, ALU.max, tag=f"{tag}.t4{i}"))
    each(lambda i, ob, x, t: ts(P, t["C"], t["C"], 2.0, None, ALU.mult, tag=f"{tag}.t4b{i}"))
    each(lambda i, ob, x, t: act(P, t["D"], t["w"], AF.Sign, tag=f"{tag}.sg{i}"))
    each(lambda i, ob, x, t: tt(P, t["A"], t["A"], t["B"], ALU.add, eng="gpsimd", tag=f"{tag}.u1{i}"))
    each(lambda i, ob, x, t: tt(P, t["A"], t["A"], t["C"], ALU.add, eng="gpsimd", tag=f"{tag}.u2{i}"))
    each(lambda i, ob, x, t: tt(P, t["A"], t["A"], t["D"], ALU.mult, eng="gpsimd", tag=f"{tag}.sx{i}"))
    each(lambda i, ob, x, t: gbm(P, ob, t["A"], t["bm"], 16, tag=f"{tag}.o{i}"))


def fp4_quant(P, out_bf, x, tmp, tag="q"):
    fp4_quant_multi(P, [(out_bf, x, tmp)], tag=tag)



def build_prog(debug_taps=()):
    P = Prog()

    def tap(nm):
        return "ExternalOutput" if nm in debug_taps else "Internal"

    io = {}
    io["x_in"] = P.dram("x_in", T, D, F32, kind="ExternalInput")
    io["gain"] = P.dram("gain", 1, 1, F32, kind="ExternalInput")
    for nm in ("w_q", "w_k", "w_v", "w_o"):
        io[nm] = P.dram(nm, D, D, F32, kind="ExternalInput")
    io["w_gate"] = P.dram("w_gate", I_FFN, D, F32, kind="ExternalInput")
    io["w_up"] = P.dram("w_up", I_FFN, D, F32, kind="ExternalInput")
    io["w_down"] = P.dram("w_down", D, I_FFN, F32, kind="ExternalInput")
    for nm in ("WqT", "WkT", "WvT", "WoT", "WfT"):
        io[nm] = P.dram(nm, D, D, F32, kind="ExternalInput")
    io["biases"] = P.dram("biases", 8, 512, F32, kind="ExternalInput")
    io["bo_row"] = P.dram("bo_row", 1, D, F32, kind="ExternalInput")
    io["bv_row"] = P.dram("bv_row", 1, D, F32, kind="ExternalInput")
    io["identf"] = P.dram("identf", 128, 128, F32, kind="ExternalInput")
    io["tri01"] = P.dram("tri01", 128, 128, BF16, kind="ExternalInput")
    io["consts"] = P.dram("consts", 128, 8, F32, kind="ExternalInput")
    io["out"] = P.dram("out", T, D, F32, kind="ExternalOutput")

    sc = {}
    sc["xn_tm"] = P.dram("xn_tm", T, D, F32, kind=tap("xn_tm"))
    sc["xn_bf"] = P.dram("xn_bf", T, D, BF16)
    sc["xq"] = P.dram("xq", T, D, BF16, kind=tap("xq"))
    sc["s_tm"] = P.dram("s_tm", T, D, F32, kind=tap("s_tm"))
    sc["x1"] = P.dram("x1", T, D, F32, kind=tap("x1"))
    sc["gate_s"] = P.dram("gate_s", T, I_FFN, F32, kind=tap("gate_s"))
    sc["attn_sc"] = P.dram("attn_sc", B_PER_CORE * NH * 64, 64, BF16)
    sc["hq"] = P.dram("hq", T, I_FFN, BF16, kind=tap("hq"))
    sc["xn1_bf"] = P.dram("xn1_bf", T, D, BF16, kind=tap("xn1_bf"))
    sc["qf"] = P.dram("qf", B_PER_CORE * D, S, BF16, kind=tap("qf"))
    sc["kf"] = P.dram("kf", B_PER_CORE * D, S, BF16, kind=tap("kf"))
    sc["vsc"] = P.dram("vsc", B_PER_CORE * NH * S, HD, BF16, kind=tap("vsc"))
    sc["g1"] = P.dram("g1", T, D, F32, kind=tap("g1"))
    sc["obsc"] = P.dram("obsc", B_PER_CORE * D, S, BF16, kind=tap("obsc"))
    for nm in ("wq_n", "wk_n", "wv_n", "wo_n"):
        sc[nm] = P.dram(nm, D, D, BF16)
    sc["wg_n"] = P.dram("wg_n", I_FFN, D, BF16)
    sc["wu_n"] = P.dram("wu_n", I_FFN, D, BF16)
    sc["wd_n"] = P.dram("wd_n", D, I_FFN, BF16)

    sb = {}
    sb["consts_sb"] = P.sbuf("consts_sb", 128, 8, F32)
    sb["identf"] = P.sbuf("identf_sb", 128, 128, F32)
    sb["tri01"] = P.sbuf("tri01_sb", 128, 128, BF16)
    sb["gain_bc"] = P.sbuf("gain_bc", 128, 1, F32)
    sb["ones_row"] = P.sbuf("ones_row", 1, 128, F32)
    sb["ones_col"] = P.sbuf("ones_col", 128, 1, F32)
    sb["bias_pp"] = P.sbuf("bias_pp", 128, 32, F32)
    sb["btmp"] = P.sbuf("btmp", 8, 512, F32)
    sb["grow"] = P.sbuf("grow", 1, D, F32)
    sb["bo_bc"] = P.sbuf("bo_bc", 128, D, F32)
    sb["bv_bc"] = P.sbuf("bv_bc", 128, D, F32)
    sb["attnT"] = P.sbuf("attnT", 128, B_PER_CORE * NH * HD, BF16)
    sb["wpool"] = P.sbuf("wpool", 128, 22528, BF16)
    sb["apool"] = P.sbuf("apool", 128, 18560, BF16)
    sb["fpool"] = P.sbuf("fpool", 128, 20480, F32)
    sb["bpool"] = P.sbuf("bpool", 128, 14336, BF16)

    pb = [P.psum(f"pb{i}", 128, 512) for i in range(8)]

    dma(P, "sync", sb["consts_sb"].full, io["consts"].full, tag="pre.c")
    dma(P, "sync", sb["identf"].full, io["identf"].full, tag="pre.i")
    dma(P, "sync", sb["tri01"].full, io["tri01"].full, tag="pre.t")
    P.n_preload = 3

    return dict(P=P, io=io, sc=sc, sb=sb, pb=pb, side=[])


class Alloc:
    def __init__(self, buf):
        self.buf = buf
        self.off = 0

    def take(self, cols, rows=None):
        r = self.buf.rows if rows is None else rows
        v = View(self.buf, 0, r, self.off, self.off + cols)
        self.off += cols
        assert self.off <= self.buf.cols, (self.buf.name, self.off)
        return v


def setup_misc(ctx):
    P, io, sb, pb = ctx["P"], ctx["io"], ctx["sb"], ctx["pb"]
    memset(P, sb["ones_row"].full, 1.0, tag="ones_r")
    memset(P, sb["ones_col"].full, 1.0, tag="ones_c")
    gtmp = sb["btmp"][0:1, 0:1]
    dma(P, "sync", gtmp, io["gain"].full, tag="gain.ld")
    mm(P, pb[0][0:128, 0:1], sb["ones_row"].full, gtmp, True, True, tag="gain.bc")
    copy(P, sb["gain_bc"].full, pb[0][0:128, 0:1], eng="scalar", tag="gain.ev")
    dma(P, "sync", sb["btmp"].full, io["biases"].full, tag="bias.ld")
    for i in range(4):
        pv = pb[1][0:128, 0:8]
        petr(P, pv, sb["btmp"][0:8, 128 * i:128 * (i + 1)], sb["identf"][0:8, 0:8],
             tag=f"bias.tr{i}")
        dstv = sb["bias_pp"][0:128, 8 * i:8 * i + 8]
        if i == 0:
            ts(P, dstv, pv, 0.125, None, ALU.mult, tag="bias.evq")
        else:
            copy(P, dstv, pv, eng="scalar", tag=f"bias.ev{i}")
    dma(P, "sync", sb["grow"].full, io["bo_row"].full, tag="bo.ld")
    for h in range(2):
        pv = pb[2][0:128, 0:512]
        mm(P, pv, sb["ones_row"].full, sb["grow"][0:1, 512 * h:512 * (h + 1)],
           True, True, tag="bo.bc")
        copy(P, sb["bo_bc"][0:128, 512 * h:512 * (h + 1)], pv, eng="scalar",
             tag="bo.ev")
    dma(P, "sync", sb["grow"].full, io["bv_row"].full, tag="bv.ld")
    for h in range(2):
        pv = pb[3][0:128, 0:512]
        mm(P, pv, sb["ones_row"].full, sb["grow"][0:1, 512 * h:512 * (h + 1)],
           True, True, tag="bv.bc")
        copy(P, sb["bv_bc"][0:128, 512 * h:512 * (h + 1)], pv, eng="scalar",
             tag="bv.ev")


def prep_mp_weights(ctx, names):
    P, io, sc, sb = ctx["P"], ctx["io"], ctx["sc"], ctx["sb"]
    f = Alloc(sb["fpool"])
    wt = [f.take(D) for _ in range(2)]
    dump = f.take(D)
    ssb = f.take(2)
    fcb = f.take(2)
    wbf_buf = Alloc(sb["bpool"])
    wb2 = [wbf_buf.take(D) for _ in range(2)]
    for nm in names:
        src = io[nm]
        dst = sc[nm.replace("w_", "w") + "_n"]
        for c in range(EC):
            b = c % 2
            dma(P, "sync", wt[b], src[128 * c:128 * (c + 1), 0:D], tag=f"{nm}.ld{c}")
            sv = ssb.sub(0, 128, b, b + 1)
            fv = fcb.sub(0, 128, b, b + 1)
            act(P, dump, wt[b], AF.Square, accum=sv, tag=f"{nm}.sq{c}")
            act(P, fv, sv, AF.Sqrt, tag=f"{nm}.rt{c}")
            ts(P, fv, fv, 32.0 * EPS_MP, None, ALU.add, tag=f"{nm}.ep{c}")
            recip(P, fv, fv, tag=f"{nm}.rc{c}")
            tt(P, fv, fv, sb["gain_bc"].full, ALU.mult, tag=f"{nm}.g{c}")
            ts(P, wb2[b], wt[b], fv, None, ALU.mult, tag=f"{nm}.sc{c}")
            dma(P, "gpsimd", dst[128 * c:128 * (c + 1), 0:D], wb2[b], tag=f"{nm}.st{c}")


def prep_fp4_weight(ctx, src_name, dst_name, nrows, pools):
    P, io, sc, sb = ctx["P"], ctx["io"], ctx["sc"], ctx["sb"]
    ncol = io[src_name].cols
    nh = 1 if ncol <= 2048 else 2
    hw = ncol // nh
    f = Alloc(sb["fpool"]); f.off = pools[0]
    wt = [f.take(ncol) for _ in range(2)]
    ntmp = 2 if nh == 1 else 1
    tmp = [dict(w=None, A=f.take(hw), B=f.take(hw), C=f.take(hw),
                D=f.take(hw), bm=f.take(hw // 16), rs=f.take(hw // 16))
           for _ in range(ntmp)]
    bb = Alloc(sb["bpool"]); bb.off = pools[1]
    ob = [bb.take(ncol) for _ in range(2)]
    nr = nrows // 128

    def mk_pair(c0):
        def emit():
            cs = [c for c in (c0, c0 + 1) if c < nr]
            jobs = []
            for c in cs:
                b = c % 2
                dma(P, "sync", wt[b], io[src_name][128 * c:128 * (c + 1), 0:ncol],
                    tag=f"{src_name}.ld{c}")
                for hf in range(nh):
                    t = dict(tmp[b % ntmp])
                    t["w"] = wt[b].sub(0, 128, hw * hf, hw * (hf + 1))
                    jobs.append((ob[b].sub(0, 128, hw * hf, hw * (hf + 1)),
                                 wt[b].sub(0, 128, hw * hf, hw * (hf + 1)), t))
            if ntmp >= len(jobs):
                fp4_quant_multi(P, jobs, tag=f"{src_name}.q{c0}")
            else:
                for ji, job in enumerate(jobs):
                    fp4_quant_multi(P, [job], tag=f"{src_name}.q{c0}.{ji}")
        
            for c in cs:
                b = c % 2
                dma(P, "gpsimd", sc[dst_name][128 * c:128 * (c + 1), 0:ncol], ob[b],
                    tag=f"{src_name}.st{c}")
        return emit

    step = 2 if nh == 1 else 1
    if step == 2:
        for c0 in range(0, nr, 2):
            ctx["side"].append(mk_pair(c0))
    else:
        def mk_one(c):
            def emit():
                b = c % 2
                dma(P, "sync", wt[b], io[src_name][128 * c:128 * (c + 1), 0:ncol],
                    tag=f"{src_name}.ld{c}")
                for hf in range(nh):
                    t = dict(tmp[0])
                    t["w"] = wt[b].sub(0, 128, hw * hf, hw * (hf + 1))
                    fp4_quant_multi(
                        P, [(ob[b].sub(0, 128, hw * hf, hw * (hf + 1)),
                             wt[b].sub(0, 128, hw * hf, hw * (hf + 1)), t)],
                        tag=f"{src_name}.q{c}.{hf}")
                dma(P, "gpsimd", sc[dst_name][128 * c:128 * (c + 1), 0:ncol], ob[b],
                    tag=f"{src_name}.st{c}")
            return emit
        for c in range(nr):
            ctx["side"].append(mk_one(c))


def load_wT(ctx, dst, src, tag):
    P = ctx["P"]
    dmat_blocks(P, "sync", dst, src, tag=f"{tag}.tr")


def load_l1_weight(ctx, dst, src_name, scale=None, sign=False, tag=""):
    P, io, sb = ctx["P"], ctx["io"], ctx["sb"]
    f = Alloc(sb["fpool"])
    wt = [f.take(D) for _ in range(2)]
    for c in range(EC):
        b = c % 2
        dma(P, "sync", wt[b], io[src_name][128 * c:128 * (c + 1), 0:D],
            tag=f"{tag}.ld{c}")
        dv = dst.sub(0, 128, D * c, D * (c + 1))
        if sign:
            act(P, dv, wt[b], AF.Sign, tag=f"{tag}.sg{c}")
        elif scale is not None:
            ts(P, dv, wt[b], scale, None, ALU.mult, tag=f"{tag}.sc{c}")
        else:
            copy(P, dv, wt[b], eng="vector", tag=f"{tag}.cp{c}")


def ln_tile2(P, x2, sqbuf, f, tag):
    m2 = f.take(2)
    ss = f.take(2)
    var = f.take(2)
    g3 = lambda ap: ap.rearrange("p (g b) -> p g b", b=D)
    red(P, m2, x2, ALU.add, src_fn=g3, tag=tag + ".mu")
    tt(P, sqbuf, x2, x2, ALU.mult, tag=tag + ".sq")
    red(P, ss, sqbuf, ALU.add, src_fn=g3, tag=tag + ".ss")
    ts(P, m2, m2, 1.0 / D, None, ALU.mult, tag=tag + ".m")
    tt(P, var, m2, m2, ALU.mult, tag=tag + ".m2")
    ts(P, ss, ss, 1.0 / D, 1e-5, ALU.mult, ALU.add, tag=tag + ".v")
    tt(P, ss, ss, var, ALU.subtract, tag=tag + ".ve")
    act(P, var, ss, AF.Sqrt, tag=tag + ".sd")
    recip(P, var, var, tag=tag + ".rs")
    gbm(P, x2, x2, m2, D, op=ALU.subtract, tag=tag + ".apm")
    gbm(P, x2, x2, var, D, op=ALU.mult, tag=tag + ".apr")


def ln_tile(P, xn, xt, f, tag):
    mean = f.take(1)
    ssum = f.take(1)
    var = f.take(1)
    dump = f.take(D)
    red(P, mean, xt, ALU.add, tag=tag + ".mu")
    act(P, dump, xt, AF.Square, accum=ssum, tag=tag + ".sq")
    ts(P, mean, mean, 1.0 / D, None, ALU.mult, tag=tag + ".m")
    tt(P, var, mean, mean, ALU.mult, tag=tag + ".m2")
    ts(P, ssum, ssum, 1.0 / D, var, ALU.mult, ALU.subtract, tag=tag + ".v")
    ts(P, ssum, ssum, 1e-5, None, ALU.add, tag=tag + ".ve")
    act(P, var, ssum, AF.Sqrt, tag=tag + ".sd")
    recip(P, var, var, tag=tag + ".rs")
    ts(P, xn, xt, mean, var, ALU.subtract, ALU.mult, tag=tag + ".ap")
    return xn


def mpnorm(P, out_bf, qf32, f, extra_scale, tag):
    sq = f.take(D)
    ss16 = f.take(NH)
    act(P, sq, qf32, AF.Square, tag=tag + ".sq")
    red(P, ss16, sq, ALU.add,
        src_fn=lambda ap: ap.rearrange("p (g b) -> p g b", b=HD), tag=tag + ".ss")
    act(P, ss16, ss16, AF.Sqrt, tag=tag + ".rt")
    ts(P, ss16, ss16, 0.125 * extra_scale, EPS_MP * extra_scale,
       ALU.mult, ALU.add, tag=tag + ".d")
    recip(P, ss16, ss16, tag=tag + ".rc")
    gbm(P, out_bf, qf32, ss16, HD, tag=tag + ".ap")


def emit_p0(ctx, wqT, wkT):
    P, io, sc, sb, pb = ctx["P"], ctx["io"], ctx["sc"], ctx["sb"], ctx["pb"]
    f = Alloc(sb["fpool"])
    xt2 = [f.take(2 * D) for _ in range(2)]
    qtmp2 = dict(w=f.take(2 * D), A=f.take(2 * D), B=f.take(2 * D),
                 C=f.take(2 * D), D=f.take(2 * D),
                 bm=f.take(2 * D // 16), rs=f.take(2 * D // 16))
    qfd = [f.take(D) for _ in range(2)]
    smalls = Alloc(sb["fpool"])
    smalls.off = f.off
    mpn_off = f.off + 64
    bb = Alloc(sb["bpool"])
    xnb2 = bb.take(2 * D)
    xqb2 = bb.take(2 * D)
    xnT = [bb.take(D) for _ in range(2)]
    qhd = [bb.take(D) for _ in range(2)]
    khd = [bb.take(D) for _ in range(2)]

    lag = 4
    for i in range(NT + lag):
        if i < NT and i % 2 == 0:
            i2 = i // 2
            b = i2 % 2
            x2 = xt2[b]
            dma(P, "sync", x2, io["x_in"][256 * i2:256 * (i2 + 1), 0:D],
                tag=f"p0.x{i2}",
                src_fn=lambda ap: ap.rearrange("(a p) c -> p a c", p=128))
            fs = Alloc(sb["fpool"]); fs.off = smalls.off
            ln_tile2(P, x2, qtmp2["A"], fs, tag=f"p0.ln{i2}")
            dma(P, "gpsimd", sc["xn_tm"][256 * i2:256 * (i2 + 1), 0:D], x2,
                tag=f"p0.sxn{i2}",
                dst_fn=lambda ap: ap.rearrange("(a p) c -> p a c", p=128))
            copy(P, xnb2, x2, eng="scalar", tag=f"p0.cb{i2}")
            dma(P, "gpsimd", sc["xn_bf"][256 * i2:256 * (i2 + 1), 0:D], xnb2,
                tag=f"p0.sxb{i2}",
                dst_fn=lambda ap: ap.rearrange("(a p) c -> p a c", p=128))
            fp4_quant(P, xqb2, x2, qtmp2, tag=f"p0.q{i2}")
            dma(P, "gpsimd", sc["xq"][256 * i2:256 * (i2 + 1), 0:D], xqb2,
                tag=f"p0.sxq{i2}",
                dst_fn=lambda ap: ap.rearrange("(a p) c -> p a c", p=128))
        j = i - lag
        j = i - lag
        if j < 0:
            continue
        bj = j % 2
        xT = xnT[bj]
        dmat_blocks(P, "sync", xT, sc["xn_bf"][128 * j:128 * (j + 1), 0:D],
                    tag=f"p0.xT{j}")
        bloc = j // NT_B
        tl = j % NT_B
        P.hint("tensor", [xT], tag=f"p0.h{j}")
        qhb = qhd[bj]
        khb = khd[bj]
        qf32 = qfd[bj]
        for (wT, pbs, outb, xsc, nm) in ((wqT, (0, 1), qhb, 1.0, "q"),
                                         (wkT, (2, 3), khb, SQS, "k")):
            for half in range(2):
                pv = pb[pbs[half]][0:128, 0:512]
                for ec in range(EC):
                    mm(P, pv, xT.sub(0, 128, 128 * ec, 128 * (ec + 1)),
                       wT.sub(0, 128, D * ec + 512 * half, D * ec + 512 * (half + 1)),
                       start=(ec == 0), stop=(ec == EC - 1),
                       tag=f"p0.{nm}mm{j}.{half}.{ec}")
            for half in range(2):
                copy(P, qf32.sub(0, 128, 512 * half, 512 * (half + 1)),
                     pb[pbs[half]][0:128, 0:512], eng="scalar",
                     tag=f"p0.{nm}ev{j}.{half}")
            fs = Alloc(sb["fpool"]); fs.off = mpn_off
            mpnorm(P, outb, qf32, fs, xsc, tag=f"p0.{nm}n{j}")
        for n in range(NH):
            bank = 4 + 2 * bloc + (n // 8)
            pv = pb[bank][0:64, 64 * (n % 8):64 * (n % 8) + 64]
            mm(P, pv, View(qhb.buf, 0, 128, qhb.c0 + 64 * n, qhb.c0 + 64 * (n + 1)),
               View(khb.buf, 0, 128, khb.c0 + 64 * n, khb.c0 + 64 * (n + 1)),
               start=(tl == 0 and n % 8 == 0), stop=(tl == NT_B - 1 and n % 8 == 7),
               tag=f"p0.sc{j}.{n}")

    f2 = Alloc(sb["fpool"])
    scf = f2.take(2 * D)
    mx = f2.take(2 * NH)
    pnorm = f2.take(2 * NH)
    for bloc in range(B_PER_CORE):
        for half in range(2):
            copy(P, scf.sub(0, 64, D * bloc + 512 * half, D * bloc + 512 * (half + 1)),
                 pb[4 + 2 * bloc + half][0:64, 0:512], eng="scalar",
                 tag=f"p0.scev{bloc}.{half}")
    sc64 = scf.sub(0, 64, 0, 2 * D)
    mxv = mx.sub(0, 64, 0, 2 * NH)
    red(P, mxv, sc64, ALU.max, negate=True,
        src_fn=lambda ap: ap.rearrange("p (g b) -> p g b", b=HD), tag="p0.smx")
    gbm(P, sc64, sc64, mxv, HD, op=ALU.add, tag="p0.ssub")
    act(P, sc64, sc64, AF.Exp, tag="p0.sexp")
    pn = pnorm.sub(0, 64, 0, 2 * NH)
    red(P, pn, sc64, ALU.add,
        src_fn=lambda ap: ap.rearrange("p (g b) -> p g b", b=HD), tag="p0.ssum")
    recip(P, pn, pn, tag="p0.srec")
    gbm(P, sc64, sc64, pn, HD, tag="p0.spn")
    memset(P, sb["attnT"].full, 0.0, tag="p0.atz")
    atmp = bb.take(64)
    for bn in range(B_PER_CORE * NH):
        pv = pb[bn % 2][0:64, 0:64]
        petr(P, pv, scf.sub(0, 64, 64 * bn, 64 * (bn + 1)),
             sb["identf"][0:64, 0:64], tag=f"p0.ptr{bn}")
        copy(P, View(atmp.buf, 0, 64, atmp.c0, atmp.c0 + 64), pv, eng="scalar",
             tag=f"p0.ptc{bn}")
        dma(P, "gpsimd", ctx["sc"]["attn_sc"][64 * bn:64 * (bn + 1), 0:64],
            View(atmp.buf, 0, 64, atmp.c0, atmp.c0 + 64), tag=f"p0.pts{bn}")
    for pair in range(B_PER_CORE * NH // 2):
        for hh in range(2):
            bn = 2 * pair + hh
            dma(P, "sync",
                sb["attnT"][64 * hh:64 * (hh + 1),
                            128 * pair + 64 * hh:128 * pair + 64 * (hh + 1)],
                ctx["sc"]["attn_sc"][64 * bn:64 * (bn + 1), 0:64],
                tag=f"p0.ptl{bn}")


def sc_x_tile(buf, i):
    return buf[128 * i:128 * (i + 1), 0:buf.cols]


def drain_side(ctx, k=1):
    for _ in range(k):
        if not ctx["side"]:
            return
        ctx["side"].pop(0)()


def emit_p1a(ctx, wvT, woT):
    P, sc, sb, pb = ctx["P"], ctx["sc"], ctx["sb"], ctx["pb"]
    f = Alloc(sb["fpool"])
    xnf = [f.take(D) for _ in range(2)]
    soutd = [f.take(D) for _ in range(2)]
    bb = Alloc(sb["bpool"])
    xnT = [bb.take(D) for _ in range(2)]
    vbd = [bb.take(D) for _ in range(2)]
    obd = [bb.take(D) for _ in range(2)]
    for j in range(NT):
        drain_side(ctx, 1)
        bj = j % 2
        xT = xnT[bj]
        vbf = vbd[bj]
        obf = obd[bj]
        sout = soutd[bj]
        dmat_blocks(P, "sync", xT, sc["xn_bf"][128 * j:128 * (j + 1), 0:D],
                    tag=f"p1a.xT{j}")
        dma(P, "sync", xnf[bj], sc_x_tile(sc["xn_tm"], j), tag=f"p1a.xn{j}")
        P.hint("tensor", [xT, sb["attnT"].full], tag=f"p1a.h{j}")
        bloc = j // NT_B
        for oc in range(EC):
            pv = pb[oc % 2][0:128, 0:128]
            for ec in range(EC):
                mm(P, pv, wvT.sub(0, 128, D * ec + 128 * oc, D * ec + 128 * (oc + 1)),
                   xT.sub(0, 128, 128 * ec, 128 * (ec + 1)),
                   start=(ec == 0), stop=(ec == EC - 1), tag=f"p1a.v{j}.{oc}.{ec}")
            copy(P, vbf.sub(0, 128, 128 * oc, 128 * (oc + 1)), pv, eng="scalar",
                 tag=f"p1a.vev{j}.{oc}")
        for oc in range(EC):
            pv = pb[2 + oc % 2]
            mm(P, pv[0:128, 0:128],
               sb["attnT"][0:128, 1024 * bloc + 128 * oc:1024 * bloc + 128 * (oc + 1)],
               vbf.sub(0, 128, 128 * oc, 128 * (oc + 1)),
               start=True, stop=True, tag=f"p1a.o{j}.{oc}")
            copy(P, obf.sub(0, 128, 128 * oc, 128 * (oc + 1)),
                 pv[0:128, 0:128], eng="scalar", tag=f"p1a.oev{j}.{oc}")
        for half in range(2):
            pv = pb[4 + half][0:128, 0:512]
            for oc in range(EC):
                mm(P, pv, obf.sub(0, 128, 128 * oc, 128 * (oc + 1)),
                   woT.sub(0, 128, D * oc + 512 * half, D * oc + 512 * (half + 1)),
                   start=(oc == 0), stop=(oc == EC - 1), tag=f"p1a.ao{j}.{half}.{oc}")
        for half in range(2):
            sv = sout.sub(0, 128, 512 * half, 512 * (half + 1))
            tt(P, sv, xnf[bj].sub(0, 128, 512 * half, 512 * (half + 1)),
               pb[4 + half][0:128, 0:512], ALU.add, tag=f"p1a.sadd{j}.{half}")
            ts(P, sv, sv, RS2, None, ALU.mult, tag=f"p1a.smul{j}.{half}")
        dma(P, "gpsimd", sc_x_tile(sc["s_tm"], j), sout, tag=f"p1a.ss{j}")


def emit_p1b_gate(ctx, wgT):
    P, sc, sb, pb = ctx["P"], ctx["sc"], ctx["sb"], ctx["pb"]
    f = Alloc(sb["fpool"])
    goutd = [f.take(I_FFN) for _ in range(2)]
    bb = Alloc(sb["bpool"])
    xqT = [bb.take(D) for _ in range(2)]
    nsub = (I_FFN + 511) // 512
    for j in range(NT):
        drain_side(ctx, 1)
        bj = j % 2
        xT = xqT[bj]
        gout = goutd[bj]
        dmat_blocks(P, "sync", xT, sc["xq"][128 * j:128 * (j + 1), 0:D],
                    tag=f"p1g.xT{j}")
        P.hint("tensor", [xT], tag=f"p1g.h{j}")
        for si in range(nsub):
            i0 = 512 * si
            nw = min(512, I_FFN - i0)
            pv = pb[si % 4][0:128, 0:nw]
            for ec in range(EC):
                mm(P, pv, xT.sub(0, 128, 128 * ec, 128 * (ec + 1)),
                   wgT.sub(0, 128, I_FFN * ec + i0, I_FFN * ec + i0 + nw),
                   start=(ec == 0), stop=(ec == EC - 1), tag=f"p1g.mm{j}.{si}.{ec}")
            act(P, gout.sub(0, 128, i0, i0 + nw), pv, AF.Silu,
                tag=f"p1g.si{j}.{si}")
        dma(P, "gpsimd", sc_x_tile(sc["gate_s"], j), gout, tag=f"p1g.st{j}")


def emit_p1b_up(ctx, wuT):
    P, sc, sb, pb = ctx["P"], ctx["sc"], ctx["sb"], ctx["pb"]
    f = Alloc(sb["fpool"])
    h = [f.take(I_FFN) for _ in range(2)]
    HB = I_FFN // 2
    qtmpd = [dict(w=None, A=f.take(HB), B=f.take(HB), C=f.take(HB),
                  D=f.take(HB), bm=f.take(HB // 16), rs=f.take(HB // 16))
             for _ in range(2)]
    bb = Alloc(sb["bpool"])
    xqT = [bb.take(D) for _ in range(2)]
    hqd = [bb.take(I_FFN) for _ in range(2)]
    nsub = (I_FFN + 511) // 512

    def stage_mm(j):
        bj = j % 2
        xT = xqT[bj]
        ht = h[bj]
        dmat_blocks(P, "sync", xT, sc["xq"][128 * j:128 * (j + 1), 0:D],
                    tag=f"p1u.xT{j}")
        dma(P, "sync", ht, sc_x_tile(sc["gate_s"], j), tag=f"p1u.g{j}")
        P.hint("tensor", [xT], tag=f"p1u.h{j}")
        for si in range(nsub):
            i0 = 512 * si
            nw = min(512, I_FFN - i0)
            pv = pb[si % 4][0:128, 0:nw]
            for ec in range(EC):
                mm(P, pv, xT.sub(0, 128, 128 * ec, 128 * (ec + 1)),
                   wuT.sub(0, 128, I_FFN * ec + i0, I_FFN * ec + i0 + nw),
                   start=(ec == 0), stop=(ec == EC - 1), tag=f"p1u.mm{j}.{si}.{ec}")
            tt(P, ht.sub(0, 128, i0, i0 + nw), ht.sub(0, 128, i0, i0 + nw),
               pv, ALU.mult, tag=f"p1u.h{j}.{si}")

    def stage_quant(j):
        bj = j % 2
        ht = h[bj]
        hqb = hqd[bj]
        jobs = []
        for hf in range(2):
            t = dict(qtmpd[hf])
            t["w"] = ht.sub(0, 128, HB * hf, HB * (hf + 1))
            jobs.append((hqb.sub(0, 128, HB * hf, HB * (hf + 1)),
                         ht.sub(0, 128, HB * hf, HB * (hf + 1)), t))
        fp4_quant_multi(P, jobs, tag=f"p1u.q{j}", act_rounds=True)
        dma(P, "gpsimd", sc_x_tile(sc["hq"], j), hqb, tag=f"p1u.st{j}")

    for j in range(NT + 1):
        if j < NT:
            stage_mm(j)
        if j >= 1:
            stage_quant(j - 1)

def emit_p1b_down(ctx, wdT):
    P, io, sc, sb, pb = ctx["P"], ctx["io"], ctx["sc"], ctx["sb"], ctx["pb"]
    f = Alloc(sb["fpool"])
    gd = [f.take(D) for _ in range(2)]
    xtile = [f.take(D) for _ in range(2)]
    stile = [f.take(D) for _ in range(2)]
    bb = Alloc(sb["bpool"])
    hqT = [bb.take(IC * 128) for _ in range(2)]
    for j in range(NT):
        bj = j % 2
        hT = hqT[bj]
        g = gd[bj]
        dmat_blocks(P, "sync", hT, sc["hq"][128 * j:128 * (j + 1), 0:I_FFN],
                    tag=f"p1d.hT{j}")
        dma(P, "sync", xtile[bj], sc_x_tile(io["x_in"], j), tag=f"p1d.x{j}")
        dma(P, "sync", stile[bj], sc_x_tile(sc["s_tm"], j), tag=f"p1d.s{j}")
        P.hint("tensor", [hT], tag=f"p1d.h{j}")
        for half in range(2):
            pv = pb[2 * (j % 2) + half][0:128, 0:512]
            for ic in range(IC):
                mm(P, pv, hT.sub(0, 128, 128 * ic, 128 * (ic + 1)),
                   wdT.sub(0, 128, D * ic + 512 * half, D * ic + 512 * (half + 1)),
                   start=(ic == 0), stop=(ic == IC - 1), tag=f"p1d.mm{j}.{half}.{ic}")
            gv = g.sub(0, 128, 512 * half, 512 * (half + 1))
            act(P, gv, pv, AF.Silu, tag=f"p1d.si{j}.{half}")
            tt(P, gv, gv, stile[bj].sub(0, 128, 512 * half, 512 * (half + 1)),
               ALU.mult, tag=f"p1d.sg{j}.{half}")
            tt(P, gv, gv, xtile[bj].sub(0, 128, 512 * half, 512 * (half + 1)),
               ALU.add, tag=f"p1d.x1{j}.{half}")
        dma(P, "gpsimd", sc_x_tile(sc["x1"], j), g, tag=f"p1d.st{j}")


def emit_p2a(ctx, WqTs, WkTs, t0=0, t1=NT, drain=0):
    P, sc, sb, pb = ctx["P"], ctx["sc"], ctx["sb"], ctx["pb"]
    f = Alloc(sb["fpool"])
    x1t2 = [f.take(2 * D) for _ in range(2)]
    sqb = f.take(2 * D)
    smalls = f.off
    bb = Alloc(sb["bpool"])
    xnb2 = bb.take(2 * D)
    xT = [bb.take(D) for _ in range(2)]
    qkd = [bb.take(D) for _ in range(2)]
    lag = 4
    for i in range(t0, t1 + lag):
        if drain:
            drain_side(ctx, drain)
        if i < t1 and i % 2 == 0:
            i2 = i // 2
            b = i2 % 2
            x2 = x1t2[b]
            dma(P, "sync", x2, sc["x1"][256 * i2:256 * (i2 + 1), 0:D],
                tag=f"p2.x{i2}",
                src_fn=lambda ap: ap.rearrange("(a p) c -> p a c", p=128))
            fs = Alloc(sb["fpool"]); fs.off = smalls
            ln_tile2(P, x2, sqb, fs, tag=f"p2.ln{i2}")
            copy(P, xnb2, x2, eng="scalar", tag=f"p2.cb{i2}")
            dma(P, "gpsimd", sc["xn1_bf"][256 * i2:256 * (i2 + 1), 0:D], xnb2,
                tag=f"p2.sxb{i2}",
                dst_fn=lambda ap: ap.rearrange("(a p) c -> p a c", p=128))
        j = i - lag
        if j < t0:
            continue
        bj = j % 2
        xTj = xT[bj]
        qkb = qkd[bj]
        dmat_blocks(P, "sync", xTj, sc["xn1_bf"][128 * j:128 * (j + 1), 0:D],
                    tag=f"p2.xT{j}")
        bloc = j // NT_B
        tl = j % NT_B
        P.hint("tensor", [xTj], tag=f"p2a.h{j}")
        for (wT, dst, bcol, nm) in ((WqTs, sc["qf"], 0, "q"), (WkTs, sc["kf"], 1, "k")):
            for oc in range(EC):
                pv = pb[oc % 2][0:128, 0:128]
                for ec in range(EC):
                    mm(P, pv, wT.sub(0, 128, D * ec + 128 * oc, D * ec + 128 * (oc + 1)),
                       xTj.sub(0, 128, 128 * ec, 128 * (ec + 1)),
                       start=(ec == 0), stop=(ec == EC - 1),
                       tag=f"p2.{nm}{j}.{oc}.{ec}")
                ov = qkb.sub(0, 128, 128 * oc, 128 * (oc + 1))
                ts(P, ov, pv, sb["bias_pp"][0:128, 8 * bcol + oc:8 * bcol + oc + 1],
                   None, ALU.add, tag=f"p2.{nm}b{j}.{oc}")
            dma(P, "gpsimd",
                dst[D * bloc:D * bloc + D, 128 * tl:128 * (tl + 1)],
                qkb, tag=f"p2.{nm}st{j}",
                dst_fn=lambda ap: ap.rearrange("(o p) c -> p o c", p=128),
                src_fn=lambda ap: ap.rearrange("p (o c) -> p o c", c=128))


def emit_p2b(ctx, WvTs, WfTs, t0=0, t1=NT, drain=0):
    P, sc, sb, pb = ctx["P"], ctx["sc"], ctx["sb"], ctx["pb"]
    f = Alloc(sb["fpool"])
    g1d = [f.take(D) for _ in range(2)]
    bb = Alloc(sb["bpool"])
    xT = [bb.take(D) for _ in range(2)]
    vbd = [bb.take(D) for _ in range(2)]
    for j in range(t0, t1):
        if drain:
            drain_side(ctx, drain)
        bj = j % 2
        xTj = xT[bj]
        vb = vbd[bj]
        g1t = g1d[bj]
        dmat_blocks(P, "sync", xTj, sc["xn1_bf"][128 * j:128 * (j + 1), 0:D],
                    tag=f"p2b.xT{j}")
        bloc = j // NT_B
        tl = j % NT_B
        P.hint("tensor", [xTj], tag=f"p2b.h{j}")
        for half in range(2):
            pv = pb[half][0:128, 0:512]
            for ec in range(EC):
                mm(P, pv, xTj.sub(0, 128, 128 * ec, 128 * (ec + 1)),
                   WvTs.sub(0, 128, D * ec + 512 * half, D * ec + 512 * (half + 1)),
                   start=(ec == 0), stop=(ec == EC - 1), tag=f"p2b.v{j}.{half}.{ec}")
            tt(P, vb.sub(0, 128, 512 * half, 512 * (half + 1)), pv,
               sb["bv_bc"][0:128, 512 * half:512 * (half + 1)], ALU.add,
               tag=f"p2b.vb{j}.{half}")
        tw = [sc["vsc"][(bloc * NH + n) * S + 128 * tl:
                        (bloc * NH + n) * S + 128 * tl + 128, 0:HD]
              for n in range(NH)]
        dma(P, "gpsimd", sc["vsc"].full, vb, tag=f"p2b.vst{j}",
            dst_fn=lambda ap, _b=bloc, _t=tl: ap.rearrange(
                "(b n q p) c -> b q p n c", b=B_PER_CORE, n=NH, p=128)[_b, _t],
            src_fn=lambda ap: ap.rearrange("p (n c) -> p n c", c=HD),
            track_writes=tw)
        for half in range(2):
            pv = pb[half][0:128, 0:512]
            for ec in range(EC):
                mm(P, pv, xTj.sub(0, 128, 128 * ec, 128 * (ec + 1)),
                   WfTs.sub(0, 128, D * ec + 512 * half, D * ec + 512 * (half + 1)),
                   start=(ec == 0), stop=(ec == EC - 1), tag=f"p2b.f{j}.{half}.{ec}")
            act(P, g1t.sub(0, 128, 512 * half, 512 * (half + 1)), pv, AF.Silu,
                tag=f"p2b.g1{j}.{half}")
        dma(P, "gpsimd", sc_x_tile(sc["g1"], j), g1t, tag=f"p2b.g1st{j}")


def emit_p3(ctx, WoTs):
    P, io, sc, sb, pb = ctx["P"], ctx["io"], ctx["sc"], ctx["sb"], ctx["pb"]
    a = Alloc(sb["apool"])
    qtd = [a.take(S, rows=65) for _ in range(2)]
    ktd = [a.take(S, rows=65) for _ in range(2)]
    vtd = [a.take(NH * 65) for _ in range(2)]
    PTd = [a.take(S) for _ in range(2)]
    opair = a.take(S)
    obt = [a.take(1024) for _ in range(2)]
    f = Alloc(sb["fpool"])
    q2 = f.take(S, rows=64)
    ssq = f.take(S, rows=1)
    aq = f.take(S, rows=1)
    kmax = f.take(1, rows=1)
    rl = f.take(S, rows=1)
    lbc = f.take(S, rows=64)
    s1d = [f.take(D) for _ in range(2)]
    x1d = [f.take(D) for _ in range(2)]
    g1d = [f.take(D) for _ in range(2)]
    for z in range(2):
        memset(P, View(vtd[z].buf, 0, 128, vtd[z].c0, vtd[z].c1), 1.0, tag=f"p3.vones{z}")
        memset(P, View(ktd[z].buf, 64, 65, ktd[z].c0, ktd[z].c1), 1.0, tag=f"p3.kones{z}")

    nsub = S // 512

    def stage_load_norms(bloc, n):
        qt, kt, vt = qtd[n % 2], ktd[n % 2], vtd[n % 2]
        qrow = D * bloc + HD * (n % 2) + 128 * (n // 2)
        dma(P, "sync", View(qt.buf, 0, 64, qt.c0, qt.c1),
            sc["qf"][qrow:qrow + 64, 0:S], tag=f"p3.q{bloc}.{n}")
        dma(P, "sync", View(kt.buf, 0, 64, kt.c0, kt.c1),
            sc["kf"][qrow:qrow + 64, 0:S], tag=f"p3.k{bloc}.{n}")
        vrow = (bloc * NH + n) * S
        dma(P, "sync", View(vt.buf, 0, 128, vt.c0, vt.c1),
            sc["vsc"][vrow:vrow + S, 0:HD], tag=f"p3.v{bloc}.{n}",
            dst_fn=lambda ap: ap.rearrange("p (j c) -> p j c", c=65)[:, :, 0:HD],
            src_fn=lambda ap: ap.rearrange("(j p) c -> p j c", p=128))
        qv = View(qt.buf, 0, 64, qt.c0, qt.c1)
        kv = View(kt.buf, 0, 64, kt.c0, kt.c1)
        tt(P, q2, qv, qv, ALU.mult, tag=f"p3.q2{bloc}.{n}")
        spots = [(2, 0), (2, 32), (2, 64), (3, 0)]
        for si in range(nsub):
            bk, p0 = spots[si]
            mm(P, pb[bk][p0:p0 + 1, 0:512], sb["ones_col"][0:64, 0:1],
               q2.sub(0, 64, 512 * si, 512 * (si + 1)), True, True,
               tag=f"p3.ssq{bloc}.{n}.{si}")
        for si in range(nsub):
            bk, p0 = spots[si]
            copy(P, aq.sub(0, 1, 512 * si, 512 * (si + 1)),
                 pb[bk][p0:p0 + 1, 0:512], eng="vector",
                 tag=f"p3.ssqe{bloc}.{n}.{si}")
        tt(P, q2, kv, kv, ALU.mult, tag=f"p3.k2{bloc}.{n}")
        for si in range(nsub):
            bk, p0 = spots[si]
            mm(P, pb[bk][p0:p0 + 1, 0:512], sb["ones_col"][0:64, 0:1],
               q2.sub(0, 64, 512 * si, 512 * (si + 1)), True, True,
               tag=f"p3.ssk{bloc}.{n}.{si}")
        for si in range(nsub):
            bk, p0 = spots[si]
            copy(P, ssq.sub(0, 1, 512 * si, 512 * (si + 1)),
                 pb[bk][p0:p0 + 1, 0:512], eng="vector",
                 tag=f"p3.sske{bloc}.{n}.{si}")
        red(P, kmax, ssq, ALU.max, absval=True, tag=f"p3.km{bloc}.{n}")
        ts(P, aq, aq, kmax, None, ALU.mult, tag=f"p3.mh1{bloc}.{n}")
        act(P, aq, aq, AF.Sqrt, tag=f"p3.mh2{bloc}.{n}")
        ts(P, aq, aq, -1.0, None, ALU.mult, tag=f"p3.mh3{bloc}.{n}")
        copy(P, View(qt.buf, 64, 65, qt.c0, qt.c1), aq, eng="vector",
             tag=f"p3.mhc{bloc}.{n}")

    def stage_jloop(bloc, n):
        qt, kt, vt = qtd[n % 2], ktd[n % 2], vtd[n % 2]
        segs = []
        for jb in range(NT_B):
            i0 = 128 * jb
            for q4 in range(4):
                ci = max(i0, 512 * q4)
                ce = 512 * (q4 + 1)
                if ci < ce:
                    segs.append((jb, q4, ci, ce - ci))
        nseg = len(segs)
        LAGS = 2
        for idx in range(nseg + LAGS):
            if LAGS <= idx < nseg + LAGS:
                jb, q4, ci, nw = segs[idx - LAGS]
                PT = PTd[(idx - LAGS) % 2]
                opv = pb[4 + q4][0:65, (ci % 512):(ci % 512) + nw]
                mm(P, opv, View(vt.buf, 0, 128, vt.c0 + 65 * jb, vt.c0 + 65 * (jb + 1)),
                   PT.sub(0, 128, ci, ci + nw),
                   start=(jb == 0), stop=(jb == min(4 * q4 + 3, NT_B - 1)),
                   tag=f"p3.O{bloc}.{n}.{jb}.{ci}")
            if idx < nseg:
                jb, q4, ci, nw = segs[idx]
                i0 = 128 * jb
                PT = PTd[idx % 2]
                spv = pb[2 + idx % 2][0:128, 0:nw]
                mm(P, spv, View(kt.buf, 0, 65, kt.c0 + i0, kt.c0 + i0 + 128),
                   View(qt.buf, 0, 65, qt.c0 + ci, qt.c0 + ci + nw),
                   start=True, stop=True, tag=f"p3.S{bloc}.{n}.{jb}.{ci}")
                act(P, PT.sub(0, 128, ci, ci + nw), spv, AF.Exp,
                    tag=f"p3.E{bloc}.{n}.{jb}.{ci}")
                if ci == i0:
                    tt(P, PT.sub(0, 128, i0, i0 + 128),
                       PT.sub(0, 128, i0, i0 + 128), sb["tri01"].full,
                       ALU.mult, eng="vector", tag=f"p3.D{bloc}.{n}.{jb}")

    def stage_norm_evac(bloc, n):
        for q4 in range(4):
            copy(P, rl.sub(0, 1, 512 * q4, 512 * (q4 + 1)),
                 pb[4 + q4][64:65, 0:512], eng="vector", tag=f"p3.rlc{bloc}.{n}.{q4}")
        act(P, rl, rl, AF.Ln, tag=f"p3.rln{bloc}.{n}")
        act(P, rl, rl, AF.Exp, scale=-1.0, tag=f"p3.rlx{bloc}.{n}")
        lspots = [(2, 0), (2, 64), (3, 0), (3, 64)]
        for q4 in range(4):
            bk, p0 = lspots[q4]
            mm(P, pb[bk][p0:p0 + 64, 0:512], sb["ones_row"][0:1, 0:64],
               rl.sub(0, 1, 512 * q4, 512 * (q4 + 1)), True, True,
               tag=f"p3.lb{bloc}.{n}.{q4}")
        for q4 in range(4):
            bk, p0 = lspots[q4]
            copy(P, lbc.sub(0, 64, 512 * q4, 512 * (q4 + 1)),
                 pb[bk][p0:p0 + 64, 0:512], eng="vector",
                 tag=f"p3.lbe{bloc}.{n}.{q4}")
        hh = n % 2
        for q4 in range(4):
            tt(P, opair.sub(64 * hh, 64 * hh + 64, 512 * q4, 512 * (q4 + 1)),
               pb[4 + q4][0:64, 0:512],
               lbc.sub(0, 64, 512 * q4, 512 * (q4 + 1)),
               ALU.mult, tag=f"p3.on{bloc}.{n}.{q4}")
        if hh == 1:
            oc = n // 2
            r0 = D * bloc + 128 * oc
            dma(P, "gpsimd", sc["obsc"][r0:r0 + 128, 0:S],
                View(opair.buf, 0, 128, opair.c0, opair.c1),
                tag=f"p3.ost{bloc}.{oc}")

    def head_iter(bloc, n):
        if n < NH:
            stage_load_norms(bloc, n)
        if n >= 1:
            stage_jloop(bloc, n - 1)
            stage_norm_evac(bloc, n - 1)

    def queue_heads(bloc):
        for n in range(NH + 1):
            ctx["side"].append(lambda _b=bloc, _n=n: head_iter(_b, _n))

    def run_heads(bloc, drain=0):
        for n in range(NH + 1):
            head_iter(bloc, n)
            if drain:
                drain_side(ctx, drain)

    def outproj(bloc):
        for tl in range(NT_B):
            outproj_tile(bloc, tl)

    def outproj_tile(bloc, tl):
            jt = bloc * NT_B + tl
            bj = tl % 2
            ot = obt[bj]
            s1 = s1d[bj]
            x1t = x1d[bj]
            g1t = g1d[bj]
            dma(P, "sync", ot,
                sc["obsc"][D * bloc:D * bloc + D, 128 * tl:128 * (tl + 1)],
                tag=f"p3.ol{jt}",
                dst_fn=lambda ap: ap.rearrange("p (o c) -> p o c", c=128),
                src_fn=lambda ap: ap.rearrange("(o p) c -> p o c", p=128))
            dma(P, "sync", x1t, sc_x_tile(sc["x1"], jt), tag=f"p3.x1{jt}")
            dma(P, "sync", g1t, sc_x_tile(sc["g1"], jt), tag=f"p3.g1{jt}")
            for half in range(2):
                pv = pb[half][0:128, 0:512]
                for oc in range(EC):
                    mm(P, pv, ot.sub(0, 128, 128 * oc, 128 * (oc + 1)),
                       WoTs.sub(0, 128, D * oc + 512 * half, D * oc + 512 * (half + 1)),
                       start=(oc == 0), stop=(oc == EC - 1),
                       tag=f"p3.pr{jt}.{half}.{oc}")
                sv = s1.sub(0, 128, 512 * half, 512 * (half + 1))
                tt(P, sv, pv, sb["bo_bc"][0:128, 512 * half:512 * (half + 1)],
                   ALU.add, tag=f"p3.bo{jt}.{half}")
                tt(P, sv, sv, g1t.sub(0, 128, 512 * half, 512 * (half + 1)),
                   ALU.mult, tag=f"p3.sg{jt}.{half}")
                tt(P, sv, sv, x1t.sub(0, 128, 512 * half, 512 * (half + 1)),
                   ALU.add, tag=f"p3.x2{jt}.{half}")
            dma(P, "gpsimd", sc_x_tile(io["out"], jt), s1, tag=f"p3.out{jt}")

    def queue_outproj(bloc):
        for tl in range(NT_B):
            ctx["side"].append(lambda _b=bloc, _t=tl: outproj_tile(_b, _t))

    ctx["p3"] = dict(queue_heads=queue_heads, run_heads=run_heads,
                     outproj=outproj, queue_outproj=queue_outproj)



def build_full(debug_taps=(), upto=99):
    ctx = build_prog(debug_taps)
    P, sb = ctx["P"], ctx["sb"]
    wp = Alloc(sb["wpool"])
    wA = wp.take(EC * D)
    wB = wp.take(EC * D)
    wBig = View(sb["wpool"], 0, 128, 0, EC * I_FFN)

    setup_misc(ctx)
    if upto >= 0:
        prep_mp_weights(ctx, ("w_q", "w_k"))
        load_wT(ctx, wA, ctx["sc"]["wq_n"].full, "wqT")
        load_wT(ctx, wB, ctx["sc"]["wk_n"].full, "wkT")
        emit_p0(ctx, wA, wB)
    SIDE_POOLS = (6144, 6144)
    if upto >= 1:
        prep_mp_weights(ctx, ("w_v", "w_o"))
        load_wT(ctx, wA, ctx["sc"]["wv_n"].full, "wvT")
        load_wT(ctx, wB, ctx["sc"]["wo_n"].full, "woT")
        if upto >= 2:
            prep_fp4_weight(ctx, "w_gate", "wg_n", I_FFN, SIDE_POOLS)
        emit_p1a(ctx, wA, wB)
        drain_side(ctx, 99)
    if upto >= 2:
        load_wT(ctx, wBig, ctx["sc"]["wg_n"].full, "wgT")
        if upto >= 3:
            prep_fp4_weight(ctx, "w_up", "wu_n", I_FFN, SIDE_POOLS)
        if upto >= 4:
            prep_fp4_weight(ctx, "w_down", "wd_n", D, SIDE_POOLS)
        emit_p1b_gate(ctx, wBig)
        drain_side(ctx, 99)
    if upto >= 3:
        load_wT(ctx, wBig, ctx["sc"]["wu_n"].full, "wuT")
        emit_p1b_up(ctx, wBig)
    if upto >= 4:
        load_wT(ctx, wBig, ctx["sc"]["wd_n"].full, "wdT")
        emit_p1b_down(ctx, wBig)
    if upto >= 5 and upto < 6:
        load_l1_weight(ctx, wA, "WqT", scale=0.125, tag="l1q")
        load_l1_weight(ctx, wB, "WkT", tag="l1k")
        emit_p2a(ctx, wA, wB)
        load_l1_weight(ctx, wC, "WvT", tag="l1v")
        load_l1_weight(ctx, wD, "WfT", sign=True, tag="l1f")
        emit_p2b(ctx, wC, wD)
    if upto >= 6:
        load_l1_weight(ctx, wA, "WqT", scale=0.125, tag="l1q")
        load_l1_weight(ctx, wB, "WkT", tag="l1k")
        emit_p2a(ctx, wA, wB, 0, NT // 2)
        load_l1_weight(ctx, wA, "WvT", tag="l1v0")
        load_l1_weight(ctx, wB, "WfT", sign=True, tag="l1f0")
        emit_p2b(ctx, wA, wB, 0, NT // 2)
        wO = View(ctx["sb"]["bpool"], 0, 128, 6144, 6144 + EC * D)
        load_l1_weight(ctx, wO, "WoT", tag="l1o")
        emit_p3(ctx, wO)
        ctx["p3"]["queue_heads"](0)
        load_l1_weight(ctx, wA, "WqT", scale=0.125, tag="l1q1")
        load_l1_weight(ctx, wB, "WkT", tag="l1k1")
        emit_p2a(ctx, wA, wB, NT // 2, NT, drain=1)
        load_l1_weight(ctx, wA, "WvT", tag="l1v1")
        load_l1_weight(ctx, wB, "WfT", sign=True, tag="l1f1")
        emit_p2b(ctx, wA, wB, NT // 2, NT, drain=1)
        drain_side(ctx, 99)
        ctx["p3"]["queue_outproj"](0)
        ctx["p3"]["run_heads"](1, drain=1)
        drain_side(ctx, 99)
        ctx["p3"]["outproj"](1)
    return ctx



def make_inputs_per_core(inputs, core):
    f32 = np.float32
    x = np.ascontiguousarray(
        inputs["L0_attn_query"][2 * core:2 * core + 2].reshape(T, D), dtype=f32)
    d = {"x_in": x}
    d["gain"] = np.asarray(inputs["L0_attn_gain"], f32).reshape(1, 1)
    d["w_q"] = np.ascontiguousarray(inputs["L0_attn_w_q"], f32)
    d["w_k"] = np.ascontiguousarray(inputs["L0_attn_w_k"], f32)
    d["w_v"] = np.ascontiguousarray(inputs["L0_attn_w_v"], f32)
    d["w_o"] = np.ascontiguousarray(inputs["L0_attn_w_o"], f32)
    d["w_gate"] = np.ascontiguousarray(inputs["L0_ffn_w_gate"], f32)
    d["w_up"] = np.ascontiguousarray(inputs["L0_ffn_w_up"], f32)
    d["w_down"] = np.ascontiguousarray(inputs["L0_ffn_w_down"], f32)
    d["WqT"] = np.ascontiguousarray(np.asarray(inputs["L1_attn_Wq"], f32).T)
    d["WkT"] = np.ascontiguousarray(np.asarray(inputs["L1_attn_Wk"], f32).T)
    d["WvT"] = np.ascontiguousarray(np.asarray(inputs["L1_attn_Wv"], f32).T)
    d["WoT"] = np.ascontiguousarray(np.asarray(inputs["L1_attn_Wo"], f32).T)
    d["WfT"] = np.ascontiguousarray(np.asarray(inputs["L1_ffn_weight"], f32).T)
    bias = np.concatenate([np.asarray(inputs["L1_attn_b" + k], f32).reshape(8, 128)
                           for k in ("q", "k", "v", "o")], axis=1)
    d["biases"] = np.ascontiguousarray(bias)
    d["bo_row"] = np.asarray(inputs["L1_attn_bo"], f32).reshape(1, D)
    d["bv_row"] = np.asarray(inputs["L1_attn_bv"], f32).reshape(1, D)
    d["identf"] = np.eye(128, dtype=f32)
    tri = (np.arange(128)[None, :] >= np.arange(128)[:, None])
    d["tri01"] = tri.astype(ml_dtypes.bfloat16)
    d["consts"] = np.tile(np.array(CONST_VALS + [0.0] * (8 - len(CONST_VALS)),
                                   dtype=f32)[None, :], (128, 1))
    return d


def kernel(**inputs):
    ctx = build_full()
    nc, stack = ctx["P"].build()
    try:
        in_maps = [make_inputs_per_core(inputs, c) for c in range(8)]
        res = run_bass_kernel_spmd(nc, in_maps, core_ids=list(range(8)))
    finally:
        stack.close()
    out = np.concatenate(
        [res.results[c]["out"].reshape(B_PER_CORE, S, D) for c in range(8)], axis=0)
    return out.astype(np.float32)


# version 22
# speedup vs baseline: 1.0153x; 1.0039x over previous
import contextlib
import math

import numpy as np
import ml_dtypes

import concourse.bass as bass
import concourse.mybir as mybir
from concourse.bass_utils import run_bass_kernel_spmd

F32 = mybir.dt.float32
BF16 = mybir.dt.bfloat16
AF = mybir.ActivationFunctionType
ALU = mybir.AluOpType
AX = mybir.AxisListType

MAGIC = 12582912.0
RS2 = 0.7071067811865476
EPS_MP = 1e-4
SQS = 45.254833995939045

B_PER_CORE = 2
S = 2048
T = B_PER_CORE * S
NT = T // 128
NT_B = S // 128
D = 1024
EC = D // 128
I_FFN = 2816
IC = I_FFN // 128
NH = 16
HD = 64

ENGINES = ("sync", "scalar", "vector", "tensor", "gpsimd")
DMA_QUEUES = ("sync", "scalar", "gpsimd")
CONST_VALS = [2.0, 0.5, -0.5, -2.0, -4.0, MAGIC, -MAGIC]



class Buf:
    __slots__ = ("space", "name", "rows", "cols", "dtype", "kind", "h",
                 "wrecs", "rrecs")

    def __init__(self, space, name, rows, cols, dtype, kind=None):
        self.space, self.name = space, name
        self.rows, self.cols, self.dtype, self.kind = rows, cols, dtype, kind
        self.h = None
        self.wrecs = []
        self.rrecs = []

    def __getitem__(self, idx):
        rs, cs = idx
        if isinstance(rs, slice):
            r0, r1, _ = rs.indices(self.rows)
        else:
            r0, r1 = rs, rs + 1
        if isinstance(cs, slice):
            c0, c1, _ = cs.indices(self.cols)
        else:
            c0, c1 = cs, cs + 1
        return View(self, r0, r1, c0, c1)

    @property
    def full(self):
        return View(self, 0, self.rows, 0, self.cols)


class View:
    __slots__ = ("buf", "r0", "r1", "c0", "c1")

    def __init__(self, buf, r0, r1, c0, c1):
        self.buf, self.r0, self.r1, self.c0, self.c1 = buf, r0, r1, c0, c1

    @property
    def ap(self):
        return self.buf.h[self.r0:self.r1, self.c0:self.c1]

    def sub(self, r0, r1, c0, c1):
        return View(self.buf, self.r0 + r0, self.r0 + r1, self.c0 + c0, self.c0 + c1)

    def rect(self):
        if self.buf.space == "psum":
            return (0, self.buf.rows, 0, self.buf.cols)
        return (self.r0, self.r1, self.c0, self.c1)

    @property
    def shape(self):
        return (self.r1 - self.r0, self.c1 - self.c0)


def _ovl(a, b):
    return a[0] < b[1] and b[0] < a[1] and a[2] < b[3] and b[2] < a[3]


class Op:
    __slots__ = ("idx", "engine", "fn", "is_dma", "tag", "inc_needed",
                 "inc_count", "wait_edges", "narrow", "minspan", "has_psum")

    def __init__(self, idx, engine, fn, is_dma, tag):
        self.idx, self.engine, self.fn, self.is_dma, self.tag = \
            idx, engine, fn, is_dma, tag
        self.inc_needed = is_dma
        self.inc_count = None
        self.wait_edges = []


class Prog:
    def __init__(self):
        self.ops = []
        self.bufs = []
        self.n_preload = 0

    def sbuf(self, name, rows, cols, dtype):
        b = Buf("sbuf", name, rows, cols, dtype)
        self.bufs.append(b)
        return b

    def psum(self, name, rows, cols):
        b = Buf("psum", name, rows, cols, F32)
        self.bufs.append(b)
        return b

    def dram(self, name, rows, cols, dtype, kind="Internal"):
        b = Buf("dram", name, rows, cols, dtype, kind)
        self.bufs.append(b)
        return b

    def hint(self, engine, views, tag="hint"):
        o = Op(len(self.ops), engine, None, False, tag)
        o.has_psum = False
        o.minspan = 0
        o.narrow = False
        edges = set()
        for v in views:
            r = v.rect()
            for rec in v.buf.wrecs:
                if _ovl(r, rec):
                    edges.add(rec[4])
        for pidx in edges:
            p = self.ops[pidx]
            if p.engine == engine and not p.is_dma:
                continue
            o.wait_edges.append(p)
            p.inc_needed = True
        self.ops.append(o)
        return o

    def op(self, engine, fn, reads=(), writes=(), tag="", is_dma=False):
        o = Op(len(self.ops), engine, fn, is_dma, tag)
        allv = list(reads) + list(writes)
        o.has_psum = any(v.buf.space == "psum" for v in allv)
        o.minspan = min((v.c1 - v.c0 for v in allv), default=0)
        if is_dma:
            spans = [v.r1 - v.r0 for v in allv if v.buf.space == "sbuf"]
            o.narrow = bool(spans) and min(spans) < 128
        else:
            o.narrow = False
        edges = set()
        for v in reads:
            r = v.rect()
            for rec in v.buf.wrecs:
                if _ovl(r, rec):
                    edges.add(rec[4])
        for v in writes:
            r = v.rect()
            for rec in v.buf.wrecs:
                if _ovl(r, rec):
                    edges.add(rec[4])
            for rec in v.buf.rrecs:
                if _ovl(r, rec):
                    edges.add(rec[4])
        for pidx in edges:
            p = self.ops[pidx]
            if (p.engine == "tensor" and engine == "tensor"
                    and not p.is_dma and not is_dma):
                continue
            o.wait_edges.append(p)
            p.inc_needed = True
        for v in writes:
            r = v.rect()
            def cov(rec, _r=r):
                return (_r[0] <= rec[0] and rec[1] <= _r[1]
                        and _r[2] <= rec[2] and rec[3] <= _r[3])
            v.buf.wrecs = [rec for rec in v.buf.wrecs if not cov(rec)]
            v.buf.rrecs = [rec for rec in v.buf.rrecs if not cov(rec)]
            v.buf.wrecs.append((*r, o.idx))
        for v in reads:
            v.buf.rrecs.append((*v.rect(), o.idx))
        self.ops.append(o)
        return o

    def build(self):
        nc = bass.Bass("TRN2", target_bir_lowering=False)
        stack = contextlib.ExitStack()
        for b in self.bufs:
            if b.space == "dram":
                b.h = nc.dram_tensor(b.name, [b.rows, b.cols], b.dtype,
                                     kind=b.kind).ap()
        for b in self.bufs:
            if b.space == "sbuf":
                b.h = stack.enter_context(
                    nc.sbuf_tensor(b.name, [b.rows, b.cols], b.dtype)).ap()
            elif b.space == "psum":
                b.h = stack.enter_context(
                    nc.psum_tensor(b.name, [b.rows, b.cols], b.dtype)).ap()
        cb = next(b for b in self.bufs if b.name == "consts_sb")
        for i, val in enumerate(CONST_VALS):
            nc.const_aps.register(F32, val, cb.h[:, i:i + 1])

        RSEM_Q = {"sync": 40, "scalar": 8, "gpsimd": 32}
        sem_counts = {}
        queue_counts = {}
        for o in self.ops:
            if o.is_dma:
                i = queue_counts.get(o.engine, 0)
                queue_counts[o.engine] = i + 1
                o.inc_count = i
            elif o.inc_needed:
                key = "e_" + o.engine
                sem_counts[key] = sem_counts.get(key, 0) + 1
                o.inc_count = sem_counts[key]
        sems = {k: stack.enter_context(nc.semaphore(k)) for k in sem_counts}
        dpool = {}
        for q in queue_counts:
            dpool[q] = [stack.enter_context(nc.semaphore(f"d_{q}{r}"))
                        for r in range(RSEM_Q[q])]

        per_engine = {e: [] for e in ENGINES}
        for o in self.ops:
            per_engine[o.engine].append(o)
        n_pre = self.n_preload

        block = stack.enter_context(nc.Block())

        def dma_key_val(p, consumer=True):
            R = RSEM_Q[p.engine]
            slot = p.inc_count % R
            rnd = p.inc_count // R
            return ("dq", p.engine, slot), 16 * (rnd + (1 if consumer else 0))

        def mk(ename):
            def run(eng):
                wm = {}
                if n_pre and ename != "sync" and "sync" in dpool:
                    for i in range(n_pre):
                        eng.wait_ge(dpool["sync"][i % RSEM_Q["sync"]], 16)
                        wm[("dq", "sync", i % RSEM_Q["sync"])] = 16
                for o in per_engine[ename]:
                    for p in o.wait_edges:
                        if p.is_dma:
                            key, val = dma_key_val(p, consumer=True)
                            sem = dpool[p.engine][key[2]]
                        else:
                            key = "e_" + p.engine
                            val = p.inc_count
                            sem = sems[key]
                        if wm.get(key, 0) >= val:
                            continue
                        eng.wait_ge(sem, val)
                        wm[key] = val
                    if o.is_dma:
                        key, val = dma_key_val(o, consumer=False)
                        if val > 0 and wm.get(key, 0) < val:
                            eng.wait_ge(dpool[o.engine][key[2]], val)
                            wm[key] = val
                    if o.fn is None:
                        continue
                    instr = o.fn(eng)
                    if o.is_dma:
                        instr.then_inc(dpool[o.engine][o.inc_count % RSEM_Q[o.engine]], 16)
                    elif o.inc_needed:
                        instr.then_inc(sems["e_" + o.engine], 1)
                if ename == "gpsimd":
                    for q, cnt in queue_counts.items():
                        R = RSEM_Q[q]
                        for r in range(R):
                            n_r = (cnt - 1 - r) // R + 1 if r < cnt else 0
                            if n_r > 0:
                                eng.wait_ge(dpool[q][r], 16 * n_r)
            return run

        for ename in ENGINES:
            getattr(block, ename)(mk(ename))
        return nc, stack



def dma(P, queue, dst, src, tag="", dst_fn=None, src_fn=None,
        track_writes=None, track_reads=None):
    def fn(e, _d=dst, _s=src):
        dap = dst_fn(_d.ap) if dst_fn else _d.ap
        sap = src_fn(_s.ap) if src_fn else _s.ap
        return e.dma_start(out=dap, in_=sap)
    return P.op(queue, fn, track_reads or [src], track_writes or [dst], tag,
                is_dma=True)


def dmat(P, queue, dst, src, tag=""):
    def fn(e, _d=dst, _s=src):
        return e.dma_start_transpose(_d.ap, _s.ap)
    return P.op(queue, fn, [src], [dst], tag, is_dma=True)


def dmat_blocks(P, queue, dst, src, tag=""):
    R = src.r1 - src.r0
    def fn(e, _d=dst, _s=src, _R=R):
        dap = _d.ap.rearrange("p (k r) -> p k r", r=_R)
        return e.dma_start_transpose(dap, _s.ap)
    return P.op(queue, fn, [src], [dst], tag, is_dma=True)


def ts(P, dst, src, s1, s2=None, op0=ALU.mult, op1=None, accum=None, tag="ts"):
    reads = [src] + [x for x in (s1, s2) if isinstance(x, View)]
    writes = [dst] + ([accum] if accum is not None else [])

    def fn(e, _d=dst, _s=src, _1=s1, _2=s2, _a=accum, _o0=op0, _o1=op1):
        k1 = _1.ap if isinstance(_1, View) else _1
        k2 = _2.ap if isinstance(_2, View) else _2
        kw = {"accum_out": _a.ap} if _a is not None else {}
        if _o1 is None:
            return e.tensor_scalar(_d.ap, _s.ap, k1, None, _o0, **kw)
        return e.tensor_scalar(_d.ap, _s.ap, k1, k2, _o0, _o1, **kw)

    return P.op("vector", fn, reads, writes, tag)


def tt(P, dst, a, b, op=ALU.mult, eng="vector", tag="tt",
       dst_fn=None, a_fn=None, b_fn=None):
    def fn(e, _d=dst, _a=a, _b=b, _op=op):
        dap = dst_fn(_d.ap) if dst_fn else _d.ap
        aap = a_fn(_a.ap) if a_fn else _a.ap
        bap = b_fn(_b.ap) if b_fn else _b.ap
        return e.tensor_tensor(dap, aap, bap, _op)
    return P.op(eng, fn, [a, b], [dst], tag)


def act(P, dst, src, func, bias=None, scale=None, accum=None, tag="act"):
    reads = [src] + ([bias] if isinstance(bias, View) else [])
    writes = [dst] + ([accum] if accum is not None else [])

    def fn(e, _d=dst, _s=src, _b=bias, _sc=scale, _a=accum, _f=func):
        kw = {}
        if _b is not None:
            kw["bias"] = _b.ap if isinstance(_b, View) else _b
        if _sc is not None:
            kw["scale"] = _sc
        if _a is not None:
            kw["accum_out"] = _a.ap
        return e.activation(_d.ap, _s.ap, _f, **kw)

    return P.op("scalar", fn, reads, writes, tag)


def red(P, dst, src, op=ALU.max, absval=False, negate=False, src_fn=None,
        tag="red"):
    def fn(e, _d=dst, _s=src, _op=op):
        sap = src_fn(_s.ap) if src_fn else _s.ap
        kw = {}
        if absval:
            kw["apply_absolute_value"] = True
        if negate:
            kw["negate"] = True
        return e.tensor_reduce(_d.ap, sap, AX.X, _op, **kw)
    return P.op("vector", fn, [src], [dst], tag)


def copy(P, dst, src, eng="vector", tag="copy"):
    if eng == "scalar":
        def fn(e, _d=dst, _s=src):
            return e.activation(_d.ap, _s.ap, AF.Copy)
    else:
        def fn(e, _d=dst, _s=src):
            return e.tensor_copy(_d.ap, _s.ap)
    return P.op(eng, fn, [src], [dst], tag)


def recip(P, dst, src, tag="recip"):
    return P.op("vector", lambda e, _d=dst, _s=src: e.reciprocal(_d.ap, _s.ap),
                [src], [dst], tag)


def mm(P, out, lhsT, rhs, start, stop, tag="mm"):
    def fn(e, _o=out, _l=lhsT, _r=rhs, _st=start, _sp=stop):
        return e.matmul(_o.ap, _l.ap, _r.ap, start=_st, stop=_sp)
    return P.op("tensor", fn, [lhsT, rhs], [out], tag)


def petr(P, out, src, ident, tag="ptr"):
    def fn(e, _o=out, _s=src, _i=ident):
        return e.transpose(_o.ap, _s.ap, _i.ap)
    return P.op("tensor", fn, [src, ident], [out], tag)


def memset(P, dst, val, eng="vector", tag="memset"):
    return P.op(eng, lambda e, _d=dst, _v=val: e.memset(_d.ap, _v), [], [dst], tag)


def gbm(P, dst, src, small, gsize, op=ALU.mult, eng="vector", tag="gbm"):
    ng = small.shape[1]

    def fn(e, _d=dst, _s=src, _m=small, _op=op, _g=gsize, _ng=ng):
        dap = _d.ap.rearrange("p (g b) -> p g b", b=_g)
        sap = _s.ap.rearrange("p (g b) -> p g b", b=_g)
        map_ = _m.ap.unsqueeze(2).broadcast_to((_m.shape[0], _ng, _g))
        return e.tensor_tensor(dap, sap, map_, _op)

    return P.op(eng, fn, [src, small], [dst], tag)



def fp4_quant_multi(P, jobs, tag="q", act_rounds=False):
    def each(fn):
        for i, (ob, x, t) in enumerate(jobs):
            fn(i, ob, x, t)
    each(lambda i, ob, x, t: red(P, t["bm"], x, ALU.max, absval=True,
         src_fn=lambda ap: ap.rearrange("p (g b) -> p g b", b=16), tag=f"{tag}.bm{i}"))
    each(lambda i, ob, x, t: ts(P, t["bm"], t["bm"], 1.0 / 6.0, 1e-12,
         ALU.mult, ALU.max, tag=f"{tag}.s{i}"))
    each(lambda i, ob, x, t: recip(P, t["rs"], t["bm"], tag=f"{tag}.rs{i}"))
    each(lambda i, ob, x, t: gbm(P, t["w"], x, t["rs"], 16, tag=f"{tag}.w{i}"))
    each(lambda i, ob, x, t: act(P, t["A"], t["w"], AF.Abs, scale=2.0, tag=f"{tag}.a2{i}"))
    each(lambda i, ob, x, t: ts(P, t["A"], t["A"], MAGIC, -MAGIC, ALU.add, ALU.add, tag=f"{tag}.r1{i}"))
    each(lambda i, ob, x, t: ts(P, t["A"], t["A"], 0.5, 2.0, ALU.mult, ALU.min, tag=f"{tag}.t1{i}"))
    each(lambda i, ob, x, t: act(P, t["B"], t["w"], AF.Abs, tag=f"{tag}.a1{i}"))
    if act_rounds:
        each(lambda i, ob, x, t: act(P, t["B"], t["B"], AF.Identity, bias=MAGIC, tag=f"{tag}.r2a{i}"))
        each(lambda i, ob, x, t: act(P, t["B"], t["B"], AF.Identity, bias=-MAGIC, tag=f"{tag}.r2b{i}"))
    else:
        each(lambda i, ob, x, t: ts(P, t["B"], t["B"], MAGIC, -MAGIC, ALU.add, ALU.add, tag=f"{tag}.r2{i}"))
    each(lambda i, ob, x, t: ts(P, t["B"], t["B"], -2.0, 0.0, ALU.add, ALU.max, tag=f"{tag}.t2{i}"))
    each(lambda i, ob, x, t: ts(P, t["B"], t["B"], 2.0, None, ALU.min, tag=f"{tag}.t2b{i}"))
    each(lambda i, ob, x, t: act(P, t["C"], t["w"], AF.Abs, scale=0.5, tag=f"{tag}.a5{i}"))
    if act_rounds:
        each(lambda i, ob, x, t: act(P, t["C"], t["C"], AF.Identity, bias=MAGIC, tag=f"{tag}.r3a{i}"))
        each(lambda i, ob, x, t: act(P, t["C"], t["C"], AF.Identity, bias=-MAGIC, tag=f"{tag}.r3b{i}"))
    else:
        each(lambda i, ob, x, t: ts(P, t["C"], t["C"], MAGIC, -MAGIC, ALU.add, ALU.add, tag=f"{tag}.r3{i}"))
    each(lambda i, ob, x, t: ts(P, t["C"], t["C"], -2.0, 0.0, ALU.add, ALU.max, tag=f"{tag}.t4{i}"))
    each(lambda i, ob, x, t: ts(P, t["C"], t["C"], 2.0, None, ALU.mult, tag=f"{tag}.t4b{i}"))
    each(lambda i, ob, x, t: act(P, t["D"], t["w"], AF.Sign, tag=f"{tag}.sg{i}"))
    each(lambda i, ob, x, t: tt(P, t["A"], t["A"], t["B"], ALU.add, eng="gpsimd", tag=f"{tag}.u1{i}"))
    each(lambda i, ob, x, t: tt(P, t["A"], t["A"], t["C"], ALU.add, eng="gpsimd", tag=f"{tag}.u2{i}"))
    each(lambda i, ob, x, t: tt(P, t["A"], t["A"], t["D"], ALU.mult, eng="gpsimd", tag=f"{tag}.sx{i}"))
    each(lambda i, ob, x, t: gbm(P, ob, t["A"], t["bm"], 16, tag=f"{tag}.o{i}"))


def fp4_quant(P, out_bf, x, tmp, tag="q"):
    fp4_quant_multi(P, [(out_bf, x, tmp)], tag=tag)



def build_prog(debug_taps=()):
    P = Prog()

    def tap(nm):
        return "ExternalOutput" if nm in debug_taps else "Internal"

    io = {}
    io["x_in"] = P.dram("x_in", T, D, F32, kind="ExternalInput")
    io["gain"] = P.dram("gain", 1, 1, F32, kind="ExternalInput")
    for nm in ("w_q", "w_k", "w_v", "w_o"):
        io[nm] = P.dram(nm, D, D, F32, kind="ExternalInput")
    io["w_gate"] = P.dram("w_gate", I_FFN, D, F32, kind="ExternalInput")
    io["w_up"] = P.dram("w_up", I_FFN, D, F32, kind="ExternalInput")
    io["w_down"] = P.dram("w_down", D, I_FFN, F32, kind="ExternalInput")
    for nm in ("WqT", "WkT", "WvT", "WoT", "WfT"):
        io[nm] = P.dram(nm, D, D, F32, kind="ExternalInput")
    io["biases"] = P.dram("biases", 8, 512, F32, kind="ExternalInput")
    io["bo_row"] = P.dram("bo_row", 1, D, F32, kind="ExternalInput")
    io["bv_row"] = P.dram("bv_row", 1, D, F32, kind="ExternalInput")
    io["identf"] = P.dram("identf", 128, 128, F32, kind="ExternalInput")
    io["tri01"] = P.dram("tri01", 128, 128, BF16, kind="ExternalInput")
    io["consts"] = P.dram("consts", 128, 8, F32, kind="ExternalInput")
    io["out"] = P.dram("out", T, D, F32, kind="ExternalOutput")

    sc = {}
    sc["xn_tm"] = P.dram("xn_tm", T, D, F32, kind=tap("xn_tm"))
    sc["xn_bf"] = P.dram("xn_bf", T, D, BF16)
    sc["xq"] = P.dram("xq", T, D, BF16, kind=tap("xq"))
    sc["s_tm"] = P.dram("s_tm", T, D, F32, kind=tap("s_tm"))
    sc["x1"] = P.dram("x1", T, D, F32, kind=tap("x1"))
    sc["gate_s"] = P.dram("gate_s", T, I_FFN, F32, kind=tap("gate_s"))
    sc["attn_sc"] = P.dram("attn_sc", B_PER_CORE * NH * 64, 64, BF16)
    sc["hq"] = P.dram("hq", T, I_FFN, BF16, kind=tap("hq"))
    sc["xn1_bf"] = P.dram("xn1_bf", T, D, BF16, kind=tap("xn1_bf"))
    sc["qf"] = P.dram("qf", B_PER_CORE * D, S, BF16, kind=tap("qf"))
    sc["kf"] = P.dram("kf", B_PER_CORE * D, S, BF16, kind=tap("kf"))
    sc["vsc"] = P.dram("vsc", B_PER_CORE * NH * S, HD, BF16, kind=tap("vsc"))
    sc["g1"] = P.dram("g1", T, D, F32, kind=tap("g1"))
    sc["obsc"] = P.dram("obsc", B_PER_CORE * D, S, BF16, kind=tap("obsc"))
    for nm in ("wq_n", "wk_n", "wv_n", "wo_n"):
        sc[nm] = P.dram(nm, D, D, BF16)
    sc["wg_n"] = P.dram("wg_n", I_FFN, D, BF16)
    sc["wu_n"] = P.dram("wu_n", I_FFN, D, BF16)
    sc["wd_n"] = P.dram("wd_n", D, I_FFN, BF16)

    sb = {}
    sb["consts_sb"] = P.sbuf("consts_sb", 128, 8, F32)
    sb["identf"] = P.sbuf("identf_sb", 128, 128, F32)
    sb["tri01"] = P.sbuf("tri01_sb", 128, 128, BF16)
    sb["gain_bc"] = P.sbuf("gain_bc", 128, 1, F32)
    sb["ones_row"] = P.sbuf("ones_row", 1, 128, F32)
    sb["ones_col"] = P.sbuf("ones_col", 128, 1, F32)
    sb["bias_pp"] = P.sbuf("bias_pp", 128, 32, F32)
    sb["btmp"] = P.sbuf("btmp", 8, 512, F32)
    sb["grow"] = P.sbuf("grow", 1, D, F32)
    sb["bo_bc"] = P.sbuf("bo_bc", 128, D, F32)
    sb["bv_bc"] = P.sbuf("bv_bc", 128, D, F32)
    sb["attnT"] = P.sbuf("attnT", 128, B_PER_CORE * NH * HD, BF16)
    sb["wpool"] = P.sbuf("wpool", 128, 22528, BF16)
    sb["apool"] = P.sbuf("apool", 128, 18560, BF16)
    sb["fpool"] = P.sbuf("fpool", 128, 20480, F32)
    sb["bpool"] = P.sbuf("bpool", 128, 14336, BF16)

    pb = [P.psum(f"pb{i}", 128, 512) for i in range(8)]

    dma(P, "sync", sb["consts_sb"].full, io["consts"].full, tag="pre.c")
    dma(P, "sync", sb["identf"].full, io["identf"].full, tag="pre.i")
    dma(P, "sync", sb["tri01"].full, io["tri01"].full, tag="pre.t")
    P.n_preload = 3

    return dict(P=P, io=io, sc=sc, sb=sb, pb=pb, side=[])


class Alloc:
    def __init__(self, buf):
        self.buf = buf
        self.off = 0

    def take(self, cols, rows=None):
        r = self.buf.rows if rows is None else rows
        v = View(self.buf, 0, r, self.off, self.off + cols)
        self.off += cols
        assert self.off <= self.buf.cols, (self.buf.name, self.off)
        return v


def setup_misc(ctx):
    P, io, sb, pb = ctx["P"], ctx["io"], ctx["sb"], ctx["pb"]
    memset(P, sb["ones_row"].full, 1.0, tag="ones_r")
    memset(P, sb["ones_col"].full, 1.0, tag="ones_c")
    gtmp = sb["btmp"][0:1, 0:1]
    dma(P, "sync", gtmp, io["gain"].full, tag="gain.ld")
    mm(P, pb[0][0:128, 0:1], sb["ones_row"].full, gtmp, True, True, tag="gain.bc")
    copy(P, sb["gain_bc"].full, pb[0][0:128, 0:1], eng="scalar", tag="gain.ev")
    dma(P, "sync", sb["btmp"].full, io["biases"].full, tag="bias.ld")
    for i in range(4):
        pv = pb[1][0:128, 0:8]
        petr(P, pv, sb["btmp"][0:8, 128 * i:128 * (i + 1)], sb["identf"][0:8, 0:8],
             tag=f"bias.tr{i}")
        dstv = sb["bias_pp"][0:128, 8 * i:8 * i + 8]
        if i == 0:
            ts(P, dstv, pv, 0.125, None, ALU.mult, tag="bias.evq")
        else:
            copy(P, dstv, pv, eng="scalar", tag=f"bias.ev{i}")
    dma(P, "sync", sb["grow"].full, io["bo_row"].full, tag="bo.ld")
    for h in range(2):
        pv = pb[2][0:128, 0:512]
        mm(P, pv, sb["ones_row"].full, sb["grow"][0:1, 512 * h:512 * (h + 1)],
           True, True, tag="bo.bc")
        copy(P, sb["bo_bc"][0:128, 512 * h:512 * (h + 1)], pv, eng="scalar",
             tag="bo.ev")
    dma(P, "sync", sb["grow"].full, io["bv_row"].full, tag="bv.ld")
    for h in range(2):
        pv = pb[3][0:128, 0:512]
        mm(P, pv, sb["ones_row"].full, sb["grow"][0:1, 512 * h:512 * (h + 1)],
           True, True, tag="bv.bc")
        copy(P, sb["bv_bc"][0:128, 512 * h:512 * (h + 1)], pv, eng="scalar",
             tag="bv.ev")


def prep_mp_weights(ctx, names):
    P, io, sc, sb = ctx["P"], ctx["io"], ctx["sc"], ctx["sb"]
    f = Alloc(sb["fpool"])
    wt = [f.take(D) for _ in range(2)]
    dump = f.take(D)
    ssb = f.take(2)
    fcb = f.take(2)
    wbf_buf = Alloc(sb["bpool"])
    wb2 = [wbf_buf.take(D) for _ in range(2)]
    for nm in names:
        src = io[nm]
        dst = sc[nm.replace("w_", "w") + "_n"]
        for c in range(EC):
            b = c % 2
            dma(P, "sync", wt[b], src[128 * c:128 * (c + 1), 0:D], tag=f"{nm}.ld{c}")
            sv = ssb.sub(0, 128, b, b + 1)
            fv = fcb.sub(0, 128, b, b + 1)
            act(P, dump, wt[b], AF.Square, accum=sv, tag=f"{nm}.sq{c}")
            act(P, fv, sv, AF.Sqrt, tag=f"{nm}.rt{c}")
            ts(P, fv, fv, 32.0 * EPS_MP, None, ALU.add, tag=f"{nm}.ep{c}")
            recip(P, fv, fv, tag=f"{nm}.rc{c}")
            tt(P, fv, fv, sb["gain_bc"].full, ALU.mult, tag=f"{nm}.g{c}")
            ts(P, wb2[b], wt[b], fv, None, ALU.mult, tag=f"{nm}.sc{c}")
            dma(P, "gpsimd", dst[128 * c:128 * (c + 1), 0:D], wb2[b], tag=f"{nm}.st{c}")


def prep_fp4_weight(ctx, src_name, dst_name, nrows, pools):
    P, io, sc, sb = ctx["P"], ctx["io"], ctx["sc"], ctx["sb"]
    ncol = io[src_name].cols
    nh = 1 if ncol <= 2048 else 2
    hw = ncol // nh
    f = Alloc(sb["fpool"]); f.off = pools[0]
    wt = [f.take(ncol) for _ in range(2)]
    ntmp = 2 if nh == 1 else 1
    tmp = [dict(w=None, A=f.take(hw), B=f.take(hw), C=f.take(hw),
                D=f.take(hw), bm=f.take(hw // 16), rs=f.take(hw // 16))
           for _ in range(ntmp)]
    bb = Alloc(sb["bpool"]); bb.off = pools[1]
    ob = [bb.take(ncol) for _ in range(2)]
    nr = nrows // 128

    def mk_pair(c0):
        def emit():
            cs = [c for c in (c0, c0 + 1) if c < nr]
            jobs = []
            for c in cs:
                b = c % 2
                dma(P, "sync", wt[b], io[src_name][128 * c:128 * (c + 1), 0:ncol],
                    tag=f"{src_name}.ld{c}")
                for hf in range(nh):
                    t = dict(tmp[b % ntmp])
                    t["w"] = wt[b].sub(0, 128, hw * hf, hw * (hf + 1))
                    jobs.append((ob[b].sub(0, 128, hw * hf, hw * (hf + 1)),
                                 wt[b].sub(0, 128, hw * hf, hw * (hf + 1)), t))
            if ntmp >= len(jobs):
                fp4_quant_multi(P, jobs, tag=f"{src_name}.q{c0}")
            else:
                for ji, job in enumerate(jobs):
                    fp4_quant_multi(P, [job], tag=f"{src_name}.q{c0}.{ji}")
        
            for c in cs:
                b = c % 2
                dma(P, "gpsimd", sc[dst_name][128 * c:128 * (c + 1), 0:ncol], ob[b],
                    tag=f"{src_name}.st{c}")
        return emit

    step = 2 if nh == 1 else 1
    if step == 2:
        for c0 in range(0, nr, 2):
            ctx["side"].append(mk_pair(c0))
    else:
        def mk_one(c):
            def emit():
                b = c % 2
                dma(P, "sync", wt[b], io[src_name][128 * c:128 * (c + 1), 0:ncol],
                    tag=f"{src_name}.ld{c}")
                for hf in range(nh):
                    t = dict(tmp[0])
                    t["w"] = wt[b].sub(0, 128, hw * hf, hw * (hf + 1))
                    fp4_quant_multi(
                        P, [(ob[b].sub(0, 128, hw * hf, hw * (hf + 1)),
                             wt[b].sub(0, 128, hw * hf, hw * (hf + 1)), t)],
                        tag=f"{src_name}.q{c}.{hf}")
                dma(P, "gpsimd", sc[dst_name][128 * c:128 * (c + 1), 0:ncol], ob[b],
                    tag=f"{src_name}.st{c}")
            return emit
        for c in range(nr):
            ctx["side"].append(mk_one(c))


def load_wT(ctx, dst, src, tag):
    P = ctx["P"]
    dmat_blocks(P, "sync", dst, src, tag=f"{tag}.tr")


def load_l1_weight(ctx, dst, src_name, scale=None, sign=False, tag=""):
    P, io, sb = ctx["P"], ctx["io"], ctx["sb"]
    f = Alloc(sb["fpool"])
    wt = [f.take(D) for _ in range(2)]
    for c in range(EC):
        b = c % 2
        dma(P, "sync", wt[b], io[src_name][128 * c:128 * (c + 1), 0:D],
            tag=f"{tag}.ld{c}")
        dv = dst.sub(0, 128, D * c, D * (c + 1))
        if sign:
            act(P, dv, wt[b], AF.Sign, tag=f"{tag}.sg{c}")
        elif scale is not None:
            ts(P, dv, wt[b], scale, None, ALU.mult, tag=f"{tag}.sc{c}")
        else:
            copy(P, dv, wt[b], eng="vector", tag=f"{tag}.cp{c}")


def ln_tile2(P, x2, sqbuf, f, tag):
    m2 = f.take(2)
    ss = f.take(2)
    var = f.take(2)
    g3 = lambda ap: ap.rearrange("p (g b) -> p g b", b=D)
    red(P, m2, x2, ALU.add, src_fn=g3, tag=tag + ".mu")
    tt(P, sqbuf, x2, x2, ALU.mult, tag=tag + ".sq")
    red(P, ss, sqbuf, ALU.add, src_fn=g3, tag=tag + ".ss")
    ts(P, m2, m2, 1.0 / D, None, ALU.mult, tag=tag + ".m")
    tt(P, var, m2, m2, ALU.mult, tag=tag + ".m2")
    ts(P, ss, ss, 1.0 / D, 1e-5, ALU.mult, ALU.add, tag=tag + ".v")
    tt(P, ss, ss, var, ALU.subtract, tag=tag + ".ve")
    act(P, var, ss, AF.Sqrt, tag=tag + ".sd")
    recip(P, var, var, tag=tag + ".rs")
    gbm(P, x2, x2, m2, D, op=ALU.subtract, tag=tag + ".apm")
    gbm(P, x2, x2, var, D, op=ALU.mult, tag=tag + ".apr")


def ln_tile(P, xn, xt, f, tag):
    mean = f.take(1)
    ssum = f.take(1)
    var = f.take(1)
    dump = f.take(D)
    red(P, mean, xt, ALU.add, tag=tag + ".mu")
    act(P, dump, xt, AF.Square, accum=ssum, tag=tag + ".sq")
    ts(P, mean, mean, 1.0 / D, None, ALU.mult, tag=tag + ".m")
    tt(P, var, mean, mean, ALU.mult, tag=tag + ".m2")
    ts(P, ssum, ssum, 1.0 / D, var, ALU.mult, ALU.subtract, tag=tag + ".v")
    ts(P, ssum, ssum, 1e-5, None, ALU.add, tag=tag + ".ve")
    act(P, var, ssum, AF.Sqrt, tag=tag + ".sd")
    recip(P, var, var, tag=tag + ".rs")
    ts(P, xn, xt, mean, var, ALU.subtract, ALU.mult, tag=tag + ".ap")
    return xn


def mpnorm(P, out_bf, qf32, f, extra_scale, tag):
    sq = f.take(D)
    ss16 = f.take(NH)
    act(P, sq, qf32, AF.Square, tag=tag + ".sq")
    red(P, ss16, sq, ALU.add,
        src_fn=lambda ap: ap.rearrange("p (g b) -> p g b", b=HD), tag=tag + ".ss")
    act(P, ss16, ss16, AF.Sqrt, tag=tag + ".rt")
    ts(P, ss16, ss16, 0.125 * extra_scale, EPS_MP * extra_scale,
       ALU.mult, ALU.add, tag=tag + ".d")
    recip(P, ss16, ss16, tag=tag + ".rc")
    gbm(P, out_bf, qf32, ss16, HD, tag=tag + ".ap")


def emit_p0(ctx, wqT, wkT):
    P, io, sc, sb, pb = ctx["P"], ctx["io"], ctx["sc"], ctx["sb"], ctx["pb"]
    f = Alloc(sb["fpool"])
    xt2 = [f.take(2 * D) for _ in range(2)]
    qtmp2 = dict(w=f.take(2 * D), A=f.take(2 * D), B=f.take(2 * D),
                 C=f.take(2 * D), D=f.take(2 * D),
                 bm=f.take(2 * D // 16), rs=f.take(2 * D // 16))
    qfd = [f.take(D) for _ in range(2)]
    smalls = Alloc(sb["fpool"])
    smalls.off = f.off
    mpn_off = f.off + 64
    bb = Alloc(sb["bpool"])
    xnb2 = bb.take(2 * D)
    xqb2 = bb.take(2 * D)
    xnT = [bb.take(D) for _ in range(2)]
    qhd = [bb.take(D) for _ in range(2)]
    khd = [bb.take(D) for _ in range(2)]

    lag = 4
    for i in range(NT + lag):
        if i < NT and i % 2 == 0:
            i2 = i // 2
            b = i2 % 2
            x2 = xt2[b]
            dma(P, "sync", x2, io["x_in"][256 * i2:256 * (i2 + 1), 0:D],
                tag=f"p0.x{i2}",
                src_fn=lambda ap: ap.rearrange("(a p) c -> p a c", p=128))
            fs = Alloc(sb["fpool"]); fs.off = smalls.off
            ln_tile2(P, x2, qtmp2["A"], fs, tag=f"p0.ln{i2}")
            dma(P, "gpsimd", sc["xn_tm"][256 * i2:256 * (i2 + 1), 0:D], x2,
                tag=f"p0.sxn{i2}",
                dst_fn=lambda ap: ap.rearrange("(a p) c -> p a c", p=128))
            copy(P, xnb2, x2, eng="scalar", tag=f"p0.cb{i2}")
            dma(P, "gpsimd", sc["xn_bf"][256 * i2:256 * (i2 + 1), 0:D], xnb2,
                tag=f"p0.sxb{i2}",
                dst_fn=lambda ap: ap.rearrange("(a p) c -> p a c", p=128))
            fp4_quant(P, xqb2, x2, qtmp2, tag=f"p0.q{i2}")
            dma(P, "gpsimd", sc["xq"][256 * i2:256 * (i2 + 1), 0:D], xqb2,
                tag=f"p0.sxq{i2}",
                dst_fn=lambda ap: ap.rearrange("(a p) c -> p a c", p=128))
        j = i - lag
        j = i - lag
        if j < 0:
            continue
        bj = j % 2
        xT = xnT[bj]
        dmat_blocks(P, "sync", xT, sc["xn_bf"][128 * j:128 * (j + 1), 0:D],
                    tag=f"p0.xT{j}")
        bloc = j // NT_B
        tl = j % NT_B
        P.hint("tensor", [xT], tag=f"p0.h{j}")
        qhb = qhd[bj]
        khb = khd[bj]
        qf32 = qfd[bj]
        for (wT, pbs, outb, xsc, nm) in ((wqT, (0, 1), qhb, 1.0, "q"),
                                         (wkT, (2, 3), khb, SQS, "k")):
            for half in range(2):
                pv = pb[pbs[half]][0:128, 0:512]
                for ec in range(EC):
                    mm(P, pv, xT.sub(0, 128, 128 * ec, 128 * (ec + 1)),
                       wT.sub(0, 128, D * ec + 512 * half, D * ec + 512 * (half + 1)),
                       start=(ec == 0), stop=(ec == EC - 1),
                       tag=f"p0.{nm}mm{j}.{half}.{ec}")
            for half in range(2):
                copy(P, qf32.sub(0, 128, 512 * half, 512 * (half + 1)),
                     pb[pbs[half]][0:128, 0:512], eng="scalar",
                     tag=f"p0.{nm}ev{j}.{half}")
            fs = Alloc(sb["fpool"]); fs.off = mpn_off
            mpnorm(P, outb, qf32, fs, xsc, tag=f"p0.{nm}n{j}")
        for n in range(NH):
            bank = 4 + 2 * bloc + (n // 8)
            pv = pb[bank][0:64, 64 * (n % 8):64 * (n % 8) + 64]
            mm(P, pv, View(qhb.buf, 0, 128, qhb.c0 + 64 * n, qhb.c0 + 64 * (n + 1)),
               View(khb.buf, 0, 128, khb.c0 + 64 * n, khb.c0 + 64 * (n + 1)),
               start=(tl == 0 and n % 8 == 0), stop=(tl == NT_B - 1 and n % 8 == 7),
               tag=f"p0.sc{j}.{n}")

    f2 = Alloc(sb["fpool"])
    scf = f2.take(2 * D)
    mx = f2.take(2 * NH)
    pnorm = f2.take(2 * NH)
    for bloc in range(B_PER_CORE):
        for half in range(2):
            copy(P, scf.sub(0, 64, D * bloc + 512 * half, D * bloc + 512 * (half + 1)),
                 pb[4 + 2 * bloc + half][0:64, 0:512], eng="scalar",
                 tag=f"p0.scev{bloc}.{half}")
    sc64 = scf.sub(0, 64, 0, 2 * D)
    mxv = mx.sub(0, 64, 0, 2 * NH)
    red(P, mxv, sc64, ALU.max, negate=True,
        src_fn=lambda ap: ap.rearrange("p (g b) -> p g b", b=HD), tag="p0.smx")
    gbm(P, sc64, sc64, mxv, HD, op=ALU.add, tag="p0.ssub")
    act(P, sc64, sc64, AF.Exp, tag="p0.sexp")
    pn = pnorm.sub(0, 64, 0, 2 * NH)
    red(P, pn, sc64, ALU.add,
        src_fn=lambda ap: ap.rearrange("p (g b) -> p g b", b=HD), tag="p0.ssum")
    recip(P, pn, pn, tag="p0.srec")
    gbm(P, sc64, sc64, pn, HD, tag="p0.spn")
    memset(P, sb["attnT"].full, 0.0, tag="p0.atz")
    atmp = bb.take(64)
    for bn in range(B_PER_CORE * NH):
        pv = pb[bn % 2][0:64, 0:64]
        petr(P, pv, scf.sub(0, 64, 64 * bn, 64 * (bn + 1)),
             sb["identf"][0:64, 0:64], tag=f"p0.ptr{bn}")
        copy(P, View(atmp.buf, 0, 64, atmp.c0, atmp.c0 + 64), pv, eng="scalar",
             tag=f"p0.ptc{bn}")
        dma(P, "gpsimd", ctx["sc"]["attn_sc"][64 * bn:64 * (bn + 1), 0:64],
            View(atmp.buf, 0, 64, atmp.c0, atmp.c0 + 64), tag=f"p0.pts{bn}")
    for pair in range(B_PER_CORE * NH // 2):
        for hh in range(2):
            bn = 2 * pair + hh
            dma(P, "sync",
                sb["attnT"][64 * hh:64 * (hh + 1),
                            128 * pair + 64 * hh:128 * pair + 64 * (hh + 1)],
                ctx["sc"]["attn_sc"][64 * bn:64 * (bn + 1), 0:64],
                tag=f"p0.ptl{bn}")


def sc_x_tile(buf, i):
    return buf[128 * i:128 * (i + 1), 0:buf.cols]


def drain_side(ctx, k=1):
    for _ in range(k):
        if not ctx["side"]:
            return
        ctx["side"].pop(0)()


def emit_p1a(ctx, wvT, woT):
    P, sc, sb, pb = ctx["P"], ctx["sc"], ctx["sb"], ctx["pb"]
    f = Alloc(sb["fpool"])
    xnf = [f.take(D) for _ in range(2)]
    soutd = [f.take(D) for _ in range(2)]
    bb = Alloc(sb["bpool"])
    xnT = [bb.take(D) for _ in range(2)]
    vbd = [bb.take(D) for _ in range(2)]
    obd = [bb.take(D) for _ in range(2)]
    for j in range(NT):
        drain_side(ctx, 1)
        bj = j % 2
        xT = xnT[bj]
        vbf = vbd[bj]
        obf = obd[bj]
        sout = soutd[bj]
        dmat_blocks(P, "sync", xT, sc["xn_bf"][128 * j:128 * (j + 1), 0:D],
                    tag=f"p1a.xT{j}")
        dma(P, "sync", xnf[bj], sc_x_tile(sc["xn_tm"], j), tag=f"p1a.xn{j}")
        P.hint("tensor", [xT, sb["attnT"].full], tag=f"p1a.h{j}")
        bloc = j // NT_B
        for oc in range(EC):
            pv = pb[(0, 1, 6, 7)[oc % 4]][0:128, 0:128]
            for ec in range(EC):
                mm(P, pv, wvT.sub(0, 128, D * ec + 128 * oc, D * ec + 128 * (oc + 1)),
                   xT.sub(0, 128, 128 * ec, 128 * (ec + 1)),
                   start=(ec == 0), stop=(ec == EC - 1), tag=f"p1a.v{j}.{oc}.{ec}")
            copy(P, vbf.sub(0, 128, 128 * oc, 128 * (oc + 1)), pv, eng="scalar",
                 tag=f"p1a.vev{j}.{oc}")
        for oc in range(EC):
            pv = pb[2 + oc % 2]
            mm(P, pv[0:128, 0:128],
               sb["attnT"][0:128, 1024 * bloc + 128 * oc:1024 * bloc + 128 * (oc + 1)],
               vbf.sub(0, 128, 128 * oc, 128 * (oc + 1)),
               start=True, stop=True, tag=f"p1a.o{j}.{oc}")
            copy(P, obf.sub(0, 128, 128 * oc, 128 * (oc + 1)),
                 pv[0:128, 0:128], eng="scalar", tag=f"p1a.oev{j}.{oc}")
        for half in range(2):
            pv = pb[4 + half][0:128, 0:512]
            for oc in range(EC):
                mm(P, pv, obf.sub(0, 128, 128 * oc, 128 * (oc + 1)),
                   woT.sub(0, 128, D * oc + 512 * half, D * oc + 512 * (half + 1)),
                   start=(oc == 0), stop=(oc == EC - 1), tag=f"p1a.ao{j}.{half}.{oc}")
        for half in range(2):
            sv = sout.sub(0, 128, 512 * half, 512 * (half + 1))
            tt(P, sv, xnf[bj].sub(0, 128, 512 * half, 512 * (half + 1)),
               pb[4 + half][0:128, 0:512], ALU.add, tag=f"p1a.sadd{j}.{half}")
            ts(P, sv, sv, RS2, None, ALU.mult, tag=f"p1a.smul{j}.{half}")
        dma(P, "gpsimd", sc_x_tile(sc["s_tm"], j), sout, tag=f"p1a.ss{j}")


def emit_p1b_gate(ctx, wgT):
    P, sc, sb, pb = ctx["P"], ctx["sc"], ctx["sb"], ctx["pb"]
    f = Alloc(sb["fpool"])
    goutd = [f.take(I_FFN) for _ in range(2)]
    bb = Alloc(sb["bpool"])
    xqT = [bb.take(D) for _ in range(2)]
    nsub = (I_FFN + 511) // 512
    for j in range(NT):
        drain_side(ctx, 1)
        bj = j % 2
        xT = xqT[bj]
        gout = goutd[bj]
        dmat_blocks(P, "sync", xT, sc["xq"][128 * j:128 * (j + 1), 0:D],
                    tag=f"p1g.xT{j}")
        P.hint("tensor", [xT], tag=f"p1g.h{j}")
        for si in range(nsub):
            i0 = 512 * si
            nw = min(512, I_FFN - i0)
            pv = pb[si % 4][0:128, 0:nw]
            for ec in range(EC):
                mm(P, pv, xT.sub(0, 128, 128 * ec, 128 * (ec + 1)),
                   wgT.sub(0, 128, I_FFN * ec + i0, I_FFN * ec + i0 + nw),
                   start=(ec == 0), stop=(ec == EC - 1), tag=f"p1g.mm{j}.{si}.{ec}")
            act(P, gout.sub(0, 128, i0, i0 + nw), pv, AF.Silu,
                tag=f"p1g.si{j}.{si}")
        dma(P, "gpsimd", sc_x_tile(sc["gate_s"], j), gout, tag=f"p1g.st{j}")


def emit_p1b_up(ctx, wuT):
    P, sc, sb, pb = ctx["P"], ctx["sc"], ctx["sb"], ctx["pb"]
    f = Alloc(sb["fpool"])
    h = [f.take(I_FFN) for _ in range(2)]
    HB = I_FFN // 2
    qtmpd = [dict(w=None, A=f.take(HB), B=f.take(HB), C=f.take(HB),
                  D=f.take(HB), bm=f.take(HB // 16), rs=f.take(HB // 16))
             for _ in range(2)]
    bb = Alloc(sb["bpool"])
    xqT = [bb.take(D) for _ in range(2)]
    hqd = [bb.take(I_FFN) for _ in range(2)]
    nsub = (I_FFN + 511) // 512

    def stage_mm(j):
        bj = j % 2
        xT = xqT[bj]
        ht = h[bj]
        dmat_blocks(P, "sync", xT, sc["xq"][128 * j:128 * (j + 1), 0:D],
                    tag=f"p1u.xT{j}")
        dma(P, "sync", ht, sc_x_tile(sc["gate_s"], j), tag=f"p1u.g{j}")
        P.hint("tensor", [xT], tag=f"p1u.h{j}")
        for si in range(nsub):
            i0 = 512 * si
            nw = min(512, I_FFN - i0)
            pv = pb[si % 4][0:128, 0:nw]
            for ec in range(EC):
                mm(P, pv, xT.sub(0, 128, 128 * ec, 128 * (ec + 1)),
                   wuT.sub(0, 128, I_FFN * ec + i0, I_FFN * ec + i0 + nw),
                   start=(ec == 0), stop=(ec == EC - 1), tag=f"p1u.mm{j}.{si}.{ec}")
            tt(P, ht.sub(0, 128, i0, i0 + nw), ht.sub(0, 128, i0, i0 + nw),
               pv, ALU.mult, tag=f"p1u.h{j}.{si}")

    def stage_quant(j):
        bj = j % 2
        ht = h[bj]
        hqb = hqd[bj]
        jobs = []
        for hf in range(2):
            t = dict(qtmpd[hf])
            t["w"] = ht.sub(0, 128, HB * hf, HB * (hf + 1))
            jobs.append((hqb.sub(0, 128, HB * hf, HB * (hf + 1)),
                         ht.sub(0, 128, HB * hf, HB * (hf + 1)), t))
        fp4_quant_multi(P, jobs, tag=f"p1u.q{j}", act_rounds=True)
        dma(P, "gpsimd", sc_x_tile(sc["hq"], j), hqb, tag=f"p1u.st{j}")

    for j in range(NT + 1):
        if j < NT:
            stage_mm(j)
        if j >= 1:
            stage_quant(j - 1)

def emit_p1b_down(ctx, wdT):
    P, io, sc, sb, pb = ctx["P"], ctx["io"], ctx["sc"], ctx["sb"], ctx["pb"]
    f = Alloc(sb["fpool"])
    gd = [f.take(D) for _ in range(2)]
    xtile = [f.take(D) for _ in range(2)]
    stile = [f.take(D) for _ in range(2)]
    bb = Alloc(sb["bpool"])
    hqT = [bb.take(IC * 128) for _ in range(2)]
    for j in range(NT):
        bj = j % 2
        hT = hqT[bj]
        g = gd[bj]
        dmat_blocks(P, "sync", hT, sc["hq"][128 * j:128 * (j + 1), 0:I_FFN],
                    tag=f"p1d.hT{j}")
        dma(P, "sync", xtile[bj], sc_x_tile(io["x_in"], j), tag=f"p1d.x{j}")
        dma(P, "sync", stile[bj], sc_x_tile(sc["s_tm"], j), tag=f"p1d.s{j}")
        P.hint("tensor", [hT], tag=f"p1d.h{j}")
        for half in range(2):
            pv = pb[2 * (j % 2) + half][0:128, 0:512]
            for ic in range(IC):
                mm(P, pv, hT.sub(0, 128, 128 * ic, 128 * (ic + 1)),
                   wdT.sub(0, 128, D * ic + 512 * half, D * ic + 512 * (half + 1)),
                   start=(ic == 0), stop=(ic == IC - 1), tag=f"p1d.mm{j}.{half}.{ic}")
            gv = g.sub(0, 128, 512 * half, 512 * (half + 1))
            act(P, gv, pv, AF.Silu, tag=f"p1d.si{j}.{half}")
            tt(P, gv, gv, stile[bj].sub(0, 128, 512 * half, 512 * (half + 1)),
               ALU.mult, tag=f"p1d.sg{j}.{half}")
            tt(P, gv, gv, xtile[bj].sub(0, 128, 512 * half, 512 * (half + 1)),
               ALU.add, tag=f"p1d.x1{j}.{half}")
        dma(P, "gpsimd", sc_x_tile(sc["x1"], j), g, tag=f"p1d.st{j}")


def emit_p2a(ctx, WqTs, WkTs, t0=0, t1=NT, drain=0):
    P, sc, sb, pb = ctx["P"], ctx["sc"], ctx["sb"], ctx["pb"]
    f = Alloc(sb["fpool"])
    x1t2 = [f.take(2 * D) for _ in range(2)]
    sqb = f.take(2 * D)
    smalls = f.off
    bb = Alloc(sb["bpool"])
    xnb2 = bb.take(2 * D)
    xT = [bb.take(D) for _ in range(2)]
    qkd = [bb.take(D) for _ in range(2)]
    lag = 4
    for i in range(t0, t1 + lag):
        if drain:
            drain_side(ctx, drain)
        if i < t1 and i % 2 == 0:
            i2 = i // 2
            b = i2 % 2
            x2 = x1t2[b]
            dma(P, "sync", x2, sc["x1"][256 * i2:256 * (i2 + 1), 0:D],
                tag=f"p2.x{i2}",
                src_fn=lambda ap: ap.rearrange("(a p) c -> p a c", p=128))
            fs = Alloc(sb["fpool"]); fs.off = smalls
            ln_tile2(P, x2, sqb, fs, tag=f"p2.ln{i2}")
            copy(P, xnb2, x2, eng="scalar", tag=f"p2.cb{i2}")
            dma(P, "gpsimd", sc["xn1_bf"][256 * i2:256 * (i2 + 1), 0:D], xnb2,
                tag=f"p2.sxb{i2}",
                dst_fn=lambda ap: ap.rearrange("(a p) c -> p a c", p=128))
        j = i - lag
        if j < t0:
            continue
        bj = j % 2
        xTj = xT[bj]
        qkb = qkd[bj]
        dmat_blocks(P, "sync", xTj, sc["xn1_bf"][128 * j:128 * (j + 1), 0:D],
                    tag=f"p2.xT{j}")
        bloc = j // NT_B
        tl = j % NT_B
        P.hint("tensor", [xTj], tag=f"p2a.h{j}")
        for (wT, dst, bcol, nm) in ((WqTs, sc["qf"], 0, "q"), (WkTs, sc["kf"], 1, "k")):
            for oc in range(EC):
                pv = pb[oc % 2][0:128, 0:128]
                for ec in range(EC):
                    mm(P, pv, wT.sub(0, 128, D * ec + 128 * oc, D * ec + 128 * (oc + 1)),
                       xTj.sub(0, 128, 128 * ec, 128 * (ec + 1)),
                       start=(ec == 0), stop=(ec == EC - 1),
                       tag=f"p2.{nm}{j}.{oc}.{ec}")
                ov = qkb.sub(0, 128, 128 * oc, 128 * (oc + 1))
                ts(P, ov, pv, sb["bias_pp"][0:128, 8 * bcol + oc:8 * bcol + oc + 1],
                   None, ALU.add, tag=f"p2.{nm}b{j}.{oc}")
            dma(P, "gpsimd",
                dst[D * bloc:D * bloc + D, 128 * tl:128 * (tl + 1)],
                qkb, tag=f"p2.{nm}st{j}",
                dst_fn=lambda ap: ap.rearrange("(o p) c -> p o c", p=128),
                src_fn=lambda ap: ap.rearrange("p (o c) -> p o c", c=128))


def emit_p2b(ctx, WvTs, WfTs, t0=0, t1=NT, drain=0):
    P, sc, sb, pb = ctx["P"], ctx["sc"], ctx["sb"], ctx["pb"]
    f = Alloc(sb["fpool"])
    g1d = [f.take(D) for _ in range(2)]
    bb = Alloc(sb["bpool"])
    xT = [bb.take(D) for _ in range(2)]
    vbd = [bb.take(D) for _ in range(2)]
    for j in range(t0, t1):
        if drain:
            drain_side(ctx, drain)
        bj = j % 2
        xTj = xT[bj]
        vb = vbd[bj]
        g1t = g1d[bj]
        dmat_blocks(P, "sync", xTj, sc["xn1_bf"][128 * j:128 * (j + 1), 0:D],
                    tag=f"p2b.xT{j}")
        bloc = j // NT_B
        tl = j % NT_B
        P.hint("tensor", [xTj], tag=f"p2b.h{j}")
        for half in range(2):
            pv = pb[half][0:128, 0:512]
            for ec in range(EC):
                mm(P, pv, xTj.sub(0, 128, 128 * ec, 128 * (ec + 1)),
                   WvTs.sub(0, 128, D * ec + 512 * half, D * ec + 512 * (half + 1)),
                   start=(ec == 0), stop=(ec == EC - 1), tag=f"p2b.v{j}.{half}.{ec}")
            tt(P, vb.sub(0, 128, 512 * half, 512 * (half + 1)), pv,
               sb["bv_bc"][0:128, 512 * half:512 * (half + 1)], ALU.add,
               tag=f"p2b.vb{j}.{half}")
        tw = [sc["vsc"][(bloc * NH + n) * S + 128 * tl:
                        (bloc * NH + n) * S + 128 * tl + 128, 0:HD]
              for n in range(NH)]
        dma(P, "gpsimd", sc["vsc"].full, vb, tag=f"p2b.vst{j}",
            dst_fn=lambda ap, _b=bloc, _t=tl: ap.rearrange(
                "(b n q p) c -> b q p n c", b=B_PER_CORE, n=NH, p=128)[_b, _t],
            src_fn=lambda ap: ap.rearrange("p (n c) -> p n c", c=HD),
            track_writes=tw)
        for half in range(2):
            pv = pb[half][0:128, 0:512]
            for ec in range(EC):
                mm(P, pv, xTj.sub(0, 128, 128 * ec, 128 * (ec + 1)),
                   WfTs.sub(0, 128, D * ec + 512 * half, D * ec + 512 * (half + 1)),
                   start=(ec == 0), stop=(ec == EC - 1), tag=f"p2b.f{j}.{half}.{ec}")
            act(P, g1t.sub(0, 128, 512 * half, 512 * (half + 1)), pv, AF.Silu,
                tag=f"p2b.g1{j}.{half}")
        dma(P, "gpsimd", sc_x_tile(sc["g1"], j), g1t, tag=f"p2b.g1st{j}")


def emit_p3(ctx, WoTs):
    P, io, sc, sb, pb = ctx["P"], ctx["io"], ctx["sc"], ctx["sb"], ctx["pb"]
    a = Alloc(sb["apool"])
    qtd = [a.take(S, rows=65) for _ in range(2)]
    ktd = [a.take(S, rows=65) for _ in range(2)]
    vtd = [a.take(NH * 65) for _ in range(2)]
    PTd = [a.take(S) for _ in range(2)]
    opair = a.take(S)
    obt = [a.take(1024) for _ in range(2)]
    f = Alloc(sb["fpool"])
    q2 = f.take(S, rows=64)
    ssq = f.take(S, rows=1)
    aq = f.take(S, rows=1)
    kmax = f.take(1, rows=1)
    rl = f.take(S, rows=1)
    lbc = f.take(S, rows=64)
    s1d = [f.take(D) for _ in range(2)]
    x1d = [f.take(D) for _ in range(2)]
    g1d = [f.take(D) for _ in range(2)]
    for z in range(2):
        memset(P, View(vtd[z].buf, 0, 128, vtd[z].c0, vtd[z].c1), 1.0, tag=f"p3.vones{z}")
        memset(P, View(ktd[z].buf, 64, 65, ktd[z].c0, ktd[z].c1), 1.0, tag=f"p3.kones{z}")

    nsub = S // 512

    def stage_load_norms(bloc, n):
        qt, kt, vt = qtd[n % 2], ktd[n % 2], vtd[n % 2]
        qrow = D * bloc + HD * (n % 2) + 128 * (n // 2)
        dma(P, "sync", View(qt.buf, 0, 64, qt.c0, qt.c1),
            sc["qf"][qrow:qrow + 64, 0:S], tag=f"p3.q{bloc}.{n}")
        dma(P, "sync", View(kt.buf, 0, 64, kt.c0, kt.c1),
            sc["kf"][qrow:qrow + 64, 0:S], tag=f"p3.k{bloc}.{n}")
        vrow = (bloc * NH + n) * S
        dma(P, "sync", View(vt.buf, 0, 128, vt.c0, vt.c1),
            sc["vsc"][vrow:vrow + S, 0:HD], tag=f"p3.v{bloc}.{n}",
            dst_fn=lambda ap: ap.rearrange("p (j c) -> p j c", c=65)[:, :, 0:HD],
            src_fn=lambda ap: ap.rearrange("(j p) c -> p j c", p=128))
        qv = View(qt.buf, 0, 64, qt.c0, qt.c1)
        kv = View(kt.buf, 0, 64, kt.c0, kt.c1)
        tt(P, q2, qv, qv, ALU.mult, tag=f"p3.q2{bloc}.{n}")
        spots = [(2, 0), (2, 32), (2, 64), (3, 0)]
        for si in range(nsub):
            bk, p0 = spots[si]
            mm(P, pb[bk][p0:p0 + 1, 0:512], sb["ones_col"][0:64, 0:1],
               q2.sub(0, 64, 512 * si, 512 * (si + 1)), True, True,
               tag=f"p3.ssq{bloc}.{n}.{si}")
        for si in range(nsub):
            bk, p0 = spots[si]
            copy(P, aq.sub(0, 1, 512 * si, 512 * (si + 1)),
                 pb[bk][p0:p0 + 1, 0:512], eng="vector",
                 tag=f"p3.ssqe{bloc}.{n}.{si}")
        tt(P, q2, kv, kv, ALU.mult, tag=f"p3.k2{bloc}.{n}")
        for si in range(nsub):
            bk, p0 = spots[si]
            mm(P, pb[bk][p0:p0 + 1, 0:512], sb["ones_col"][0:64, 0:1],
               q2.sub(0, 64, 512 * si, 512 * (si + 1)), True, True,
               tag=f"p3.ssk{bloc}.{n}.{si}")
        for si in range(nsub):
            bk, p0 = spots[si]
            copy(P, ssq.sub(0, 1, 512 * si, 512 * (si + 1)),
                 pb[bk][p0:p0 + 1, 0:512], eng="vector",
                 tag=f"p3.sske{bloc}.{n}.{si}")
        red(P, kmax, ssq, ALU.max, absval=True, tag=f"p3.km{bloc}.{n}")
        ts(P, aq, aq, kmax, None, ALU.mult, tag=f"p3.mh1{bloc}.{n}")
        act(P, aq, aq, AF.Sqrt, tag=f"p3.mh2{bloc}.{n}")
        ts(P, aq, aq, -1.0, None, ALU.mult, tag=f"p3.mh3{bloc}.{n}")
        copy(P, View(qt.buf, 64, 65, qt.c0, qt.c1), aq, eng="vector",
             tag=f"p3.mhc{bloc}.{n}")

    def stage_jloop(bloc, n):
        qt, kt, vt = qtd[n % 2], ktd[n % 2], vtd[n % 2]
        segs = []
        for jb in range(NT_B):
            i0 = 128 * jb
            for q4 in range(4):
                ci = max(i0, 512 * q4)
                ce = 512 * (q4 + 1)
                if ci < ce:
                    segs.append((jb, q4, ci, ce - ci))
        nseg = len(segs)
        LAGS = 2
        for idx in range(nseg + LAGS):
            if LAGS <= idx < nseg + LAGS:
                jb, q4, ci, nw = segs[idx - LAGS]
                PT = PTd[(idx - LAGS) % 2]
                opv = pb[4 + q4][0:65, (ci % 512):(ci % 512) + nw]
                mm(P, opv, View(vt.buf, 0, 128, vt.c0 + 65 * jb, vt.c0 + 65 * (jb + 1)),
                   PT.sub(0, 128, ci, ci + nw),
                   start=(jb == 0), stop=(jb == min(4 * q4 + 3, NT_B - 1)),
                   tag=f"p3.O{bloc}.{n}.{jb}.{ci}")
            if idx < nseg:
                jb, q4, ci, nw = segs[idx]
                i0 = 128 * jb
                PT = PTd[idx % 2]
                spv = pb[2 + idx % 2][0:128, 0:nw]
                mm(P, spv, View(kt.buf, 0, 65, kt.c0 + i0, kt.c0 + i0 + 128),
                   View(qt.buf, 0, 65, qt.c0 + ci, qt.c0 + ci + nw),
                   start=True, stop=True, tag=f"p3.S{bloc}.{n}.{jb}.{ci}")
                act(P, PT.sub(0, 128, ci, ci + nw), spv, AF.Exp,
                    tag=f"p3.E{bloc}.{n}.{jb}.{ci}")
                if ci == i0:
                    tt(P, PT.sub(0, 128, i0, i0 + 128),
                       PT.sub(0, 128, i0, i0 + 128), sb["tri01"].full,
                       ALU.mult, eng="vector", tag=f"p3.D{bloc}.{n}.{jb}")

    def stage_norm_evac(bloc, n):
        for q4 in range(4):
            copy(P, rl.sub(0, 1, 512 * q4, 512 * (q4 + 1)),
                 pb[4 + q4][64:65, 0:512], eng="vector", tag=f"p3.rlc{bloc}.{n}.{q4}")
        act(P, rl, rl, AF.Ln, tag=f"p3.rln{bloc}.{n}")
        act(P, rl, rl, AF.Exp, scale=-1.0, tag=f"p3.rlx{bloc}.{n}")
        lspots = [(2, 0), (2, 64), (3, 0), (3, 64)]
        for q4 in range(4):
            bk, p0 = lspots[q4]
            mm(P, pb[bk][p0:p0 + 64, 0:512], sb["ones_row"][0:1, 0:64],
               rl.sub(0, 1, 512 * q4, 512 * (q4 + 1)), True, True,
               tag=f"p3.lb{bloc}.{n}.{q4}")
        for q4 in range(4):
            bk, p0 = lspots[q4]
            copy(P, lbc.sub(0, 64, 512 * q4, 512 * (q4 + 1)),
                 pb[bk][p0:p0 + 64, 0:512], eng="vector",
                 tag=f"p3.lbe{bloc}.{n}.{q4}")
        hh = n % 2
        for q4 in range(4):
            tt(P, opair.sub(64 * hh, 64 * hh + 64, 512 * q4, 512 * (q4 + 1)),
               pb[4 + q4][0:64, 0:512],
               lbc.sub(0, 64, 512 * q4, 512 * (q4 + 1)),
               ALU.mult, tag=f"p3.on{bloc}.{n}.{q4}")
        if hh == 1:
            oc = n // 2
            r0 = D * bloc + 128 * oc
            dma(P, "gpsimd", sc["obsc"][r0:r0 + 128, 0:S],
                View(opair.buf, 0, 128, opair.c0, opair.c1),
                tag=f"p3.ost{bloc}.{oc}")

    def head_iter(bloc, n):
        if n < NH:
            stage_load_norms(bloc, n)
        if n >= 1:
            stage_jloop(bloc, n - 1)
            stage_norm_evac(bloc, n - 1)

    def queue_heads(bloc):
        for n in range(NH + 1):
            ctx["side"].append(lambda _b=bloc, _n=n: head_iter(_b, _n))

    def run_heads(bloc, drain=0):
        for n in range(NH + 1):
            head_iter(bloc, n)
            if drain:
                drain_side(ctx, drain)

    def outproj(bloc):
        for tl in range(NT_B):
            outproj_tile(bloc, tl)

    def outproj_tile(bloc, tl):
            jt = bloc * NT_B + tl
            bj = tl % 2
            ot = obt[bj]
            s1 = s1d[bj]
            x1t = x1d[bj]
            g1t = g1d[bj]
            dma(P, "sync", ot,
                sc["obsc"][D * bloc:D * bloc + D, 128 * tl:128 * (tl + 1)],
                tag=f"p3.ol{jt}",
                dst_fn=lambda ap: ap.rearrange("p (o c) -> p o c", c=128),
                src_fn=lambda ap: ap.rearrange("(o p) c -> p o c", p=128))
            dma(P, "sync", x1t, sc_x_tile(sc["x1"], jt), tag=f"p3.x1{jt}")
            dma(P, "sync", g1t, sc_x_tile(sc["g1"], jt), tag=f"p3.g1{jt}")
            for half in range(2):
                pv = pb[half][0:128, 0:512]
                for oc in range(EC):
                    mm(P, pv, ot.sub(0, 128, 128 * oc, 128 * (oc + 1)),
                       WoTs.sub(0, 128, D * oc + 512 * half, D * oc + 512 * (half + 1)),
                       start=(oc == 0), stop=(oc == EC - 1),
                       tag=f"p3.pr{jt}.{half}.{oc}")
                sv = s1.sub(0, 128, 512 * half, 512 * (half + 1))
                tt(P, sv, pv, sb["bo_bc"][0:128, 512 * half:512 * (half + 1)],
                   ALU.add, tag=f"p3.bo{jt}.{half}")
                tt(P, sv, sv, g1t.sub(0, 128, 512 * half, 512 * (half + 1)),
                   ALU.mult, tag=f"p3.sg{jt}.{half}")
                tt(P, sv, sv, x1t.sub(0, 128, 512 * half, 512 * (half + 1)),
                   ALU.add, tag=f"p3.x2{jt}.{half}")
            dma(P, "gpsimd", sc_x_tile(io["out"], jt), s1, tag=f"p3.out{jt}")

    def queue_outproj(bloc):
        for tl in range(NT_B):
            ctx["side"].append(lambda _b=bloc, _t=tl: outproj_tile(_b, _t))

    ctx["p3"] = dict(queue_heads=queue_heads, run_heads=run_heads,
                     outproj=outproj, queue_outproj=queue_outproj)



def build_full(debug_taps=(), upto=99):
    ctx = build_prog(debug_taps)
    P, sb = ctx["P"], ctx["sb"]
    wp = Alloc(sb["wpool"])
    wA = wp.take(EC * D)
    wB = wp.take(EC * D)
    wBig = View(sb["wpool"], 0, 128, 0, EC * I_FFN)

    setup_misc(ctx)
    if upto >= 0:
        prep_mp_weights(ctx, ("w_q", "w_k"))
        load_wT(ctx, wA, ctx["sc"]["wq_n"].full, "wqT")
        load_wT(ctx, wB, ctx["sc"]["wk_n"].full, "wkT")
        emit_p0(ctx, wA, wB)
    SIDE_POOLS = (6144, 6144)
    if upto >= 1:
        prep_mp_weights(ctx, ("w_v", "w_o"))
        load_wT(ctx, wA, ctx["sc"]["wv_n"].full, "wvT")
        load_wT(ctx, wB, ctx["sc"]["wo_n"].full, "woT")
        if upto >= 2:
            prep_fp4_weight(ctx, "w_gate", "wg_n", I_FFN, SIDE_POOLS)
        emit_p1a(ctx, wA, wB)
        drain_side(ctx, 99)
    if upto >= 2:
        load_wT(ctx, wBig, ctx["sc"]["wg_n"].full, "wgT")
        if upto >= 3:
            prep_fp4_weight(ctx, "w_up", "wu_n", I_FFN, SIDE_POOLS)
        if upto >= 4:
            prep_fp4_weight(ctx, "w_down", "wd_n", D, SIDE_POOLS)
        emit_p1b_gate(ctx, wBig)
        drain_side(ctx, 99)
    if upto >= 3:
        load_wT(ctx, wBig, ctx["sc"]["wu_n"].full, "wuT")
        emit_p1b_up(ctx, wBig)
    if upto >= 4:
        load_wT(ctx, wBig, ctx["sc"]["wd_n"].full, "wdT")
        emit_p1b_down(ctx, wBig)
    if upto >= 5 and upto < 6:
        load_l1_weight(ctx, wA, "WqT", scale=0.125, tag="l1q")
        load_l1_weight(ctx, wB, "WkT", tag="l1k")
        emit_p2a(ctx, wA, wB)
        load_l1_weight(ctx, wC, "WvT", tag="l1v")
        load_l1_weight(ctx, wD, "WfT", sign=True, tag="l1f")
        emit_p2b(ctx, wC, wD)
    if upto >= 6:
        load_l1_weight(ctx, wA, "WqT", scale=0.125, tag="l1q")
        load_l1_weight(ctx, wB, "WkT", tag="l1k")
        emit_p2a(ctx, wA, wB, 0, NT // 2)
        load_l1_weight(ctx, wA, "WvT", tag="l1v0")
        load_l1_weight(ctx, wB, "WfT", sign=True, tag="l1f0")
        emit_p2b(ctx, wA, wB, 0, NT // 2)
        wO = View(ctx["sb"]["bpool"], 0, 128, 6144, 6144 + EC * D)
        load_l1_weight(ctx, wO, "WoT", tag="l1o")
        emit_p3(ctx, wO)
        ctx["p3"]["queue_heads"](0)
        load_l1_weight(ctx, wA, "WqT", scale=0.125, tag="l1q1")
        load_l1_weight(ctx, wB, "WkT", tag="l1k1")
        emit_p2a(ctx, wA, wB, NT // 2, NT, drain=1)
        load_l1_weight(ctx, wA, "WvT", tag="l1v1")
        load_l1_weight(ctx, wB, "WfT", sign=True, tag="l1f1")
        emit_p2b(ctx, wA, wB, NT // 2, NT, drain=1)
        drain_side(ctx, 99)
        ctx["p3"]["queue_outproj"](0)
        ctx["p3"]["run_heads"](1, drain=1)
        drain_side(ctx, 99)
        ctx["p3"]["outproj"](1)
    return ctx



def make_inputs_per_core(inputs, core):
    f32 = np.float32
    x = np.ascontiguousarray(
        inputs["L0_attn_query"][2 * core:2 * core + 2].reshape(T, D), dtype=f32)
    d = {"x_in": x}
    d["gain"] = np.asarray(inputs["L0_attn_gain"], f32).reshape(1, 1)
    d["w_q"] = np.ascontiguousarray(inputs["L0_attn_w_q"], f32)
    d["w_k"] = np.ascontiguousarray(inputs["L0_attn_w_k"], f32)
    d["w_v"] = np.ascontiguousarray(inputs["L0_attn_w_v"], f32)
    d["w_o"] = np.ascontiguousarray(inputs["L0_attn_w_o"], f32)
    d["w_gate"] = np.ascontiguousarray(inputs["L0_ffn_w_gate"], f32)
    d["w_up"] = np.ascontiguousarray(inputs["L0_ffn_w_up"], f32)
    d["w_down"] = np.ascontiguousarray(inputs["L0_ffn_w_down"], f32)
    d["WqT"] = np.ascontiguousarray(np.asarray(inputs["L1_attn_Wq"], f32).T)
    d["WkT"] = np.ascontiguousarray(np.asarray(inputs["L1_attn_Wk"], f32).T)
    d["WvT"] = np.ascontiguousarray(np.asarray(inputs["L1_attn_Wv"], f32).T)
    d["WoT"] = np.ascontiguousarray(np.asarray(inputs["L1_attn_Wo"], f32).T)
    d["WfT"] = np.ascontiguousarray(np.asarray(inputs["L1_ffn_weight"], f32).T)
    bias = np.concatenate([np.asarray(inputs["L1_attn_b" + k], f32).reshape(8, 128)
                           for k in ("q", "k", "v", "o")], axis=1)
    d["biases"] = np.ascontiguousarray(bias)
    d["bo_row"] = np.asarray(inputs["L1_attn_bo"], f32).reshape(1, D)
    d["bv_row"] = np.asarray(inputs["L1_attn_bv"], f32).reshape(1, D)
    d["identf"] = np.eye(128, dtype=f32)
    tri = (np.arange(128)[None, :] >= np.arange(128)[:, None])
    d["tri01"] = tri.astype(ml_dtypes.bfloat16)
    d["consts"] = np.tile(np.array(CONST_VALS + [0.0] * (8 - len(CONST_VALS)),
                                   dtype=f32)[None, :], (128, 1))
    return d


def kernel(**inputs):
    ctx = build_full()
    nc, stack = ctx["P"].build()
    try:
        in_maps = [make_inputs_per_core(inputs, c) for c in range(8)]
        res = run_bass_kernel_spmd(nc, in_maps, core_ids=list(range(8)))
    finally:
        stack.close()
    out = np.concatenate(
        [res.results[c]["out"].reshape(B_PER_CORE, S, D) for c in range(8)], axis=0)
    return out.astype(np.float32)


# version 23
# speedup vs baseline: 1.2180x; 1.1996x over previous
import contextlib
import math

import numpy as np
import ml_dtypes

import concourse.bass as bass
import concourse.mybir as mybir
from concourse.bass_utils import run_bass_kernel_spmd

F32 = mybir.dt.float32
BF16 = mybir.dt.bfloat16
AF = mybir.ActivationFunctionType
ALU = mybir.AluOpType
AX = mybir.AxisListType

MAGIC = 12582912.0
RS2 = 0.7071067811865476
EPS_MP = 1e-4
SQS = 45.254833995939045

B_PER_CORE = 2
S = 2048
T = B_PER_CORE * S
NT = T // 128
NT_B = S // 128
D = 1024
EC = D // 128
I_FFN = 2816
IC = I_FFN // 128
NH = 16
HD = 64

ENGINES = ("sync", "scalar", "vector", "tensor", "gpsimd")
DMA_QUEUES = ("sync", "scalar", "gpsimd")
CONST_VALS = [2.0, 0.5, -0.5, -2.0, -4.0, MAGIC, -MAGIC]



class Buf:
    __slots__ = ("space", "name", "rows", "cols", "dtype", "kind", "h",
                 "wrecs", "rrecs")

    def __init__(self, space, name, rows, cols, dtype, kind=None):
        self.space, self.name = space, name
        self.rows, self.cols, self.dtype, self.kind = rows, cols, dtype, kind
        self.h = None
        self.wrecs = []
        self.rrecs = []

    def __getitem__(self, idx):
        rs, cs = idx
        if isinstance(rs, slice):
            r0, r1, _ = rs.indices(self.rows)
        else:
            r0, r1 = rs, rs + 1
        if isinstance(cs, slice):
            c0, c1, _ = cs.indices(self.cols)
        else:
            c0, c1 = cs, cs + 1
        return View(self, r0, r1, c0, c1)

    @property
    def full(self):
        return View(self, 0, self.rows, 0, self.cols)


class View:
    __slots__ = ("buf", "r0", "r1", "c0", "c1")

    def __init__(self, buf, r0, r1, c0, c1):
        self.buf, self.r0, self.r1, self.c0, self.c1 = buf, r0, r1, c0, c1

    @property
    def ap(self):
        return self.buf.h[self.r0:self.r1, self.c0:self.c1]

    def sub(self, r0, r1, c0, c1):
        return View(self.buf, self.r0 + r0, self.r0 + r1, self.c0 + c0, self.c0 + c1)

    def rect(self):
        if self.buf.space == "psum":
            return (0, self.buf.rows, 0, self.buf.cols)
        return (self.r0, self.r1, self.c0, self.c1)

    @property
    def shape(self):
        return (self.r1 - self.r0, self.c1 - self.c0)


def _ovl(a, b):
    return a[0] < b[1] and b[0] < a[1] and a[2] < b[3] and b[2] < a[3]


class Op:
    __slots__ = ("idx", "engine", "fn", "is_dma", "tag", "inc_needed",
                 "inc_count", "wait_edges", "narrow", "minspan", "has_psum")

    def __init__(self, idx, engine, fn, is_dma, tag):
        self.idx, self.engine, self.fn, self.is_dma, self.tag = \
            idx, engine, fn, is_dma, tag
        self.inc_needed = is_dma
        self.inc_count = None
        self.wait_edges = []


class Prog:
    def __init__(self):
        self.ops = []
        self.bufs = []
        self.n_preload = 0

    def sbuf(self, name, rows, cols, dtype):
        b = Buf("sbuf", name, rows, cols, dtype)
        self.bufs.append(b)
        return b

    def psum(self, name, rows, cols):
        b = Buf("psum", name, rows, cols, F32)
        self.bufs.append(b)
        return b

    def dram(self, name, rows, cols, dtype, kind="Internal"):
        b = Buf("dram", name, rows, cols, dtype, kind)
        self.bufs.append(b)
        return b

    def hint(self, engine, views, tag="hint"):
        o = Op(len(self.ops), engine, None, False, tag)
        o.has_psum = False
        o.minspan = 0
        o.narrow = False
        edges = set()
        for v in views:
            r = v.rect()
            for rec in v.buf.wrecs:
                if _ovl(r, rec):
                    edges.add(rec[4])
        for pidx in edges:
            p = self.ops[pidx]
            if p.engine == engine and not p.is_dma:
                continue
            o.wait_edges.append(p)
            p.inc_needed = True
        self.ops.append(o)
        return o

    def op(self, engine, fn, reads=(), writes=(), tag="", is_dma=False):
        o = Op(len(self.ops), engine, fn, is_dma, tag)
        allv = list(reads) + list(writes)
        o.has_psum = any(v.buf.space == "psum" for v in allv)
        o.minspan = min((v.c1 - v.c0 for v in allv), default=0)
        if is_dma:
            spans = [v.r1 - v.r0 for v in allv if v.buf.space == "sbuf"]
            o.narrow = bool(spans) and min(spans) < 128
        else:
            o.narrow = False
        edges = set()
        for v in reads:
            r = v.rect()
            for rec in v.buf.wrecs:
                if _ovl(r, rec):
                    edges.add(rec[4])
        for v in writes:
            r = v.rect()
            for rec in v.buf.wrecs:
                if _ovl(r, rec):
                    edges.add(rec[4])
            for rec in v.buf.rrecs:
                if _ovl(r, rec):
                    edges.add(rec[4])
        for pidx in edges:
            p = self.ops[pidx]
            if (p.engine == "tensor" and engine == "tensor"
                    and not p.is_dma and not is_dma):
                continue
            o.wait_edges.append(p)
            p.inc_needed = True
        for v in writes:
            r = v.rect()
            def cov(rec, _r=r):
                return (_r[0] <= rec[0] and rec[1] <= _r[1]
                        and _r[2] <= rec[2] and rec[3] <= _r[3])
            v.buf.wrecs = [rec for rec in v.buf.wrecs if not cov(rec)]
            v.buf.rrecs = [rec for rec in v.buf.rrecs if not cov(rec)]
            v.buf.wrecs.append((*r, o.idx))
        for v in reads:
            v.buf.rrecs.append((*v.rect(), o.idx))
        self.ops.append(o)
        return o

    def build(self):
        nc = bass.Bass("TRN2", target_bir_lowering=False)
        stack = contextlib.ExitStack()
        for b in self.bufs:
            if b.space == "dram":
                b.h = nc.dram_tensor(b.name, [b.rows, b.cols], b.dtype,
                                     kind=b.kind).ap()
        for b in self.bufs:
            if b.space == "sbuf":
                b.h = stack.enter_context(
                    nc.sbuf_tensor(b.name, [b.rows, b.cols], b.dtype)).ap()
            elif b.space == "psum":
                b.h = stack.enter_context(
                    nc.psum_tensor(b.name, [b.rows, b.cols], b.dtype)).ap()
        cb = next(b for b in self.bufs if b.name == "consts_sb")
        for i, val in enumerate(CONST_VALS):
            nc.const_aps.register(F32, val, cb.h[:, i:i + 1])

        RSEM_Q = {"sync": 40, "scalar": 8, "gpsimd": 32}
        sem_counts = {}
        queue_counts = {}
        for o in self.ops:
            if o.is_dma:
                i = queue_counts.get(o.engine, 0)
                queue_counts[o.engine] = i + 1
                o.inc_count = i
            elif o.inc_needed:
                key = "e_" + o.engine
                sem_counts[key] = sem_counts.get(key, 0) + 1
                o.inc_count = sem_counts[key]
        sems = {k: stack.enter_context(nc.semaphore(k)) for k in sem_counts}
        dpool = {}
        for q in queue_counts:
            dpool[q] = [stack.enter_context(nc.semaphore(f"d_{q}{r}"))
                        for r in range(RSEM_Q[q])]

        per_engine = {e: [] for e in ENGINES}
        for o in self.ops:
            per_engine[o.engine].append(o)
        n_pre = self.n_preload

        block = stack.enter_context(nc.Block())

        def dma_key_val(p, consumer=True):
            R = RSEM_Q[p.engine]
            slot = p.inc_count % R
            rnd = p.inc_count // R
            return ("dq", p.engine, slot), 16 * (rnd + (1 if consumer else 0))

        def mk(ename):
            def run(eng):
                wm = {}
                if n_pre and ename != "sync" and "sync" in dpool:
                    for i in range(n_pre):
                        eng.wait_ge(dpool["sync"][i % RSEM_Q["sync"]], 16)
                        wm[("dq", "sync", i % RSEM_Q["sync"])] = 16
                for o in per_engine[ename]:
                    for p in o.wait_edges:
                        if p.is_dma:
                            key, val = dma_key_val(p, consumer=True)
                            sem = dpool[p.engine][key[2]]
                        else:
                            key = "e_" + p.engine
                            val = p.inc_count
                            sem = sems[key]
                        if wm.get(key, 0) >= val:
                            continue
                        eng.wait_ge(sem, val)
                        wm[key] = val
                    if o.is_dma:
                        key, val = dma_key_val(o, consumer=False)
                        if val > 0 and wm.get(key, 0) < val:
                            eng.wait_ge(dpool[o.engine][key[2]], val)
                            wm[key] = val
                    if o.fn is None:
                        continue
                    instr = o.fn(eng)
                    if o.is_dma:
                        instr.then_inc(dpool[o.engine][o.inc_count % RSEM_Q[o.engine]], 16)
                    elif o.inc_needed:
                        instr.then_inc(sems["e_" + o.engine], 1)
                if ename == "gpsimd":
                    for q, cnt in queue_counts.items():
                        R = RSEM_Q[q]
                        for r in range(R):
                            n_r = (cnt - 1 - r) // R + 1 if r < cnt else 0
                            if n_r > 0:
                                eng.wait_ge(dpool[q][r], 16 * n_r)
            return run

        for ename in ENGINES:
            getattr(block, ename)(mk(ename))
        return nc, stack



def dma(P, queue, dst, src, tag="", dst_fn=None, src_fn=None,
        track_writes=None, track_reads=None):
    def fn(e, _d=dst, _s=src):
        dap = dst_fn(_d.ap) if dst_fn else _d.ap
        sap = src_fn(_s.ap) if src_fn else _s.ap
        return e.dma_start(out=dap, in_=sap)
    return P.op(queue, fn, track_reads or [src], track_writes or [dst], tag,
                is_dma=True)


def dmat(P, queue, dst, src, tag=""):
    def fn(e, _d=dst, _s=src):
        return e.dma_start_transpose(_d.ap, _s.ap)
    return P.op(queue, fn, [src], [dst], tag, is_dma=True)


def dmat_blocks(P, queue, dst, src, tag=""):
    R = src.r1 - src.r0
    def fn(e, _d=dst, _s=src, _R=R):
        dap = _d.ap.rearrange("p (k r) -> p k r", r=_R)
        return e.dma_start_transpose(dap, _s.ap)
    return P.op(queue, fn, [src], [dst], tag, is_dma=True)


def ts(P, dst, src, s1, s2=None, op0=ALU.mult, op1=None, accum=None, tag="ts"):
    reads = [src] + [x for x in (s1, s2) if isinstance(x, View)]
    writes = [dst] + ([accum] if accum is not None else [])

    def fn(e, _d=dst, _s=src, _1=s1, _2=s2, _a=accum, _o0=op0, _o1=op1):
        k1 = _1.ap if isinstance(_1, View) else _1
        k2 = _2.ap if isinstance(_2, View) else _2
        kw = {"accum_out": _a.ap} if _a is not None else {}
        if _o1 is None:
            return e.tensor_scalar(_d.ap, _s.ap, k1, None, _o0, **kw)
        return e.tensor_scalar(_d.ap, _s.ap, k1, k2, _o0, _o1, **kw)

    return P.op("vector", fn, reads, writes, tag)


def tt(P, dst, a, b, op=ALU.mult, eng="vector", tag="tt",
       dst_fn=None, a_fn=None, b_fn=None):
    def fn(e, _d=dst, _a=a, _b=b, _op=op):
        dap = dst_fn(_d.ap) if dst_fn else _d.ap
        aap = a_fn(_a.ap) if a_fn else _a.ap
        bap = b_fn(_b.ap) if b_fn else _b.ap
        return e.tensor_tensor(dap, aap, bap, _op)
    return P.op(eng, fn, [a, b], [dst], tag)


def act(P, dst, src, func, bias=None, scale=None, accum=None, tag="act"):
    reads = [src] + ([bias] if isinstance(bias, View) else [])
    writes = [dst] + ([accum] if accum is not None else [])

    def fn(e, _d=dst, _s=src, _b=bias, _sc=scale, _a=accum, _f=func):
        kw = {}
        if _b is not None:
            kw["bias"] = _b.ap if isinstance(_b, View) else _b
        if _sc is not None:
            kw["scale"] = _sc
        if _a is not None:
            kw["accum_out"] = _a.ap
        return e.activation(_d.ap, _s.ap, _f, **kw)

    return P.op("scalar", fn, reads, writes, tag)


def red(P, dst, src, op=ALU.max, absval=False, negate=False, src_fn=None,
        tag="red"):
    def fn(e, _d=dst, _s=src, _op=op):
        sap = src_fn(_s.ap) if src_fn else _s.ap
        kw = {}
        if absval:
            kw["apply_absolute_value"] = True
        if negate:
            kw["negate"] = True
        return e.tensor_reduce(_d.ap, sap, AX.X, _op, **kw)
    return P.op("vector", fn, [src], [dst], tag)


def copy(P, dst, src, eng="vector", tag="copy"):
    if eng == "scalar":
        def fn(e, _d=dst, _s=src):
            return e.activation(_d.ap, _s.ap, AF.Copy)
    else:
        def fn(e, _d=dst, _s=src):
            return e.tensor_copy(_d.ap, _s.ap)
    return P.op(eng, fn, [src], [dst], tag)


def recip(P, dst, src, tag="recip"):
    return P.op("vector", lambda e, _d=dst, _s=src: e.reciprocal(_d.ap, _s.ap),
                [src], [dst], tag)


def mm(P, out, lhsT, rhs, start, stop, tag="mm"):
    def fn(e, _o=out, _l=lhsT, _r=rhs, _st=start, _sp=stop):
        return e.matmul(_o.ap, _l.ap, _r.ap, start=_st, stop=_sp)
    return P.op("tensor", fn, [lhsT, rhs], [out], tag)


def petr(P, out, src, ident, tag="ptr"):
    def fn(e, _o=out, _s=src, _i=ident):
        return e.transpose(_o.ap, _s.ap, _i.ap)
    return P.op("tensor", fn, [src, ident], [out], tag)


def memset(P, dst, val, eng="vector", tag="memset"):
    return P.op(eng, lambda e, _d=dst, _v=val: e.memset(_d.ap, _v), [], [dst], tag)


def gbm(P, dst, src, small, gsize, op=ALU.mult, eng="vector", tag="gbm"):
    ng = small.shape[1]

    def fn(e, _d=dst, _s=src, _m=small, _op=op, _g=gsize, _ng=ng):
        dap = _d.ap.rearrange("p (g b) -> p g b", b=_g)
        sap = _s.ap.rearrange("p (g b) -> p g b", b=_g)
        map_ = _m.ap.unsqueeze(2).broadcast_to((_m.shape[0], _ng, _g))
        return e.tensor_tensor(dap, sap, map_, _op)

    return P.op(eng, fn, [src, small], [dst], tag)



def fp4_quant_multi(P, jobs, tag="q", act_rounds=False):
    def each(fn):
        for i, (ob, x, t) in enumerate(jobs):
            fn(i, ob, x, t)
    each(lambda i, ob, x, t: red(P, t["bm"], x, ALU.max, absval=True,
         src_fn=lambda ap: ap.rearrange("p (g b) -> p g b", b=16), tag=f"{tag}.bm{i}"))
    each(lambda i, ob, x, t: ts(P, t["bm"], t["bm"], 1.0 / 6.0, 1e-12,
         ALU.mult, ALU.max, tag=f"{tag}.s{i}"))
    each(lambda i, ob, x, t: recip(P, t["rs"], t["bm"], tag=f"{tag}.rs{i}"))
    each(lambda i, ob, x, t: gbm(P, t["w"], x, t["rs"], 16, tag=f"{tag}.w{i}"))
    each(lambda i, ob, x, t: act(P, t["A"], t["w"], AF.Abs, scale=2.0, tag=f"{tag}.a2{i}"))
    each(lambda i, ob, x, t: ts(P, t["A"], t["A"], MAGIC, -MAGIC, ALU.add, ALU.add, tag=f"{tag}.r1{i}"))
    each(lambda i, ob, x, t: ts(P, t["A"], t["A"], 0.5, 2.0, ALU.mult, ALU.min, tag=f"{tag}.t1{i}"))
    each(lambda i, ob, x, t: act(P, t["B"], t["w"], AF.Abs, tag=f"{tag}.a1{i}"))
    if act_rounds:
        each(lambda i, ob, x, t: act(P, t["B"], t["B"], AF.Identity, bias=MAGIC, tag=f"{tag}.r2a{i}"))
        each(lambda i, ob, x, t: act(P, t["B"], t["B"], AF.Identity, bias=-MAGIC, tag=f"{tag}.r2b{i}"))
    else:
        each(lambda i, ob, x, t: ts(P, t["B"], t["B"], MAGIC, -MAGIC, ALU.add, ALU.add, tag=f"{tag}.r2{i}"))
    each(lambda i, ob, x, t: ts(P, t["B"], t["B"], -2.0, 0.0, ALU.add, ALU.max, tag=f"{tag}.t2{i}"))
    each(lambda i, ob, x, t: ts(P, t["B"], t["B"], 2.0, None, ALU.min, tag=f"{tag}.t2b{i}"))
    each(lambda i, ob, x, t: act(P, t["C"], t["w"], AF.Abs, scale=0.5, tag=f"{tag}.a5{i}"))
    if act_rounds:
        each(lambda i, ob, x, t: act(P, t["C"], t["C"], AF.Identity, bias=MAGIC, tag=f"{tag}.r3a{i}"))
        each(lambda i, ob, x, t: act(P, t["C"], t["C"], AF.Identity, bias=-MAGIC, tag=f"{tag}.r3b{i}"))
    else:
        each(lambda i, ob, x, t: ts(P, t["C"], t["C"], MAGIC, -MAGIC, ALU.add, ALU.add, tag=f"{tag}.r3{i}"))
    each(lambda i, ob, x, t: ts(P, t["C"], t["C"], -2.0, 0.0, ALU.add, ALU.max, tag=f"{tag}.t4{i}"))
    each(lambda i, ob, x, t: ts(P, t["C"], t["C"], 2.0, None, ALU.mult, tag=f"{tag}.t4b{i}"))
    each(lambda i, ob, x, t: act(P, t["D"], t["w"], AF.Sign, tag=f"{tag}.sg{i}"))
    each(lambda i, ob, x, t: tt(P, t["A"], t["A"], t["B"], ALU.add, eng="gpsimd", tag=f"{tag}.u1{i}"))
    each(lambda i, ob, x, t: tt(P, t["A"], t["A"], t["C"], ALU.add, eng="gpsimd", tag=f"{tag}.u2{i}"))
    each(lambda i, ob, x, t: tt(P, t["A"], t["A"], t["D"], ALU.mult, eng="gpsimd", tag=f"{tag}.sx{i}"))
    each(lambda i, ob, x, t: gbm(P, ob, t["A"], t["bm"], 16, tag=f"{tag}.o{i}"))


def fp4_quant(P, out_bf, x, tmp, tag="q"):
    fp4_quant_multi(P, [(out_bf, x, tmp)], tag=tag)



def build_prog(debug_taps=()):
    P = Prog()

    def tap(nm):
        return "ExternalOutput" if nm in debug_taps else "Internal"

    io = {}
    io["x_in"] = P.dram("x_in", T, D, F32, kind="ExternalInput")
    io["gain"] = P.dram("gain", 1, 1, F32, kind="ExternalInput")
    for nm in ("w_q", "w_k", "w_v", "w_o"):
        io[nm] = P.dram(nm, D, D, F32, kind="ExternalInput")
    io["w_gate"] = P.dram("w_gate", I_FFN, D, F32, kind="ExternalInput")
    io["w_up"] = P.dram("w_up", I_FFN, D, F32, kind="ExternalInput")
    io["w_down"] = P.dram("w_down", D, I_FFN, F32, kind="ExternalInput")
    for nm in ("WqT", "WkT", "WvT", "WoT", "WfT"):
        io[nm] = P.dram(nm, D, D, F32, kind="ExternalInput")
    io["biases"] = P.dram("biases", 8, 512, F32, kind="ExternalInput")
    io["bo_row"] = P.dram("bo_row", 1, D, F32, kind="ExternalInput")
    io["bv_row"] = P.dram("bv_row", 1, D, F32, kind="ExternalInput")
    io["identf"] = P.dram("identf", 128, 128, F32, kind="ExternalInput")
    io["tri01"] = P.dram("tri01", 128, 128, BF16, kind="ExternalInput")
    io["consts"] = P.dram("consts", 128, 8, F32, kind="ExternalInput")
    io["out"] = P.dram("out", T, D, F32, kind="ExternalOutput")

    sc = {}
    sc["xn_tm"] = P.dram("xn_tm", T, D, F32, kind=tap("xn_tm"))
    sc["xn_bf"] = P.dram("xn_bf", T, D, BF16)
    sc["xq"] = P.dram("xq", T, D, BF16, kind=tap("xq"))
    sc["s_tm"] = P.dram("s_tm", T, D, F32, kind=tap("s_tm"))
    sc["x1"] = P.dram("x1", T, D, F32, kind=tap("x1"))
    sc["gate_s"] = P.dram("gate_s", T, I_FFN, F32, kind=tap("gate_s"))
    sc["attn_sc"] = P.dram("attn_sc", B_PER_CORE * NH * 64, 64, BF16)
    sc["hq"] = P.dram("hq", T, I_FFN, BF16, kind=tap("hq"))
    sc["xn1_bf"] = P.dram("xn1_bf", T, D, BF16, kind=tap("xn1_bf"))
    sc["qf"] = P.dram("qf", B_PER_CORE * D, S, BF16, kind=tap("qf"))
    sc["kf"] = P.dram("kf", B_PER_CORE * D, S, BF16, kind=tap("kf"))
    sc["vsc"] = P.dram("vsc", B_PER_CORE * NH * S, HD, BF16, kind=tap("vsc"))
    sc["g1"] = P.dram("g1", T, D, F32, kind=tap("g1"))
    sc["obsc"] = P.dram("obsc", B_PER_CORE * D, S, BF16, kind=tap("obsc"))
    for nm in ("wq_n", "wk_n", "wv_n", "wo_n"):
        sc[nm] = P.dram(nm, D, D, BF16)
    sc["wg_n"] = P.dram("wg_n", I_FFN, D, BF16)
    sc["wu_n"] = P.dram("wu_n", I_FFN, D, BF16)
    sc["wd_n"] = P.dram("wd_n", D, I_FFN, BF16)

    sb = {}
    sb["consts_sb"] = P.sbuf("consts_sb", 128, 8, F32)
    sb["identf"] = P.sbuf("identf_sb", 128, 128, F32)
    sb["tri01"] = P.sbuf("tri01_sb", 128, 128, BF16)
    sb["gain_bc"] = P.sbuf("gain_bc", 128, 1, F32)
    sb["ones_row"] = P.sbuf("ones_row", 1, 128, F32)
    sb["ones_col"] = P.sbuf("ones_col", 128, 1, F32)
    sb["bias_pp"] = P.sbuf("bias_pp", 128, 32, F32)
    sb["btmp"] = P.sbuf("btmp", 8, 512, F32)
    sb["grow"] = P.sbuf("grow", 1, D, F32)
    sb["bo_bc"] = P.sbuf("bo_bc", 128, D, F32)
    sb["bv_bc"] = P.sbuf("bv_bc", 128, D, F32)
    sb["attnT"] = P.sbuf("attnT", 128, B_PER_CORE * NH * HD, BF16)
    sb["wpool"] = P.sbuf("wpool", 128, 22528, BF16)
    sb["apool"] = P.sbuf("apool", 128, 18560, BF16)
    sb["fpool"] = P.sbuf("fpool", 128, 20480, F32)
    sb["bpool"] = P.sbuf("bpool", 128, 14336, BF16)

    pb = [P.psum(f"pb{i}", 128, 512) for i in range(8)]

    dma(P, "sync", sb["consts_sb"].full, io["consts"].full, tag="pre.c")
    dma(P, "sync", sb["identf"].full, io["identf"].full, tag="pre.i")
    dma(P, "sync", sb["tri01"].full, io["tri01"].full, tag="pre.t")
    P.n_preload = 3

    return dict(P=P, io=io, sc=sc, sb=sb, pb=pb, side=[])


class Alloc:
    def __init__(self, buf):
        self.buf = buf
        self.off = 0

    def take(self, cols, rows=None):
        r = self.buf.rows if rows is None else rows
        v = View(self.buf, 0, r, self.off, self.off + cols)
        self.off += cols
        assert self.off <= self.buf.cols, (self.buf.name, self.off)
        return v


def setup_misc(ctx):
    P, io, sb, pb = ctx["P"], ctx["io"], ctx["sb"], ctx["pb"]
    memset(P, sb["ones_row"].full, 1.0, tag="ones_r")
    memset(P, sb["ones_col"].full, 1.0, tag="ones_c")
    gtmp = sb["btmp"][0:1, 0:1]
    dma(P, "sync", gtmp, io["gain"].full, tag="gain.ld")
    mm(P, pb[0][0:128, 0:1], sb["ones_row"].full, gtmp, True, True, tag="gain.bc")
    copy(P, sb["gain_bc"].full, pb[0][0:128, 0:1], eng="scalar", tag="gain.ev")
    dma(P, "sync", sb["btmp"].full, io["biases"].full, tag="bias.ld")
    for i in range(4):
        pv = pb[1][0:128, 0:8]
        petr(P, pv, sb["btmp"][0:8, 128 * i:128 * (i + 1)], sb["identf"][0:8, 0:8],
             tag=f"bias.tr{i}")
        dstv = sb["bias_pp"][0:128, 8 * i:8 * i + 8]
        if i == 0:
            ts(P, dstv, pv, 0.125, None, ALU.mult, tag="bias.evq")
        else:
            copy(P, dstv, pv, eng="scalar", tag=f"bias.ev{i}")
    dma(P, "sync", sb["grow"].full, io["bo_row"].full, tag="bo.ld")
    for h in range(2):
        pv = pb[2][0:128, 0:512]
        mm(P, pv, sb["ones_row"].full, sb["grow"][0:1, 512 * h:512 * (h + 1)],
           True, True, tag="bo.bc")
        copy(P, sb["bo_bc"][0:128, 512 * h:512 * (h + 1)], pv, eng="scalar",
             tag="bo.ev")
    dma(P, "sync", sb["grow"].full, io["bv_row"].full, tag="bv.ld")
    for h in range(2):
        pv = pb[3][0:128, 0:512]
        mm(P, pv, sb["ones_row"].full, sb["grow"][0:1, 512 * h:512 * (h + 1)],
           True, True, tag="bv.bc")
        copy(P, sb["bv_bc"][0:128, 512 * h:512 * (h + 1)], pv, eng="scalar",
             tag="bv.ev")


def prep_mp_weights(ctx, names):
    P, io, sc, sb = ctx["P"], ctx["io"], ctx["sc"], ctx["sb"]
    f = Alloc(sb["fpool"])
    wt = [f.take(D) for _ in range(2)]
    dump = f.take(D)
    ssb = f.take(2)
    fcb = f.take(2)
    wbf_buf = Alloc(sb["bpool"])
    wb2 = [wbf_buf.take(D) for _ in range(2)]
    for nm in names:
        src = io[nm]
        dst = sc[nm.replace("w_", "w") + "_n"]
        for c in range(EC):
            b = c % 2
            dma(P, "sync", wt[b], src[128 * c:128 * (c + 1), 0:D], tag=f"{nm}.ld{c}")
            sv = ssb.sub(0, 128, b, b + 1)
            fv = fcb.sub(0, 128, b, b + 1)
            act(P, dump, wt[b], AF.Square, accum=sv, tag=f"{nm}.sq{c}")
            act(P, fv, sv, AF.Sqrt, tag=f"{nm}.rt{c}")
            ts(P, fv, fv, 32.0 * EPS_MP, None, ALU.add, tag=f"{nm}.ep{c}")
            recip(P, fv, fv, tag=f"{nm}.rc{c}")
            tt(P, fv, fv, sb["gain_bc"].full, ALU.mult, tag=f"{nm}.g{c}")
            ts(P, wb2[b], wt[b], fv, None, ALU.mult, tag=f"{nm}.sc{c}")
            dma(P, "gpsimd", dst[128 * c:128 * (c + 1), 0:D], wb2[b], tag=f"{nm}.st{c}")


def prep_fp4_weight(ctx, src_name, dst_name, nrows, pools):
    P, io, sc, sb = ctx["P"], ctx["io"], ctx["sc"], ctx["sb"]
    ncol = io[src_name].cols
    nh = 1 if ncol <= 2048 else 2
    hw = ncol // nh
    f = Alloc(sb["fpool"]); f.off = pools[0]
    wt = [f.take(ncol) for _ in range(2)]
    ntmp = 2 if nh == 1 else 1
    tmp = [dict(w=None, A=f.take(hw), B=f.take(hw), C=f.take(hw),
                D=f.take(hw), bm=f.take(hw // 16), rs=f.take(hw // 16))
           for _ in range(ntmp)]
    bb = Alloc(sb["bpool"]); bb.off = pools[1]
    ob = [bb.take(ncol) for _ in range(2)]
    nr = nrows // 128

    def mk_pair(c0):
        def emit():
            cs = [c for c in (c0, c0 + 1) if c < nr]
            jobs = []
            for c in cs:
                b = c % 2
                dma(P, "sync", wt[b], io[src_name][128 * c:128 * (c + 1), 0:ncol],
                    tag=f"{src_name}.ld{c}")
                for hf in range(nh):
                    t = dict(tmp[b % ntmp])
                    t["w"] = wt[b].sub(0, 128, hw * hf, hw * (hf + 1))
                    jobs.append((ob[b].sub(0, 128, hw * hf, hw * (hf + 1)),
                                 wt[b].sub(0, 128, hw * hf, hw * (hf + 1)), t))
            if ntmp >= len(jobs):
                fp4_quant_multi(P, jobs, tag=f"{src_name}.q{c0}")
            else:
                for ji, job in enumerate(jobs):
                    fp4_quant_multi(P, [job], tag=f"{src_name}.q{c0}.{ji}")
        
            for c in cs:
                b = c % 2
                dma(P, "gpsimd", sc[dst_name][128 * c:128 * (c + 1), 0:ncol], ob[b],
                    tag=f"{src_name}.st{c}")
        return emit

    step = 2 if nh == 1 else 1
    if step == 2:
        for c0 in range(0, nr, 2):
            ctx["side"].append(mk_pair(c0))
    else:
        def mk_one(c):
            def emit():
                b = c % 2
                dma(P, "sync", wt[b], io[src_name][128 * c:128 * (c + 1), 0:ncol],
                    tag=f"{src_name}.ld{c}")
                for hf in range(nh):
                    t = dict(tmp[0])
                    t["w"] = wt[b].sub(0, 128, hw * hf, hw * (hf + 1))
                    fp4_quant_multi(
                        P, [(ob[b].sub(0, 128, hw * hf, hw * (hf + 1)),
                             wt[b].sub(0, 128, hw * hf, hw * (hf + 1)), t)],
                        tag=f"{src_name}.q{c}.{hf}")
                dma(P, "gpsimd", sc[dst_name][128 * c:128 * (c + 1), 0:ncol], ob[b],
                    tag=f"{src_name}.st{c}")
            return emit
        for c in range(nr):
            ctx["side"].append(mk_one(c))


def load_wT(ctx, dst, src, tag):
    P = ctx["P"]
    dmat_blocks(P, "sync", dst, src, tag=f"{tag}.tr")


def load_l1_weight(ctx, dst, src_name, scale=None, sign=False, tag=""):
    P, io, sb = ctx["P"], ctx["io"], ctx["sb"]
    f = Alloc(sb["fpool"])
    wt = [f.take(D) for _ in range(2)]
    for c in range(EC):
        b = c % 2
        dma(P, "sync", wt[b], io[src_name][128 * c:128 * (c + 1), 0:D],
            tag=f"{tag}.ld{c}")
        dv = dst.sub(0, 128, D * c, D * (c + 1))
        if sign:
            act(P, dv, wt[b], AF.Sign, tag=f"{tag}.sg{c}")
        elif scale is not None:
            ts(P, dv, wt[b], scale, None, ALU.mult, tag=f"{tag}.sc{c}")
        else:
            copy(P, dv, wt[b], eng="vector", tag=f"{tag}.cp{c}")


def ln_tile2(P, x2, sqbuf, f, tag):
    m2 = f.take(2)
    ss = f.take(2)
    var = f.take(2)
    g3 = lambda ap: ap.rearrange("p (g b) -> p g b", b=D)
    red(P, m2, x2, ALU.add, src_fn=g3, tag=tag + ".mu")
    tt(P, sqbuf, x2, x2, ALU.mult, tag=tag + ".sq")
    red(P, ss, sqbuf, ALU.add, src_fn=g3, tag=tag + ".ss")
    ts(P, m2, m2, 1.0 / D, None, ALU.mult, tag=tag + ".m")
    tt(P, var, m2, m2, ALU.mult, tag=tag + ".m2")
    ts(P, ss, ss, 1.0 / D, 1e-5, ALU.mult, ALU.add, tag=tag + ".v")
    tt(P, ss, ss, var, ALU.subtract, tag=tag + ".ve")
    act(P, var, ss, AF.Sqrt, tag=tag + ".sd")
    recip(P, var, var, tag=tag + ".rs")
    gbm(P, x2, x2, m2, D, op=ALU.subtract, tag=tag + ".apm")
    gbm(P, x2, x2, var, D, op=ALU.mult, tag=tag + ".apr")


def ln_tile(P, xn, xt, f, tag):
    mean = f.take(1)
    ssum = f.take(1)
    var = f.take(1)
    dump = f.take(D)
    red(P, mean, xt, ALU.add, tag=tag + ".mu")
    act(P, dump, xt, AF.Square, accum=ssum, tag=tag + ".sq")
    ts(P, mean, mean, 1.0 / D, None, ALU.mult, tag=tag + ".m")
    tt(P, var, mean, mean, ALU.mult, tag=tag + ".m2")
    ts(P, ssum, ssum, 1.0 / D, var, ALU.mult, ALU.subtract, tag=tag + ".v")
    ts(P, ssum, ssum, 1e-5, None, ALU.add, tag=tag + ".ve")
    act(P, var, ssum, AF.Sqrt, tag=tag + ".sd")
    recip(P, var, var, tag=tag + ".rs")
    ts(P, xn, xt, mean, var, ALU.subtract, ALU.mult, tag=tag + ".ap")
    return xn


def mpnorm(P, out_bf, qf32, f, extra_scale, tag):
    sq = f.take(D)
    ss16 = f.take(NH)
    act(P, sq, qf32, AF.Square, tag=tag + ".sq")
    red(P, ss16, sq, ALU.add,
        src_fn=lambda ap: ap.rearrange("p (g b) -> p g b", b=HD), tag=tag + ".ss")
    act(P, ss16, ss16, AF.Sqrt, tag=tag + ".rt")
    ts(P, ss16, ss16, 0.125 * extra_scale, EPS_MP * extra_scale,
       ALU.mult, ALU.add, tag=tag + ".d")
    recip(P, ss16, ss16, tag=tag + ".rc")
    gbm(P, out_bf, qf32, ss16, HD, tag=tag + ".ap")


def emit_p0(ctx, wqT, wkT):
    P, io, sc, sb, pb = ctx["P"], ctx["io"], ctx["sc"], ctx["sb"], ctx["pb"]
    f = Alloc(sb["fpool"])
    xt2 = [f.take(2 * D) for _ in range(2)]
    qtmp2 = dict(w=f.take(2 * D), A=f.take(2 * D), B=f.take(2 * D),
                 C=f.take(2 * D), D=f.take(2 * D),
                 bm=f.take(2 * D // 16), rs=f.take(2 * D // 16))
    qfd = [f.take(D) for _ in range(2)]
    smalls = Alloc(sb["fpool"])
    smalls.off = f.off
    mpn_off = f.off + 64
    bb = Alloc(sb["bpool"])
    xnb2 = bb.take(2 * D)
    xqb2 = bb.take(2 * D)
    xnT = [bb.take(D) for _ in range(2)]
    qhd = [bb.take(D) for _ in range(2)]
    khd = [bb.take(D) for _ in range(2)]

    lag = 4
    for i in range(NT + lag):
        if i < NT and i % 2 == 0:
            i2 = i // 2
            b = i2 % 2
            x2 = xt2[b]
            dma(P, "sync", x2, io["x_in"][256 * i2:256 * (i2 + 1), 0:D],
                tag=f"p0.x{i2}",
                src_fn=lambda ap: ap.rearrange("(a p) c -> p a c", p=128))
            fs = Alloc(sb["fpool"]); fs.off = smalls.off
            ln_tile2(P, x2, qtmp2["A"], fs, tag=f"p0.ln{i2}")
            dma(P, "gpsimd", sc["xn_tm"][256 * i2:256 * (i2 + 1), 0:D], x2,
                tag=f"p0.sxn{i2}",
                dst_fn=lambda ap: ap.rearrange("(a p) c -> p a c", p=128))
            copy(P, xnb2, x2, eng="scalar", tag=f"p0.cb{i2}")
            dma(P, "gpsimd", sc["xn_bf"][256 * i2:256 * (i2 + 1), 0:D], xnb2,
                tag=f"p0.sxb{i2}",
                dst_fn=lambda ap: ap.rearrange("(a p) c -> p a c", p=128))
            fp4_quant(P, xqb2, x2, qtmp2, tag=f"p0.q{i2}")
            dma(P, "gpsimd", sc["xq"][256 * i2:256 * (i2 + 1), 0:D], xqb2,
                tag=f"p0.sxq{i2}",
                dst_fn=lambda ap: ap.rearrange("(a p) c -> p a c", p=128))
        j = i - lag
        j = i - lag
        if j < 0:
            continue
        bj = j % 2
        xT = xnT[bj]
        dmat_blocks(P, "sync", xT, sc["xn_bf"][128 * j:128 * (j + 1), 0:D],
                    tag=f"p0.xT{j}")
        bloc = j // NT_B
        tl = j % NT_B
        P.hint("tensor", [xT], tag=f"p0.h{j}")
        qhb = qhd[bj]
        khb = khd[bj]
        qf32 = qfd[bj]
        for (wT, pbs, outb, xsc, nm) in ((wqT, (0, 1), qhb, 1.0, "q"),
                                         (wkT, (2, 3), khb, SQS, "k")):
            for half in range(2):
                pv = pb[pbs[half]][0:128, 0:512]
                for ec in range(EC):
                    mm(P, pv, xT.sub(0, 128, 128 * ec, 128 * (ec + 1)),
                       wT.sub(0, 128, D * ec + 512 * half, D * ec + 512 * (half + 1)),
                       start=(ec == 0), stop=(ec == EC - 1),
                       tag=f"p0.{nm}mm{j}.{half}.{ec}")
            for half in range(2):
                copy(P, qf32.sub(0, 128, 512 * half, 512 * (half + 1)),
                     pb[pbs[half]][0:128, 0:512], eng="scalar",
                     tag=f"p0.{nm}ev{j}.{half}")
            fs = Alloc(sb["fpool"]); fs.off = mpn_off
            mpnorm(P, outb, qf32, fs, xsc, tag=f"p0.{nm}n{j}")
        for n in range(NH):
            bank = 4 + 2 * bloc + (n // 8)
            pv = pb[bank][0:64, 64 * (n % 8):64 * (n % 8) + 64]
            mm(P, pv, View(qhb.buf, 0, 128, qhb.c0 + 64 * n, qhb.c0 + 64 * (n + 1)),
               View(khb.buf, 0, 128, khb.c0 + 64 * n, khb.c0 + 64 * (n + 1)),
               start=(tl == 0 and n % 8 == 0), stop=(tl == NT_B - 1 and n % 8 == 7),
               tag=f"p0.sc{j}.{n}")

    f2 = Alloc(sb["fpool"])
    scf = f2.take(2 * D)
    mx = f2.take(2 * NH)
    pnorm = f2.take(2 * NH)
    for bloc in range(B_PER_CORE):
        for half in range(2):
            copy(P, scf.sub(0, 64, D * bloc + 512 * half, D * bloc + 512 * (half + 1)),
                 pb[4 + 2 * bloc + half][0:64, 0:512], eng="scalar",
                 tag=f"p0.scev{bloc}.{half}")
    sc64 = scf.sub(0, 64, 0, 2 * D)
    mxv = mx.sub(0, 64, 0, 2 * NH)
    red(P, mxv, sc64, ALU.max, negate=True,
        src_fn=lambda ap: ap.rearrange("p (g b) -> p g b", b=HD), tag="p0.smx")
    gbm(P, sc64, sc64, mxv, HD, op=ALU.add, tag="p0.ssub")
    act(P, sc64, sc64, AF.Exp, tag="p0.sexp")
    pn = pnorm.sub(0, 64, 0, 2 * NH)
    red(P, pn, sc64, ALU.add,
        src_fn=lambda ap: ap.rearrange("p (g b) -> p g b", b=HD), tag="p0.ssum")
    recip(P, pn, pn, tag="p0.srec")
    gbm(P, sc64, sc64, pn, HD, tag="p0.spn")
    memset(P, sb["attnT"].full, 0.0, tag="p0.atz")
    atmp = bb.take(64)
    for bn in range(B_PER_CORE * NH):
        pv = pb[bn % 2][0:64, 0:64]
        petr(P, pv, scf.sub(0, 64, 64 * bn, 64 * (bn + 1)),
             sb["identf"][0:64, 0:64], tag=f"p0.ptr{bn}")
        copy(P, View(atmp.buf, 0, 64, atmp.c0, atmp.c0 + 64), pv, eng="scalar",
             tag=f"p0.ptc{bn}")
        dma(P, "gpsimd", ctx["sc"]["attn_sc"][64 * bn:64 * (bn + 1), 0:64],
            View(atmp.buf, 0, 64, atmp.c0, atmp.c0 + 64), tag=f"p0.pts{bn}")
    for pair in range(B_PER_CORE * NH // 2):
        for hh in range(2):
            bn = 2 * pair + hh
            dma(P, "sync",
                sb["attnT"][64 * hh:64 * (hh + 1),
                            128 * pair + 64 * hh:128 * pair + 64 * (hh + 1)],
                ctx["sc"]["attn_sc"][64 * bn:64 * (bn + 1), 0:64],
                tag=f"p0.ptl{bn}")


def sc_x_tile(buf, i):
    return buf[128 * i:128 * (i + 1), 0:buf.cols]


def drain_side(ctx, k=1):
    for _ in range(k):
        if not ctx["side"]:
            return
        ctx["side"].pop(0)()


def emit_p1a(ctx, wvT, woT):
    P, sc, sb, pb = ctx["P"], ctx["sc"], ctx["sb"], ctx["pb"]
    f = Alloc(sb["fpool"])
    xnf = [f.take(D) for _ in range(2)]
    soutd = [f.take(D) for _ in range(2)]
    bb = Alloc(sb["bpool"])
    xnT = [bb.take(D) for _ in range(2)]
    vbd = [bb.take(D) for _ in range(2)]
    obd = [bb.take(D) for _ in range(2)]
    for j in range(NT):
        drain_side(ctx, 1)
        bj = j % 2
        xT = xnT[bj]
        vbf = vbd[bj]
        obf = obd[bj]
        sout = soutd[bj]
        dmat_blocks(P, "sync", xT, sc["xn_bf"][128 * j:128 * (j + 1), 0:D],
                    tag=f"p1a.xT{j}")
        dma(P, "sync", xnf[bj], sc_x_tile(sc["xn_tm"], j), tag=f"p1a.xn{j}")
        P.hint("tensor", [xT, sb["attnT"].full], tag=f"p1a.h{j}")
        bloc = j // NT_B
        for oc in range(EC):
            pv = pb[oc % 2][0:128, 0:128]
            for ec in range(EC):
                mm(P, pv, wvT.sub(0, 128, D * ec + 128 * oc, D * ec + 128 * (oc + 1)),
                   xT.sub(0, 128, 128 * ec, 128 * (ec + 1)),
                   start=(ec == 0), stop=(ec == EC - 1), tag=f"p1a.v{j}.{oc}.{ec}")
            copy(P, vbf.sub(0, 128, 128 * oc, 128 * (oc + 1)), pv, eng="scalar",
                 tag=f"p1a.vev{j}.{oc}")
        for oc in range(EC):
            pv = pb[2 + oc % 2]
            mm(P, pv[0:128, 0:128],
               sb["attnT"][0:128, 1024 * bloc + 128 * oc:1024 * bloc + 128 * (oc + 1)],
               vbf.sub(0, 128, 128 * oc, 128 * (oc + 1)),
               start=True, stop=True, tag=f"p1a.o{j}.{oc}")
            copy(P, obf.sub(0, 128, 128 * oc, 128 * (oc + 1)),
                 pv[0:128, 0:128], eng="scalar", tag=f"p1a.oev{j}.{oc}")
        for half in range(2):
            pv = pb[4 + half][0:128, 0:512]
            for oc in range(EC):
                mm(P, pv, obf.sub(0, 128, 128 * oc, 128 * (oc + 1)),
                   woT.sub(0, 128, D * oc + 512 * half, D * oc + 512 * (half + 1)),
                   start=(oc == 0), stop=(oc == EC - 1), tag=f"p1a.ao{j}.{half}.{oc}")
        for half in range(2):
            sv = sout.sub(0, 128, 512 * half, 512 * (half + 1))
            tt(P, sv, xnf[bj].sub(0, 128, 512 * half, 512 * (half + 1)),
               pb[4 + half][0:128, 0:512], ALU.add, tag=f"p1a.sadd{j}.{half}")
            ts(P, sv, sv, RS2, None, ALU.mult, tag=f"p1a.smul{j}.{half}")
        dma(P, "gpsimd", sc_x_tile(sc["s_tm"], j), sout, tag=f"p1a.ss{j}")


def emit_p1b_gate(ctx, wgT):
    P, sc, sb, pb = ctx["P"], ctx["sc"], ctx["sb"], ctx["pb"]
    f = Alloc(sb["fpool"])
    goutd = [f.take(I_FFN) for _ in range(2)]
    bb = Alloc(sb["bpool"])
    xqT = [bb.take(D) for _ in range(2)]
    nsub = (I_FFN + 511) // 512
    for j in range(NT):
        drain_side(ctx, 1)
        bj = j % 2
        xT = xqT[bj]
        gout = goutd[bj]
        dmat_blocks(P, "sync", xT, sc["xq"][128 * j:128 * (j + 1), 0:D],
                    tag=f"p1g.xT{j}")
        P.hint("tensor", [xT], tag=f"p1g.h{j}")
        for si in range(nsub):
            i0 = 512 * si
            nw = min(512, I_FFN - i0)
            pv = pb[si % 6][0:128, 0:nw]
            for ec in range(EC):
                mm(P, pv, xT.sub(0, 128, 128 * ec, 128 * (ec + 1)),
                   wgT.sub(0, 128, I_FFN * ec + i0, I_FFN * ec + i0 + nw),
                   start=(ec == 0), stop=(ec == EC - 1), tag=f"p1g.mm{j}.{si}.{ec}")
            act(P, gout.sub(0, 128, i0, i0 + nw), pv, AF.Silu,
                tag=f"p1g.si{j}.{si}")
        dma(P, "gpsimd", sc_x_tile(sc["gate_s"], j), gout, tag=f"p1g.st{j}")


def emit_p1b_up(ctx, wuT):
    P, sc, sb, pb = ctx["P"], ctx["sc"], ctx["sb"], ctx["pb"]
    f = Alloc(sb["fpool"])
    h = [f.take(I_FFN) for _ in range(2)]
    HB = I_FFN // 2
    qtmpd = [dict(w=None, A=f.take(HB), B=f.take(HB), C=f.take(HB),
                  D=f.take(HB), bm=f.take(HB // 16), rs=f.take(HB // 16))
             for _ in range(2)]
    bb = Alloc(sb["bpool"])
    xqT = [bb.take(D) for _ in range(2)]
    hqd = [bb.take(I_FFN) for _ in range(2)]
    nsub = (I_FFN + 511) // 512

    def stage_mm(j):
        bj = j % 2
        xT = xqT[bj]
        ht = h[bj]
        dmat_blocks(P, "sync", xT, sc["xq"][128 * j:128 * (j + 1), 0:D],
                    tag=f"p1u.xT{j}")
        dma(P, "sync", ht, sc_x_tile(sc["gate_s"], j), tag=f"p1u.g{j}")
        P.hint("tensor", [xT], tag=f"p1u.h{j}")
        for si in range(nsub):
            i0 = 512 * si
            nw = min(512, I_FFN - i0)
            pv = pb[si % 6][0:128, 0:nw]
            for ec in range(EC):
                mm(P, pv, xT.sub(0, 128, 128 * ec, 128 * (ec + 1)),
                   wuT.sub(0, 128, I_FFN * ec + i0, I_FFN * ec + i0 + nw),
                   start=(ec == 0), stop=(ec == EC - 1), tag=f"p1u.mm{j}.{si}.{ec}")
            tt(P, ht.sub(0, 128, i0, i0 + nw), ht.sub(0, 128, i0, i0 + nw),
               pv, ALU.mult, tag=f"p1u.h{j}.{si}")

    def stage_quant(j):
        bj = j % 2
        ht = h[bj]
        hqb = hqd[bj]
        jobs = []
        for hf in range(2):
            t = dict(qtmpd[hf])
            t["w"] = ht.sub(0, 128, HB * hf, HB * (hf + 1))
            jobs.append((hqb.sub(0, 128, HB * hf, HB * (hf + 1)),
                         ht.sub(0, 128, HB * hf, HB * (hf + 1)), t))
        fp4_quant_multi(P, jobs, tag=f"p1u.q{j}", act_rounds=True)
        dma(P, "gpsimd", sc_x_tile(sc["hq"], j), hqb, tag=f"p1u.st{j}")

    for j in range(NT + 1):
        if j < NT:
            stage_mm(j)
        if j >= 1:
            stage_quant(j - 1)

def emit_p1b_down(ctx, wdT):
    P, io, sc, sb, pb = ctx["P"], ctx["io"], ctx["sc"], ctx["sb"], ctx["pb"]
    f = Alloc(sb["fpool"])
    gd = [f.take(D) for _ in range(2)]
    xtile = [f.take(D) for _ in range(2)]
    stile = [f.take(D) for _ in range(2)]
    bb = Alloc(sb["bpool"])
    hqT = [bb.take(IC * 128) for _ in range(2)]
    for j in range(NT):
        bj = j % 2
        hT = hqT[bj]
        g = gd[bj]
        dmat_blocks(P, "sync", hT, sc["hq"][128 * j:128 * (j + 1), 0:I_FFN],
                    tag=f"p1d.hT{j}")
        dma(P, "sync", xtile[bj], sc_x_tile(io["x_in"], j), tag=f"p1d.x{j}")
        dma(P, "sync", stile[bj], sc_x_tile(sc["s_tm"], j), tag=f"p1d.s{j}")
        P.hint("tensor", [hT], tag=f"p1d.h{j}")
        for half in range(2):
            pv = pb[(2 * j + half) % 8][0:128, 0:512]
            for ic in range(IC):
                mm(P, pv, hT.sub(0, 128, 128 * ic, 128 * (ic + 1)),
                   wdT.sub(0, 128, D * ic + 512 * half, D * ic + 512 * (half + 1)),
                   start=(ic == 0), stop=(ic == IC - 1), tag=f"p1d.mm{j}.{half}.{ic}")
            gv = g.sub(0, 128, 512 * half, 512 * (half + 1))
            act(P, gv, pv, AF.Silu, tag=f"p1d.si{j}.{half}")
            tt(P, gv, gv, stile[bj].sub(0, 128, 512 * half, 512 * (half + 1)),
               ALU.mult, tag=f"p1d.sg{j}.{half}")
            tt(P, gv, gv, xtile[bj].sub(0, 128, 512 * half, 512 * (half + 1)),
               ALU.add, tag=f"p1d.x1{j}.{half}")
        dma(P, "gpsimd", sc_x_tile(sc["x1"], j), g, tag=f"p1d.st{j}")


def emit_p2a(ctx, WqTs, WkTs, t0=0, t1=NT, drain=0):
    P, sc, sb, pb = ctx["P"], ctx["sc"], ctx["sb"], ctx["pb"]
    f = Alloc(sb["fpool"])
    x1t2 = [f.take(2 * D) for _ in range(2)]
    sqb = f.take(2 * D)
    smalls = f.off
    bb = Alloc(sb["bpool"])
    xnb2 = bb.take(2 * D)
    xT = [bb.take(D) for _ in range(2)]
    qkd = [bb.take(D) for _ in range(2)]
    lag = 4
    for i in range(t0, t1 + lag):
        if drain:
            drain_side(ctx, drain)
        if i < t1 and i % 2 == 0:
            i2 = i // 2
            b = i2 % 2
            x2 = x1t2[b]
            dma(P, "sync", x2, sc["x1"][256 * i2:256 * (i2 + 1), 0:D],
                tag=f"p2.x{i2}",
                src_fn=lambda ap: ap.rearrange("(a p) c -> p a c", p=128))
            fs = Alloc(sb["fpool"]); fs.off = smalls
            ln_tile2(P, x2, sqb, fs, tag=f"p2.ln{i2}")
            copy(P, xnb2, x2, eng="scalar", tag=f"p2.cb{i2}")
            dma(P, "gpsimd", sc["xn1_bf"][256 * i2:256 * (i2 + 1), 0:D], xnb2,
                tag=f"p2.sxb{i2}",
                dst_fn=lambda ap: ap.rearrange("(a p) c -> p a c", p=128))
        j = i - lag
        if j < t0:
            continue
        bj = j % 2
        xTj = xT[bj]
        qkb = qkd[bj]
        dmat_blocks(P, "sync", xTj, sc["xn1_bf"][128 * j:128 * (j + 1), 0:D],
                    tag=f"p2.xT{j}")
        bloc = j // NT_B
        tl = j % NT_B
        P.hint("tensor", [xTj], tag=f"p2a.h{j}")
        for (wT, dst, bcol, nm) in ((WqTs, sc["qf"], 0, "q"), (WkTs, sc["kf"], 1, "k")):
            for oc in range(EC):
                pv = pb[oc % 2][0:128, 0:128]
                for ec in range(EC):
                    mm(P, pv, wT.sub(0, 128, D * ec + 128 * oc, D * ec + 128 * (oc + 1)),
                       xTj.sub(0, 128, 128 * ec, 128 * (ec + 1)),
                       start=(ec == 0), stop=(ec == EC - 1),
                       tag=f"p2.{nm}{j}.{oc}.{ec}")
                ov = qkb.sub(0, 128, 128 * oc, 128 * (oc + 1))
                ts(P, ov, pv, sb["bias_pp"][0:128, 8 * bcol + oc:8 * bcol + oc + 1],
                   None, ALU.add, tag=f"p2.{nm}b{j}.{oc}")
            dma(P, "gpsimd",
                dst[D * bloc:D * bloc + D, 128 * tl:128 * (tl + 1)],
                qkb, tag=f"p2.{nm}st{j}",
                dst_fn=lambda ap: ap.rearrange("(o p) c -> p o c", p=128),
                src_fn=lambda ap: ap.rearrange("p (o c) -> p o c", c=128))


def emit_p2b(ctx, WvTs, WfTs, t0=0, t1=NT, drain=0):
    P, sc, sb, pb = ctx["P"], ctx["sc"], ctx["sb"], ctx["pb"]
    f = Alloc(sb["fpool"])
    g1d = [f.take(D) for _ in range(2)]
    bb = Alloc(sb["bpool"])
    xT = [bb.take(D) for _ in range(2)]
    vbd = [bb.take(D) for _ in range(2)]
    for j in range(t0, t1):
        if drain:
            drain_side(ctx, drain)
        bj = j % 2
        xTj = xT[bj]
        vb = vbd[bj]
        g1t = g1d[bj]
        dmat_blocks(P, "sync", xTj, sc["xn1_bf"][128 * j:128 * (j + 1), 0:D],
                    tag=f"p2b.xT{j}")
        bloc = j // NT_B
        tl = j % NT_B
        P.hint("tensor", [xTj], tag=f"p2b.h{j}")
        for half in range(2):
            pv = pb[half][0:128, 0:512]
            for ec in range(EC):
                mm(P, pv, xTj.sub(0, 128, 128 * ec, 128 * (ec + 1)),
                   WvTs.sub(0, 128, D * ec + 512 * half, D * ec + 512 * (half + 1)),
                   start=(ec == 0), stop=(ec == EC - 1), tag=f"p2b.v{j}.{half}.{ec}")
            tt(P, vb.sub(0, 128, 512 * half, 512 * (half + 1)), pv,
               sb["bv_bc"][0:128, 512 * half:512 * (half + 1)], ALU.add,
               tag=f"p2b.vb{j}.{half}")
        tw = [sc["vsc"][(bloc * NH + n) * S + 128 * tl:
                        (bloc * NH + n) * S + 128 * tl + 128, 0:HD]
              for n in range(NH)]
        dma(P, "gpsimd", sc["vsc"].full, vb, tag=f"p2b.vst{j}",
            dst_fn=lambda ap, _b=bloc, _t=tl: ap.rearrange(
                "(b n q p) c -> b q p n c", b=B_PER_CORE, n=NH, p=128)[_b, _t],
            src_fn=lambda ap: ap.rearrange("p (n c) -> p n c", c=HD),
            track_writes=tw)
        for half in range(2):
            pv = pb[half][0:128, 0:512]
            for ec in range(EC):
                mm(P, pv, xTj.sub(0, 128, 128 * ec, 128 * (ec + 1)),
                   WfTs.sub(0, 128, D * ec + 512 * half, D * ec + 512 * (half + 1)),
                   start=(ec == 0), stop=(ec == EC - 1), tag=f"p2b.f{j}.{half}.{ec}")
            act(P, g1t.sub(0, 128, 512 * half, 512 * (half + 1)), pv, AF.Silu,
                tag=f"p2b.g1{j}.{half}")
        dma(P, "gpsimd", sc_x_tile(sc["g1"], j), g1t, tag=f"p2b.g1st{j}")


def emit_p3(ctx, WoTs):
    P, io, sc, sb, pb = ctx["P"], ctx["io"], ctx["sc"], ctx["sb"], ctx["pb"]
    a = Alloc(sb["apool"])
    qtd = [a.take(S, rows=65) for _ in range(2)]
    ktd = [a.take(S, rows=65) for _ in range(2)]
    vtd = [a.take(NH * 65) for _ in range(2)]
    PTd = [a.take(S) for _ in range(2)]
    opair = a.take(S)
    obt = [a.take(1024) for _ in range(2)]
    f = Alloc(sb["fpool"])
    q2 = f.take(S, rows=64)
    ssq = f.take(S, rows=1)
    aq = f.take(S, rows=1)
    kmax = f.take(1, rows=1)
    rl = f.take(S, rows=1)
    lbc = f.take(S, rows=64)
    s1d = [f.take(D) for _ in range(2)]
    x1d = [f.take(D) for _ in range(2)]
    g1d = [f.take(D) for _ in range(2)]
    for z in range(2):
        memset(P, View(vtd[z].buf, 0, 128, vtd[z].c0, vtd[z].c1), 1.0, tag=f"p3.vones{z}")
        memset(P, View(ktd[z].buf, 64, 65, ktd[z].c0, ktd[z].c1), 1.0, tag=f"p3.kones{z}")

    nsub = S // 512

    def stage_load_norms(bloc, n):
        qt, kt, vt = qtd[n % 2], ktd[n % 2], vtd[n % 2]
        qrow = D * bloc + HD * (n % 2) + 128 * (n // 2)
        dma(P, "sync", View(qt.buf, 0, 64, qt.c0, qt.c1),
            sc["qf"][qrow:qrow + 64, 0:S], tag=f"p3.q{bloc}.{n}")
        dma(P, "sync", View(kt.buf, 0, 64, kt.c0, kt.c1),
            sc["kf"][qrow:qrow + 64, 0:S], tag=f"p3.k{bloc}.{n}")
        vrow = (bloc * NH + n) * S
        dma(P, "sync", View(vt.buf, 0, 128, vt.c0, vt.c1),
            sc["vsc"][vrow:vrow + S, 0:HD], tag=f"p3.v{bloc}.{n}",
            dst_fn=lambda ap: ap.rearrange("p (j c) -> p j c", c=65)[:, :, 0:HD],
            src_fn=lambda ap: ap.rearrange("(j p) c -> p j c", p=128))
        qv = View(qt.buf, 0, 64, qt.c0, qt.c1)
        kv = View(kt.buf, 0, 64, kt.c0, kt.c1)
        tt(P, q2, qv, qv, ALU.mult, tag=f"p3.q2{bloc}.{n}")
        spots = [(2, 0), (2, 32), (2, 64), (3, 0)]
        for si in range(nsub):
            bk, p0 = spots[si]
            mm(P, pb[bk][p0:p0 + 1, 0:512], sb["ones_col"][0:64, 0:1],
               q2.sub(0, 64, 512 * si, 512 * (si + 1)), True, True,
               tag=f"p3.ssq{bloc}.{n}.{si}")
        for si in range(nsub):
            bk, p0 = spots[si]
            copy(P, aq.sub(0, 1, 512 * si, 512 * (si + 1)),
                 pb[bk][p0:p0 + 1, 0:512], eng="vector",
                 tag=f"p3.ssqe{bloc}.{n}.{si}")
        tt(P, q2, kv, kv, ALU.mult, tag=f"p3.k2{bloc}.{n}")
        for si in range(nsub):
            bk, p0 = spots[si]
            mm(P, pb[bk][p0:p0 + 1, 0:512], sb["ones_col"][0:64, 0:1],
               q2.sub(0, 64, 512 * si, 512 * (si + 1)), True, True,
               tag=f"p3.ssk{bloc}.{n}.{si}")
        for si in range(nsub):
            bk, p0 = spots[si]
            copy(P, ssq.sub(0, 1, 512 * si, 512 * (si + 1)),
                 pb[bk][p0:p0 + 1, 0:512], eng="vector",
                 tag=f"p3.sske{bloc}.{n}.{si}")
        red(P, kmax, ssq, ALU.max, absval=True, tag=f"p3.km{bloc}.{n}")
        ts(P, aq, aq, kmax, None, ALU.mult, tag=f"p3.mh1{bloc}.{n}")
        act(P, aq, aq, AF.Sqrt, tag=f"p3.mh2{bloc}.{n}")
        ts(P, aq, aq, -1.0, None, ALU.mult, tag=f"p3.mh3{bloc}.{n}")
        copy(P, View(qt.buf, 64, 65, qt.c0, qt.c1), aq, eng="vector",
             tag=f"p3.mhc{bloc}.{n}")

    def stage_jloop(bloc, n):
        qt, kt, vt = qtd[n % 2], ktd[n % 2], vtd[n % 2]
        segs = []
        for jb in range(NT_B):
            i0 = 128 * jb
            for q4 in range(4):
                ci = max(i0, 512 * q4)
                ce = 512 * (q4 + 1)
                if ci < ce:
                    segs.append((jb, q4, ci, ce - ci))
        nseg = len(segs)
        LAGS = 2
        for idx in range(nseg + LAGS):
            if LAGS <= idx < nseg + LAGS:
                jb, q4, ci, nw = segs[idx - LAGS]
                PT = PTd[(idx - LAGS) % 2]
                opv = pb[4 + q4][0:65, (ci % 512):(ci % 512) + nw]
                mm(P, opv, View(vt.buf, 0, 128, vt.c0 + 65 * jb, vt.c0 + 65 * (jb + 1)),
                   PT.sub(0, 128, ci, ci + nw),
                   start=(jb == 0), stop=(jb == min(4 * q4 + 3, NT_B - 1)),
                   tag=f"p3.O{bloc}.{n}.{jb}.{ci}")
            if idx < nseg:
                jb, q4, ci, nw = segs[idx]
                i0 = 128 * jb
                PT = PTd[idx % 2]
                spv = pb[2 + idx % 2][0:128, 0:nw]
                mm(P, spv, View(kt.buf, 0, 65, kt.c0 + i0, kt.c0 + i0 + 128),
                   View(qt.buf, 0, 65, qt.c0 + ci, qt.c0 + ci + nw),
                   start=True, stop=True, tag=f"p3.S{bloc}.{n}.{jb}.{ci}")
                act(P, PT.sub(0, 128, ci, ci + nw), spv, AF.Exp,
                    tag=f"p3.E{bloc}.{n}.{jb}.{ci}")
                if ci == i0:
                    tt(P, PT.sub(0, 128, i0, i0 + 128),
                       PT.sub(0, 128, i0, i0 + 128), sb["tri01"].full,
                       ALU.mult, eng="vector", tag=f"p3.D{bloc}.{n}.{jb}")

    def stage_norm_evac(bloc, n):
        for q4 in range(4):
            copy(P, rl.sub(0, 1, 512 * q4, 512 * (q4 + 1)),
                 pb[4 + q4][64:65, 0:512], eng="vector", tag=f"p3.rlc{bloc}.{n}.{q4}")
        act(P, rl, rl, AF.Ln, tag=f"p3.rln{bloc}.{n}")
        act(P, rl, rl, AF.Exp, scale=-1.0, tag=f"p3.rlx{bloc}.{n}")
        lspots = [(2, 0), (2, 64), (3, 0), (3, 64)]
        for q4 in range(4):
            bk, p0 = lspots[q4]
            mm(P, pb[bk][p0:p0 + 64, 0:512], sb["ones_row"][0:1, 0:64],
               rl.sub(0, 1, 512 * q4, 512 * (q4 + 1)), True, True,
               tag=f"p3.lb{bloc}.{n}.{q4}")
        for q4 in range(4):
            bk, p0 = lspots[q4]
            copy(P, lbc.sub(0, 64, 512 * q4, 512 * (q4 + 1)),
                 pb[bk][p0:p0 + 64, 0:512], eng="vector",
                 tag=f"p3.lbe{bloc}.{n}.{q4}")
        hh = n % 2
        for q4 in range(4):
            tt(P, opair.sub(64 * hh, 64 * hh + 64, 512 * q4, 512 * (q4 + 1)),
               pb[4 + q4][0:64, 0:512],
               lbc.sub(0, 64, 512 * q4, 512 * (q4 + 1)),
               ALU.mult, tag=f"p3.on{bloc}.{n}.{q4}")
        if hh == 1:
            oc = n // 2
            r0 = D * bloc + 128 * oc
            dma(P, "gpsimd", sc["obsc"][r0:r0 + 128, 0:S],
                View(opair.buf, 0, 128, opair.c0, opair.c1),
                tag=f"p3.ost{bloc}.{oc}")

    def head_iter(bloc, n):
        if n < NH:
            stage_load_norms(bloc, n)
        if n >= 1:
            stage_jloop(bloc, n - 1)
            stage_norm_evac(bloc, n - 1)

    def queue_heads(bloc):
        for n in range(NH + 1):
            ctx["side"].append(lambda _b=bloc, _n=n: head_iter(_b, _n))

    def run_heads(bloc, drain=0):
        for n in range(NH + 1):
            head_iter(bloc, n)
            if drain:
                drain_side(ctx, drain)

    def outproj(bloc):
        for tl in range(NT_B):
            outproj_tile(bloc, tl)

    def outproj_tile(bloc, tl):
            jt = bloc * NT_B + tl
            bj = tl % 2
            ot = obt[bj]
            s1 = s1d[bj]
            x1t = x1d[bj]
            g1t = g1d[bj]
            dma(P, "sync", ot,
                sc["obsc"][D * bloc:D * bloc + D, 128 * tl:128 * (tl + 1)],
                tag=f"p3.ol{jt}",
                dst_fn=lambda ap: ap.rearrange("p (o c) -> p o c", c=128),
                src_fn=lambda ap: ap.rearrange("(o p) c -> p o c", p=128))
            dma(P, "sync", x1t, sc_x_tile(sc["x1"], jt), tag=f"p3.x1{jt}")
            dma(P, "sync", g1t, sc_x_tile(sc["g1"], jt), tag=f"p3.g1{jt}")
            for half in range(2):
                pv = pb[half][0:128, 0:512]
                for oc in range(EC):
                    mm(P, pv, ot.sub(0, 128, 128 * oc, 128 * (oc + 1)),
                       WoTs.sub(0, 128, D * oc + 512 * half, D * oc + 512 * (half + 1)),
                       start=(oc == 0), stop=(oc == EC - 1),
                       tag=f"p3.pr{jt}.{half}.{oc}")
                sv = s1.sub(0, 128, 512 * half, 512 * (half + 1))
                tt(P, sv, pv, sb["bo_bc"][0:128, 512 * half:512 * (half + 1)],
                   ALU.add, tag=f"p3.bo{jt}.{half}")
                tt(P, sv, sv, g1t.sub(0, 128, 512 * half, 512 * (half + 1)),
                   ALU.mult, tag=f"p3.sg{jt}.{half}")
                tt(P, sv, sv, x1t.sub(0, 128, 512 * half, 512 * (half + 1)),
                   ALU.add, tag=f"p3.x2{jt}.{half}")
            dma(P, "gpsimd", sc_x_tile(io["out"], jt), s1, tag=f"p3.out{jt}")

    def queue_outproj(bloc):
        for tl in range(NT_B):
            ctx["side"].append(lambda _b=bloc, _t=tl: outproj_tile(_b, _t))

    ctx["p3"] = dict(queue_heads=queue_heads, run_heads=run_heads,
                     outproj=outproj, queue_outproj=queue_outproj)



def build_full(debug_taps=(), upto=99):
    ctx = build_prog(debug_taps)
    P, sb = ctx["P"], ctx["sb"]
    wp = Alloc(sb["wpool"])
    wA = wp.take(EC * D)
    wB = wp.take(EC * D)
    wBig = View(sb["wpool"], 0, 128, 0, EC * I_FFN)

    setup_misc(ctx)
    if upto >= 0:
        prep_mp_weights(ctx, ("w_q", "w_k"))
        load_wT(ctx, wA, ctx["sc"]["wq_n"].full, "wqT")
        load_wT(ctx, wB, ctx["sc"]["wk_n"].full, "wkT")
        emit_p0(ctx, wA, wB)
    SIDE_POOLS = (6144, 6144)
    if upto >= 1:
        prep_mp_weights(ctx, ("w_v", "w_o"))
        load_wT(ctx, wA, ctx["sc"]["wv_n"].full, "wvT")
        load_wT(ctx, wB, ctx["sc"]["wo_n"].full, "woT")
        if upto >= 2:
            prep_fp4_weight(ctx, "w_gate", "wg_n", I_FFN, SIDE_POOLS)
        emit_p1a(ctx, wA, wB)
        drain_side(ctx, 99)
    if upto >= 2:
        load_wT(ctx, wBig, ctx["sc"]["wg_n"].full, "wgT")
        if upto >= 3:
            prep_fp4_weight(ctx, "w_up", "wu_n", I_FFN, SIDE_POOLS)
        if upto >= 4:
            prep_fp4_weight(ctx, "w_down", "wd_n", D, SIDE_POOLS)
        emit_p1b_gate(ctx, wBig)
        drain_side(ctx, 99)
    if upto >= 3:
        load_wT(ctx, wBig, ctx["sc"]["wu_n"].full, "wuT")
        emit_p1b_up(ctx, wBig)
    if upto >= 4:
        load_wT(ctx, wBig, ctx["sc"]["wd_n"].full, "wdT")
        emit_p1b_down(ctx, wBig)
    if upto >= 5 and upto < 6:
        load_l1_weight(ctx, wA, "WqT", scale=0.125, tag="l1q")
        load_l1_weight(ctx, wB, "WkT", tag="l1k")
        emit_p2a(ctx, wA, wB)
        load_l1_weight(ctx, wC, "WvT", tag="l1v")
        load_l1_weight(ctx, wD, "WfT", sign=True, tag="l1f")
        emit_p2b(ctx, wC, wD)
    if upto >= 6:
        load_l1_weight(ctx, wA, "WqT", scale=0.125, tag="l1q")
        load_l1_weight(ctx, wB, "WkT", tag="l1k")
        emit_p2a(ctx, wA, wB, 0, NT // 2)
        load_l1_weight(ctx, wA, "WvT", tag="l1v0")
        load_l1_weight(ctx, wB, "WfT", sign=True, tag="l1f0")
        emit_p2b(ctx, wA, wB, 0, NT // 2)
        wO = View(ctx["sb"]["bpool"], 0, 128, 6144, 6144 + EC * D)
        load_l1_weight(ctx, wO, "WoT", tag="l1o")
        emit_p3(ctx, wO)
        ctx["p3"]["queue_heads"](0)
        load_l1_weight(ctx, wA, "WqT", scale=0.125, tag="l1q1")
        load_l1_weight(ctx, wB, "WkT", tag="l1k1")
        emit_p2a(ctx, wA, wB, NT // 2, NT, drain=1)
        load_l1_weight(ctx, wA, "WvT", tag="l1v1")
        load_l1_weight(ctx, wB, "WfT", sign=True, tag="l1f1")
        emit_p2b(ctx, wA, wB, NT // 2, NT, drain=1)
        drain_side(ctx, 99)
        ctx["p3"]["queue_outproj"](0)
        ctx["p3"]["run_heads"](1, drain=1)
        drain_side(ctx, 99)
        ctx["p3"]["outproj"](1)
    return ctx



def make_inputs_per_core(inputs, core):
    f32 = np.float32
    x = np.ascontiguousarray(
        inputs["L0_attn_query"][2 * core:2 * core + 2].reshape(T, D), dtype=f32)
    d = {"x_in": x}
    d["gain"] = np.asarray(inputs["L0_attn_gain"], f32).reshape(1, 1)
    d["w_q"] = np.ascontiguousarray(inputs["L0_attn_w_q"], f32)
    d["w_k"] = np.ascontiguousarray(inputs["L0_attn_w_k"], f32)
    d["w_v"] = np.ascontiguousarray(inputs["L0_attn_w_v"], f32)
    d["w_o"] = np.ascontiguousarray(inputs["L0_attn_w_o"], f32)
    d["w_gate"] = np.ascontiguousarray(inputs["L0_ffn_w_gate"], f32)
    d["w_up"] = np.ascontiguousarray(inputs["L0_ffn_w_up"], f32)
    d["w_down"] = np.ascontiguousarray(inputs["L0_ffn_w_down"], f32)
    d["WqT"] = np.ascontiguousarray(np.asarray(inputs["L1_attn_Wq"], f32).T)
    d["WkT"] = np.ascontiguousarray(np.asarray(inputs["L1_attn_Wk"], f32).T)
    d["WvT"] = np.ascontiguousarray(np.asarray(inputs["L1_attn_Wv"], f32).T)
    d["WoT"] = np.ascontiguousarray(np.asarray(inputs["L1_attn_Wo"], f32).T)
    d["WfT"] = np.ascontiguousarray(np.asarray(inputs["L1_ffn_weight"], f32).T)
    bias = np.concatenate([np.asarray(inputs["L1_attn_b" + k], f32).reshape(8, 128)
                           for k in ("q", "k", "v", "o")], axis=1)
    d["biases"] = np.ascontiguousarray(bias)
    d["bo_row"] = np.asarray(inputs["L1_attn_bo"], f32).reshape(1, D)
    d["bv_row"] = np.asarray(inputs["L1_attn_bv"], f32).reshape(1, D)
    d["identf"] = np.eye(128, dtype=f32)
    tri = (np.arange(128)[None, :] >= np.arange(128)[:, None])
    d["tri01"] = tri.astype(ml_dtypes.bfloat16)
    d["consts"] = np.tile(np.array(CONST_VALS + [0.0] * (8 - len(CONST_VALS)),
                                   dtype=f32)[None, :], (128, 1))
    return d


def kernel(**inputs):
    ctx = build_full()
    nc, stack = ctx["P"].build()
    try:
        in_maps = [make_inputs_per_core(inputs, c) for c in range(8)]
        res = run_bass_kernel_spmd(nc, in_maps, core_ids=list(range(8)))
    finally:
        stack.close()
    out = np.concatenate(
        [res.results[c]["out"].reshape(B_PER_CORE, S, D) for c in range(8)], axis=0)
    return out.astype(np.float32)
